# version 07
# baseline (speedup 1.0000x reference)
import numpy as np
from contextlib import ExitStack

import concourse.bass as bass
import concourse.mybir as mybir

F32 = mybir.dt.float32
BF16 = mybir.dt.bfloat16
AF = mybir.ActivationFunctionType
ALU = mybir.AluOpType

DM, DI, RK, NS, DCV = 1024, 2048, 64, 16, 4
KC, NDC, NEC = DM // 128, DI // 128, 2 * DI // 128
RMS_EPS = 1e-5
LN_EPS = 1e-5
FFN_EPS = 1e-8
QMAX = 127.0
RND_M = float(3 * 2 ** 22)

DMA_CH = ["idl", "misc", "stgl0", "stgl1", "stgs0", "stgs1", "xl0", "xl1",
          "wwl0", "wwl1", "wol0", "wol1", "xcl0", "xcl1", "brl0", "brl1",
          "xcs00", "xcs01", "xcs10", "xcs11", "szs00", "szs01", "szs10", "szs11",
          "ygs0", "ygs1", "bcs0", "bcs1", "drl",
          "xas0", "xas1", "ffl", "fos"]
ENG = ["T", "A", "V", "G"]


class Sched:
    def __init__(self):
        self.items = []
        self.cnt = {k: 0 for k in ENG + DMA_CH}

    SELF_SERIAL = ("A", "V", "G")

    def emit(self, eng, fn, waits=(), inc=0):
        waits = list(waits)
        if eng in self.SELF_SERIAL:
            waits.append((eng, self.cnt[eng]))
            inc = 1
        self.items.append((eng, tuple(waits), fn, eng if inc else None, inc))
        if inc:
            self.cnt[eng] += inc
        return self.cnt[eng]

    def dma(self, ch, fn, waits=(), eng="S"):
        assert ch in DMA_CH, ch
        self.items.append((eng, tuple(waits), fn, ch, 16))
        self.cnt[ch] += 16
        return self.cnt[ch]

    def now(self, key):
        return self.cnt[key]


def build_program(L=8192, layers=4, Ts=512, Lp=1024, debug_internals=False):
    nsp = L // Ts
    nlp = L // Lp
    HB = Lp // 512
    NT = Ts // 128
    nc = bass.Bass("TRN2", target_bir_lowering=False, debug=False)

    EI, EO, IN = "ExternalInput", "ExternalOutput", "Internal"
    x_in = nc.dram_tensor("x_in", [DM, L], F32, kind=EI).ap()
    winT = nc.dram_tensor("winT", [layers, NEC, KC, 128, 128], F32, kind=EI).ap()
    woutT = nc.dram_tensor("woutT", [layers, KC, NDC, 128, 128], F32, kind=EI).ap()
    wxT = nc.dram_tensor("wxT", [layers, 128, NDC * 96], F32, kind=EI).ap()
    wdtT = nc.dram_tensor("wdtT", [layers, 64, NDC * 128], F32, kind=EI).ap()
    alog = nc.dram_tensor("alog", [layers, 128, NDC * NS], F32, kind=EI).ap()
    convw = nc.dram_tensor("convw", [layers, 128, NDC * DCV], F32, kind=EI).ap()
    convb = nc.dram_tensor("convb", [layers, 128, NDC], F32, kind=EI).ap()
    bdt = nc.dram_tensor("bdt", [layers, 128, NDC], F32, kind=EI).ap()
    dvec = nc.dram_tensor("dvec", [layers, 128, NDC], F32, kind=EI).ap()
    normw = nc.dram_tensor("normw", [layers, 128, KC], F32, kind=EI).ap()
    fw = nc.dram_tensor("fw", [KC, 128, DM], F32, kind=EI).ap()
    fb = nc.dram_tensor("fb", [128, KC], F32, kind=EI).ap()
    ident_in = nc.dram_tensor("ident_in", [128, 128], BF16, kind=EI).ap()
    identf_in = nc.dram_tensor("identf_in", [128, 128], F32, kind=EI).ap()

    y_out = nc.dram_tensor("y_out", [DM, L], F32, kind=EO).ap()

    ikind = EO if debug_internals else IN
    xA = nc.dram_tensor("xA", [KC, 128, L], F32, kind=ikind).ap()
    XC = nc.dram_tensor("XCb", [NDC, 128, L], F32, kind=ikind).ap()
    SZ = nc.dram_tensor("SZb", [NDC, 128, L], BF16, kind=ikind).ap()
    YG = nc.dram_tensor("YGb", [NDC, 128, L], BF16, kind=ikind).ap()
    BCB = nc.dram_tensor("BCB", [2, NS, L], BF16, kind=ikind).ap()
    DR = nc.dram_tensor("DRb", [64, L], BF16, kind=IN).ap()
    WINB = nc.dram_tensor("WINB", [NEC, 128, KC * 128], BF16, kind=IN).ap()
    WOUTB = nc.dram_tensor("WOUTB", [KC, 128, NDC * 128], BF16, kind=IN).ap()

    ctx = ExitStack()
    sb = lambda name, shape, dt: ctx.enter_context(nc.sbuf_tensor(name, shape, dt))
    psb = lambda name, shape, dt: ctx.enter_context(nc.psum_tensor(name, shape, dt))

    dr_sb = sb("dr_sb", [64, 2 * Lp], BF16)
    stage96 = sb("stage96", [128, 512], BF16)
    browA = sb("browA", [1, 2 * Lp], BF16)
    browC = sb("browC", [1, 2 * Lp], BF16)
    states = sb("states", [128, NDC * NS], F32)
    halo = sb("halo", [128, NDC * (DCV - 1)], F32)
    wx_sb = sb("wx_sb", [128, NDC * 96], F32)
    wdt_sb = sb("wdt_sb", [64, NDC * 128], BF16)
    A_sb = sb("A_sb", [128, NDC * NS], F32)
    cw_sb = sb("cw_sb", [128, NDC * DCV], F32)
    cb_sb = sb("cb_sb", [128, NDC], F32)
    bdt_sb = sb("bdt_sb", [128, NDC], F32)
    dv_sb = sb("dv_sb", [128, NDC], F32)
    nw_sb = sb("nw_sb", [128, KC], F32)
    wsc_sb = sb("wsc_sb", [128, KC], F32)
    fb_sb = sb("fb_sb", [128, KC], F32)
    am_sb = sb("am_sb", [128, 2 * NT], F32)
    ident = sb("ident", [128, 128], BF16)
    identf = sb("identf", [128, 128], F32)
    ones_f = sb("ones_f", [1, 128], F32)
    ones_bf = sb("ones_bf", [1, 128], BF16)
    ones_bc = sb("ones_bc", [128, 1], BF16)
    wwin = sb("wwin", [128, 2 * KC * 128], BF16)
    wowin = sb("wowin", [128, 2 * NDC * 128], BF16)
    stg_f = sb("stg_f", [128, 2048], F32)
    stg_b = sb("stg_b", [128, 2048], BF16)
    rowbuf = sb("rowbuf", [1, 3 * Ts], F32)

    AFC = 13318
    ABC = 50176
    arena_f = sb("arena_f", [128, AFC], F32)
    arena_b = sb("arena_b", [128, ABC], BF16)

    def fsl(off, cols):
        assert off + cols <= AFC, (off, cols)
        return arena_f[:, off:off + cols]

    def bsl(off, cols):
        assert off + cols <= ABC, (off, cols)
        return arena_b[:, off:off + cols]

    delta_sl = [fsl(i * Lp, Lp) for i in range(2)]
    da_sl = [fsl(2 * Lp + i * Lp, Lp) for i in range(2)]
    xc2_sl = [fsl(4 * Lp + i * Lp, Lp) for i in range(2)]
    t2_sl = fsl(6 * Lp, Lp)
    _p1 = 7 * Lp
    xsp = fsl(_p1, KC * Ts)
    xi_buf = [fsl(_p1 + KC * Ts + i * (Ts + 3), Ts + 3) for i in range(2)]
    xc_sl = [fsl(_p1 + KC * Ts + 2 * (Ts + 3) + i * Ts, Ts) for i in range(2)]
    brep = bsl(0, NS * Lp)
    crep = bsl(NS * Lp, NS * Lp)
    _o = 2 * NS * Lp
    dx_sl = [bsl(_o + i * Lp, Lp) for i in range(2)]
    bx_sl = [bsl(_o + 2 * Lp + i * Lp, Lp) for i in range(2)]
    h_sl = [bsl(_o + 4 * Lp + i * Lp, Lp) for i in range(2)]
    t_sl = [bsl(_o + 6 * Lp + i * Lp, Lp) for i in range(2)]
    sz2_sl = [bsl(_o + 8 * Lp + i * Lp, Lp) for i in range(2)]
    yg2_sl = [bsl(_o + 10 * Lp + i * Lp, Lp) for i in range(2)]
    _p1b = _o + 12 * Lp
    hrms = bsl(_p1b, KC * Ts)
    sz_sl = [bsl(_p1b + KC * Ts + i * Ts, Ts) for i in range(2)]

    xsp3 = [fsl(i * KC * Ts, KC * Ts) for i in range(2)]
    ygsp = bsl(0, NDC * Ts)

    u_sp = fsl(0, KC * Ts)
    o1_sp = fsl(KC * Ts, KC * Ts)
    srep_sp = fsl(2 * KC * Ts, Ts)
    xq_sp = bsl(0, KC * Ts)
    ub_sp = bsl(KC * Ts, KC * Ts)
    wqt_sb = bsl(2 * KC * Ts, KC * KC * 128)

    ps_mm = psb("ps_mm", [128, 512], F32)
    ps_mm2 = psb("ps_mm2", [128, 512], F32)
    ps_y = psb("ps_y", [128, Lp], F32)
    ps_bc = psb("ps_bc", [128, 1024], F32)
    ps_st = psb("ps_st", [128, 512], F32)
    ps_aux = psb("ps_aux", [128, 512], F32)

    SEMS = {}
    for k in ENG + DMA_CH:
        SEMS[k] = ctx.enter_context(nc.semaphore("sem_" + k))

    sch = Sched()

    sch.dma("idl", lambda e: e.dma_start(out=ident[:], in_=ident_in[:, :]))
    sch.dma("idl", lambda e: e.dma_start(out=identf[:], in_=identf_in[:, :]))
    pre_S = sch.now("idl")
    sch.emit("V", lambda e: e.memset(ones_f[:], 1.0))
    sch.emit("V", lambda e: e.memset(ones_bf[:], 1.0))
    sch.emit("V", lambda e: e.memset(ones_bc[:], 1.0), inc=1)
    pre_V = sch.now("V")

    def x_src(li, kc, l0, l1):
        if li == 0:
            return x_in[kc * 128:(kc + 1) * 128, l0:l1]
        return xA[kc, :, l0:l1]

    xas_tot = [0, 0]
    psbc_prev = [0]

    for li in range(layers):
        sch.emit("V", lambda e: e.memset(states[:], 0.0))
        sch.emit("V", lambda e: e.memset(halo[:], 0.0), inc=1)
        zero_V = sch.now("V")

        for (dst, src) in ((cw_sb, convw[li]), (cb_sb, convb[li]), (bdt_sb, bdt[li]),
                           (dv_sb, dvec[li]), (nw_sb, normw[li]), (A_sb, alog[li])):
            sch.dma("misc", lambda e, d=dst, s=src: e.dma_start(out=d[:], in_=s))
        cA_S = sch.now("misc")
        sch.emit("A", lambda e: e.activation(out=A_sb[:], in_=A_sb[:], func=AF.Exp),
                 waits=[("misc", cA_S)], inc=1)
        expA_A = sch.now("A")
        sch.emit("V", lambda e: e.tensor_scalar_mul(A_sb[:], A_sb[:], -1.0),
                 waits=[("A", expA_A)], inc=1)
        negA_V = sch.now("V")
        wxw = [("V", negA_V), ("A", sch.now("A"))]
        if li > 0:
            wxw.append(("T", sch.now("T")))
        sch.dma("misc", lambda e, s=wxT[li]: e.dma_start(out=wx_sb[:], in_=s), waits=wxw)
        sch.dma("misc", lambda e, s=wdtT[li]: e.dma_start(out=stg_f[0:64, 0:NDC * 128], in_=s))
        cWd_S = sch.now("misc")
        sch.emit("A", lambda e: e.activation(out=wdt_sb[:], in_=stg_f[0:64, 0:NDC * 128], func=AF.Copy),
                 waits=[("misc", cWd_S)], inc=1)
        const_A = sch.now("A")

        stage_cast = []
        stage_store = {0: [], 1: []}
        units = []
        for ec in range(NEC):
            units.append((winT[li, ec], WINB[ec], 0))
        for kc in range(KC):
            units.append((woutT[li, kc, 0:8], WOUTB[kc], 0))
            units.append((woutT[li, kc, 8:16], WOUTB[kc], 1024))
        for u, (src, dstrow, dcol) in enumerate(units):
            par = u % 2
            chL, chS = f"stgl{par}", f"stgs{par}"
            f_s = stg_f[:, par * 1024:(par + 1) * 1024]
            b_s = stg_b[:, par * 1024:(par + 1) * 1024]
            w = [("A", const_A)] if u < 2 else [("A", stage_cast[u - 2])]
            sch.dma(chL, lambda e, d=f_s, src=src:
                    e.dma_start(out=d.rearrange("p (k c) -> p k c", k=8),
                                in_=src.rearrange("k p c -> p k c")),
                    waits=w)
            sL = sch.now(chL)
            w2 = [(chL, sL)]
            if len(stage_store[par]) >= 1:
                w2.append((chS, stage_store[par][-1]))
            sch.emit("A", lambda e, d=b_s, sf=f_s: e.activation(out=d, in_=sf, func=AF.Copy),
                     waits=w2, inc=1)
            stage_cast.append(sch.now("A"))
            sch.dma(chS, lambda e, d=dstrow, dcol=dcol, sbf=b_s:
                    e.dma_start(out=d[:, dcol:dcol + 1024], in_=sbf),
                    waits=[("A", stage_cast[-1])], eng="A")
            stage_store[par].append(sch.now(chS))
        wstage = [("stgs0", sch.now("stgs0")), ("stgs1", sch.now("stgs1"))]

        NPIECE = 37
        xsp_free = []
        wwin_freeT = []
        psum_freeA = []
        xi_freeA = []
        xcs_tot = []
        szs_tot = []
        bcs_tot = []
        wxmm_T = []
        xl_tot = {}
        hrv_prev = [0]
        drbar = [0]
        rep_evacA = []

        def emit_xload(s):
            l0, l1 = s * Ts, (s + 1) * Ts
            if s == 0:
                w = [("V", zero_V), ("xas0", xas_tot[0]), ("xas1", xas_tot[1])]
            else:
                w = [("V", xsp_free[s - 1])]
            for kc in range(KC):
                sch.dma("xl0", lambda e, kc=kc, l0=l0, l1=l1, li=li:
                        e.dma_start(out=xsp[:, kc * Ts:(kc + 1) * Ts], in_=x_src(li, kc, l0, l1)),
                        waits=(w if kc == 0 else []))
            xl_tot[s] = sch.now("xl0")

        def p1_gen():
            for s in range(nsp):
                l0, l1 = s * Ts, (s + 1) * Ts
                gp = (s // 2 - 1) % 2
                if s == 0:
                    emit_xload(0)
                w_sq = [("xl0", xl_tot[s])]
                if wwin_freeT:
                    w_sq.append(("T", wwin_freeT[-1]))
                for kc in range(KC):
                    sch.emit("A", lambda e, kc=kc:
                             e.activation(out=hrms[:, kc * Ts:(kc + 1) * Ts],
                                          in_=xsp[:, kc * Ts:(kc + 1) * Ts], func=AF.Square),
                             waits=(w_sq if kc == 0 else []), inc=(1 if kc == KC - 1 else 0))
                sq_A = sch.now("A")
                yield
                w_st = [("A", sq_A), ("V", pre_V)]
                if hrv_prev[0]:
                    w_st.append(("V", hrv_prev[0]))
                for kc in range(KC):
                    sch.emit("T", lambda e, kc=kc:
                             e.matmul(ps_aux[0:1, 0:Ts], ones_bc[:], hrms[:, kc * Ts:(kc + 1) * Ts],
                                      start=(kc == 0), stop=(kc == KC - 1)),
                             waits=(w_st if kc == 0 else []), inc=(1 if kc == KC - 1 else 0))
                st_T = sch.now("T")
                yield
                sch.emit("V", lambda e: e.tensor_scalar(rowbuf[0:1, 0:Ts], ps_aux[0:1, 0:Ts],
                                                        1.0 / DM, RMS_EPS, ALU.mult, ALU.add),
                         waits=[("T", st_T)], inc=1)
                ms_V = sch.now("V")
                sch.emit("A", lambda e: e.activation(out=rowbuf[0:1, Ts:2 * Ts], in_=rowbuf[0:1, 0:Ts],
                                                     func=AF.Ln),
                         waits=[("V", ms_V)])
                sch.emit("A", lambda e: e.activation(out=rowbuf[0:1, Ts:2 * Ts], in_=rowbuf[0:1, Ts:2 * Ts],
                                                     func=AF.Exp, scale=-0.5), inc=1)
                rstd_A = sch.now("A")
                sch.emit("T", lambda e: e.matmul(ps_aux[:, 0:Ts], ones_f[:], rowbuf[0:1, Ts:2 * Ts],
                                                 start=True, stop=True),
                         waits=[("A", rstd_A), ("idl", pre_S)], inc=1)
                bc_T = sch.now("T")
                yield
                for kc in range(KC):
                    sch.emit("V", lambda e, kc=kc:
                             e.scalar_tensor_tensor(hrms[:, kc * Ts:(kc + 1) * Ts],
                                                    xsp[:, kc * Ts:(kc + 1) * Ts],
                                                    nw_sb[:, kc:kc + 1], ps_aux[:, 0:Ts],
                                                    ALU.mult, ALU.mult),
                             waits=([("T", bc_T)] if kc == 0 else []), inc=(1 if kc == KC - 1 else 0))
                hr_V = sch.now("V")
                hrv_prev[0] = hr_V
                xsp_free.append(hr_V)
                yield
                silu_As = {}
                conv_Vs = {}

                def emit_wx(dc):
                    wmx = [("A", silu_As[dc])]
                    if dc == 0:
                        wmx.append(("A", const_A))
                        if drbar[0]:
                            wmx.append(("A", drbar[0]))
                    sch.emit("T", lambda e, dc=dc:
                             e.matmul(ps_st[0:96, 0:Ts], wx_sb[:, dc * 96:(dc + 1) * 96],
                                      xc_sl[dc % 2][:, 0:Ts],
                                      start=(dc == 0), stop=(dc == NDC - 1)),
                             waits=wmx, inc=1)
                    wxmm_T.append(sch.now("T"))

                def emit_silu_store(dc):
                    sch.emit("A", lambda e, xcv=xc_sl[dc % 2]:
                             e.activation(out=xcv, in_=xcv, func=AF.Silu),
                             waits=[("V", conv_Vs[dc])], inc=1)
                    silu_As[dc] = sch.now("A")
                    chXS = f"xcs{gp}{dc % 2}"
                    sch.dma(chXS, lambda e, dc=dc, l0=l0, l1=l1:
                            e.dma_start(out=XC[dc, :, l0:l1], in_=xc_sl[dc % 2]),
                            waits=[("A", silu_As[dc])], eng="A")
                    xcs_tot.append((chXS, sch.now(chXS)))

                for ec in range(NEC):
                    gi = s * NEC + ec
                    par = ec % 2
                    chW = f"wwl{par}"
                    wv = wwin[:, par * KC * 128:(par + 1) * KC * 128]
                    w2 = list(wstage) if gi < 2 else [("T", wwin_freeT[gi - 2])]
                    sch.dma(chW, lambda e, d=wv, ec=ec: e.dma_start(out=d, in_=WINB[ec]), waits=w2)
                    wl_S = sch.now(chW)
                    psel = ps_bc[:, 0:512] if ec % 2 == 0 else ps_bc[:, 512:1024]
                    for kc in range(KC):
                        w3 = []
                        if kc == 0:
                            w3 = [(chW, wl_S), ("V", hr_V)]
                            if gi >= 2:
                                w3.append(("A", psum_freeA[gi - 2]))
                            if rep_evacA:
                                w3.append(("A", rep_evacA[-1]))
                            elif psbc_prev[0]:
                                w3.append(("A", psbc_prev[0]))
                        sch.emit("T", lambda e, kc=kc, wv=wv, psel=psel:
                                 e.matmul(psel[:, 0:Ts], wv[:, kc * 128:(kc + 1) * 128],
                                          hrms[:, kc * Ts:(kc + 1) * Ts],
                                          start=(kc == 0), stop=(kc == KC - 1)),
                                 waits=w3, inc=(1 if kc == KC - 1 else 0))
                    mm_T = sch.now("T")
                    wwin_freeT.append(mm_T)
                    if ec < NDC:
                        dc = ec
                        if dc >= 2:
                            emit_wx(dc - 2)
                        xb = xi_buf[dc % 2]
                        w4 = [] if len(xi_freeA) < 2 else [("A", xi_freeA[-2])]
                        sch.emit("V", lambda e, xb=xb, dc=dc:
                                 e.tensor_copy(xb[:, 0:DCV - 1],
                                               halo[:, dc * (DCV - 1):(dc + 1) * (DCV - 1)]),
                                 waits=w4, inc=1)
                        hv_V = sch.now("V")
                        sch.emit("A", lambda e, xb=xb, psel=psel:
                                 e.activation(out=xb[:, DCV - 1:DCV - 1 + Ts], in_=psel[:, 0:Ts],
                                              func=AF.Copy),
                                 waits=[("T", mm_T), ("V", hv_V)], inc=1)
                        xi_freeA.append(sch.now("A"))
                        psum_freeA.append(sch.now("A"))
                        ev_A = sch.now("A")
                        xcv = xc_sl[dc % 2]
                        w5 = [("A", ev_A)]
                        if dc == 0:
                            if len(xcs_tot) >= 2:
                                w5.append(xcs_tot[-2])
                        elif xcs_tot:
                            w5.append(xcs_tot[-1])
                        if dc >= 2 or (dc == 1 and wxmm_T):
                            w5.append(("T", wxmm_T[-1]))
                        elif dc == 0 and len(wxmm_T) >= 2:
                            w5.append(("T", wxmm_T[-2]))
                        sch.emit("V", lambda e, xb=xb, xcv=xcv, dc=dc:
                                 e.tensor_scalar(xcv, xb[:, 0:Ts], cw_sb[:, dc * DCV:dc * DCV + 1],
                                                 cb_sb[:, dc:dc + 1], ALU.mult, ALU.add),
                                 waits=w5)
                        for k in range(1, DCV):
                            sch.emit("V", lambda e, xb=xb, xcv=xcv, dc=dc, k=k:
                                     e.scalar_tensor_tensor(xcv, xb[:, k:k + Ts],
                                                            cw_sb[:, dc * DCV + k:dc * DCV + k + 1],
                                                            xcv, ALU.mult, ALU.add))
                        sch.emit("V", lambda e, xb=xb, dc=dc:
                                 e.tensor_copy(halo[:, dc * (DCV - 1):(dc + 1) * (DCV - 1)],
                                               xb[:, Ts:Ts + DCV - 1]),
                                 inc=1)
                        conv_Vs[dc] = sch.now("V")
                        if dc >= 1:
                            emit_silu_store(dc - 1)
                        if dc == 2 and s + 1 < nsp:
                            emit_xload(s + 1)
                    else:
                        if ec == NDC:
                            emit_silu_store(NDC - 1)
                        dcz = ec - NDC
                        par2 = dcz % 2
                        szv = sz_sl[par2]
                        chSS = f"szs{gp}{par2}"
                        w4 = [("T", mm_T)]
                        if len(szs_tot) >= 2:
                            w4.append(szs_tot[-2])
                        sch.emit("A", lambda e, szv=szv, psel=psel:
                                 e.activation(out=szv, in_=psel[:, 0:Ts], func=AF.Silu),
                                 waits=w4, inc=1)
                        psum_freeA.append(sch.now("A"))
                        sz_A = sch.now("A")
                        sch.dma(chSS, lambda e, dcz=dcz, l0=l0, l1=l1, szv=szv:
                                e.dma_start(out=SZ[dcz, :, l0:l1], in_=szv),
                                waits=[("A", sz_A)], eng="A")
                        szs_tot.append((chSS, sch.now(chSS)))
                    yield
                emit_wx(NDC - 2)
                emit_wx(NDC - 1)
                wx_T = sch.now("T")
                w7 = [("T", wx_T)]
                if bcs_tot:
                    w7.append(bcs_tot[-1])
                sch.emit("A", lambda e:
                         e.activation(out=stage96[0:64, 0:Ts], in_=ps_st[0:64, 0:Ts], func=AF.Copy),
                         waits=w7)
                sch.emit("A", lambda e: e.activation(out=stage96[64:96, 0:Ts], in_=ps_st[64:96, 0:Ts],
                                                     func=AF.Copy), inc=1)
                bc_A = sch.now("A")
                chBC = f"bcs{gp}"
                sch.dma(chBC, lambda e, l0=l0, l1=l1:
                        e.dma_start(out=DR[:, l0:l1], in_=stage96[0:64, 0:Ts]),
                        waits=[("A", bc_A)], eng="A")
                sch.dma(chBC, lambda e, l0=l0, l1=l1:
                        e.dma_start(out=BCB[0, :, l0:l1], in_=stage96[64:80, 0:Ts]),
                        eng="A")
                sch.dma(chBC, lambda e, l0=l0, l1=l1:
                        e.dma_start(out=BCB[1, :, l0:l1], in_=stage96[80:96, 0:Ts]),
                        eng="A")
                bcs_tot.append((chBC, sch.now(chBC)))
                drbar[0] = bc_A
                yield

        gen_bar = {}

        def record_bar(lp, gp):
            gen_bar[lp] = {
                "xc": [(f"xcs{gp}0", sch.now(f"xcs{gp}0")), (f"xcs{gp}1", sch.now(f"xcs{gp}1")),
                       (f"szs{gp}0", sch.now(f"szs{gp}0")), (f"szs{gp}1", sch.now(f"szs{gp}1"))],
                "bc": [(f"bcs{gp}", sch.now(f"bcs{gp}"))],
            }

        p2_ld_freeV = []
        ygs_tot = []
        psmm_freeA = []
        da_freeV = []
        tmm_T = []
        psy_freeV = []
        brow_T = []

        bevac = {}
        cevac = {}
        vend_at_lp = [0]

        def p2_rep_build(lp, n):
            g0, g1 = lp * Lp, (lp + 1) * Lp
            par = n % 2
            chB = f"brl{par}"
            rsA = browA[0:1, par * Lp:(par + 1) * Lp]
            rsC = browC[0:1, par * Lp:(par + 1) * Lp]
            w = list(gen_bar[lp]["bc"]) if n == 0 else []
            if len(brow_T) >= 2:
                w.append(("T", brow_T[-2]))
            sch.dma(chB, lambda e, d=rsA, n=n, g0=g0, g1=g1:
                    e.dma_start(out=d, in_=BCB[0, n:n + 1, g0:g1]), waits=w)
            sch.dma(chB, lambda e, d=rsC, n=n, g0=g0, g1=g1:
                    e.dma_start(out=d, in_=BCB[1, n:n + 1, g0:g1]))
            rl_S = sch.now(chB)
            for bi, (row, rep, evl) in enumerate(((rsA, brep, bevac), (rsC, crep, cevac))):
                w2 = [(chB, rl_S)]
                if rep_evacA:
                    w2.append(("A", rep_evacA[-1]))
                if psum_freeA:
                    w2.append(("A", psum_freeA[-1]))
                for hb in range(HB):
                    sch.emit("T", lambda e, d=row, hb=hb:
                             e.matmul(ps_bc[:, hb * 512:(hb + 1) * 512], ones_bf[:],
                                      d[0:1, hb * 512:(hb + 1) * 512], start=True, stop=True),
                             waits=(w2 if hb == 0 else []), inc=(1 if hb == HB - 1 else 0))
                rep_T = sch.now("T")
                wE = [("T", rep_T)]
                if lp > 0:
                    wE.append(("V", vend_at_lp[0]))
                sch.emit("A", lambda e, rep=rep, n=n:
                         e.activation(out=rep[:, n * Lp:(n + 1) * Lp], in_=ps_bc[:, 0:Lp], func=AF.Copy),
                         waits=wE, inc=1)
                rep_evacA.append(sch.now("A"))
                evl[n] = sch.now("A")
            brow_T.append(sch.now("T"))

        sp_As = {}
        dx_Vs = {}
        ld_Ss = {}
        wdt_last = {}
        drl_tot = {}

        def p2_prolog(lp, dc):
            g0, g1 = lp * Lp, (lp + 1) * Lp
            dsl = dc % 2
            chL = f"xcl{dsl}"
            dls = delta_sl[dsl]
            dxs = dx_sl[dsl]
            xcv = xc2_sl[dsl]
            szv = sz2_sl[dsl]
            gidx = lp * NDC + dc
            w = list(gen_bar[lp]["xc"]) if dc == 0 else []
            if gidx >= 2:
                w.append(("V", p2_ld_freeV[gidx - 2]))
            sch.dma(chL, lambda e, dc=dc, g0=g0, g1=g1, d=xcv:
                    e.dma_start(out=d, in_=XC[dc, :, g0:g1]), waits=w)
            sch.dma(chL, lambda e, dc=dc, g0=g0, g1=g1, d=szv:
                    e.dma_start(out=d, in_=SZ[dc, :, g0:g1]))
            ld_Ss[dc] = sch.now(chL)
            for hb in range(HB):
                psel = ps_mm if hb % 2 == 0 else ps_mm2
                w2 = []
                if dc == 0 and hb == 0:
                    w2.append(("drl", drl_tot[lp]))
                if len(psmm_freeA) >= 2:
                    w2.append(("A", psmm_freeA[-2]))
                sch.emit("T", lambda e, dc=dc, hb=hb, psel=psel, lp=lp:
                         e.matmul(psel[:, 0:512], wdt_sb[:, dc * 128:(dc + 1) * 128],
                                  dr_sb[:, (lp % 2) * Lp + hb * 512:(lp % 2) * Lp + (hb + 1) * 512],
                                  start=True, stop=True),
                         waits=w2, inc=1)
                dt_T = sch.now("T")
                sch.emit("A", lambda e, d=dls, hb=hb, dc=dc, psel=psel:
                         e.activation(out=d[:, hb * 512:(hb + 1) * 512], in_=psel[:, 0:512],
                                      func=AF.Exp, bias=bdt_sb[:, dc:dc + 1]),
                         waits=[("T", dt_T)])
                sch.emit("A", lambda e, d=dls, hb=hb:
                         e.activation(out=d[:, hb * 512:(hb + 1) * 512], in_=d[:, hb * 512:(hb + 1) * 512],
                                      func=AF.Ln, bias=1.0), inc=1)
                psmm_freeA.append(sch.now("A"))
            wdt_last[lp] = sch.now("T")
            sp_As[dc] = sch.now("A")
            sch.emit("V", lambda e, d=dxs, a=dls, b=xcv: e.tensor_mul(d, a, b),
                     waits=[("A", sp_As[dc]), (chL, ld_Ss[dc])], inc=1)
            dx_Vs[dc] = sch.now("V")

        def p2_lp(lp, gen=None, bud=None):
            g0, g1 = lp * Lp, (lp + 1) * Lp
            vend_at_lp[0] = sch.now("V")
            drv = dr_sb[:, (lp % 2) * Lp:(lp % 2 + 1) * Lp]
            wdr = list(gen_bar[lp]["bc"])
            if lp >= 2:
                wdr.append(("T", wdt_last[lp - 2]))
            sch.dma("drl", lambda e, d=drv, g0=g0, g1=g1:
                    e.dma_start(out=d, in_=DR[:, g0:g1]), waits=wdr)
            drl_tot[lp] = sch.now("drl")
            p2_rep_build(lp, 0)
            p2_rep_build(lp, 1)
            p2_prolog(lp, 0)

            for dci in range(NDC):
                dc = dci
                dsl = dci % 2
                dls = delta_sl[dsl]
                dxs = dx_sl[dsl]
                xcv = xc2_sl[dsl]
                szv = sz2_sl[dsl]
                ygv = yg2_sl[dsl]
                sp_A = sp_As[dc]
                dx_V = dx_Vs[dc]
                for n in range(NS):
                    it = len(da_freeV)
                    asl = it % 2
                    dav = da_sl[asl]
                    bxv = bx_sl[asl]
                    hv = h_sl[asl]
                    tv = t_sl[asl]
                    w3 = [("A", sp_A)] if n == 0 else []
                    if it == 0 and lp == 0 and li == 0:
                        w3.append(("V", negA_V))
                    if it >= 2:
                        w3.append(("V", da_freeV[it - 2]))
                    sch.emit("A", lambda e, d=dav, s_=dls, dc=dc, n=n:
                             e.activation(out=d, in_=s_, func=AF.Exp,
                                          scale=A_sb[:, dc * NS + n:dc * NS + n + 1]),
                             waits=w3, inc=1)
                    exp_A = sch.now("A")
                    w4 = [("V", dx_V)] if n == 0 else []
                    if dci == 0:
                        w4.append(("A", bevac[n]))
                    sch.emit("V", lambda e, d=bxv, a=dxs, n=n:
                             e.tensor_mul(d, a, brep[:, n * Lp:(n + 1) * Lp]), waits=w4)
                    sidx = dc * NS + n
                    sch.emit("V", lambda e, d=hv, a=dav, b=bxv, sidx=sidx:
                             e.tensor_tensor_scan(d, a, b, states[:, sidx:sidx + 1], ALU.mult, ALU.add),
                             waits=[("A", exp_A)])
                    sch.emit("V", lambda e, d=hv, sidx=sidx:
                             e.tensor_copy(states[:, sidx:sidx + 1], d[:, Lp - 1:Lp]))
                    w4b = [("A", cevac[n])] if dci == 0 else []
                    if it >= 2:
                        w4b.append(("T", tmm_T[it - 2]))
                    sch.emit("V", lambda e, d=tv, a=hv, n=n:
                             e.tensor_mul(d, a, crep[:, n * Lp:(n + 1) * Lp]), waits=w4b, inc=1)
                    da_freeV.append(sch.now("V"))
                    t_V = sch.now("V")
                    for hb in range(HB):
                        w5 = [("V", t_V)] if hb == 0 else []
                        if n == 0 and hb == 0 and psy_freeV:
                            w5.append(("V", psy_freeV[-1]))
                        sch.emit("T", lambda e, d=tv, hb=hb, n=n:
                                 e.matmul(ps_y[:, hb * 512:(hb + 1) * 512], ident[:],
                                          d[:, hb * 512:(hb + 1) * 512],
                                          start=(n == 0), stop=(n == NS - 1)),
                                 waits=w5, inc=(1 if hb == HB - 1 else 0))
                    tmm_T.append(sch.now("T"))
                    if dci == 0 and n < NS - 2:
                        p2_rep_build(lp, n + 2)
                    if n == 4 and dci + 1 < NDC:
                        p2_prolog(lp, dci + 1)
                    if gen is not None and bud[0] > 0 and dci >= 1 and n % 2 == 1:
                        next(gen)
                        bud[0] -= 1
                y_T = sch.now("T")
                sch.emit("V", lambda e, a=xcv, dc=dc:
                         e.scalar_tensor_tensor(t2_sl, a, dv_sb[:, dc:dc + 1], ps_y[:, 0:Lp],
                                                ALU.mult, ALU.add),
                         waits=[("T", y_T)], inc=1)
                psy_freeV.append(sch.now("V"))
                chY = f"ygs{dsl}"
                wyg = []
                if len(ygs_tot) >= 2:
                    wyg.append((chY, ygs_tot[-2]))
                sch.emit("V", lambda e, d=ygv, b=szv: e.tensor_mul(d, t2_sl, b), waits=wyg, inc=1)
                yg_V = sch.now("V")
                p2_ld_freeV.append(yg_V)
                sch.dma(chY, lambda e, dc=dc, g0=g0, g1=g1, d=ygv:
                        e.dma_start(out=YG[dc, :, g0:g1], in_=d),
                        waits=[("V", yg_V)], eng="G")
                ygs_tot.append(sch.now(chY))

        gen = p1_gen()
        for _ in range(2 * NPIECE):
            next(gen)
        record_bar(0, 1)
        for lp in range(nlp):
            if lp < nlp - 1:
                bud = [2 * NPIECE]
                p2_lp(lp, gen, bud)
                assert bud[0] == 0, (li, lp, bud[0])
                record_bar(lp + 1, lp % 2)
            else:
                p2_lp(lp)
        exhausted = False
        try:
            next(gen)
        except StopIteration:
            exhausted = True
        assert exhausted, "p1 generator not exhausted"
        psbc_prev[0] = rep_evacA[-1]
        p1_A = sch.now("A")
        p2_bar = [("ygs0", sch.now("ygs0")), ("ygs1", sch.now("ygs1"))]

        p3_freeV = []
        wowin_freeT = []
        p3psum_freeV = []
        xas_tots = []

        def p3_span(s):
            l0, l1 = s * Ts, (s + 1) * Ts
            xv = xsp3[s % 2]
            chX = f"xl{s % 2}"
            chYL = f"xcl{s % 2}"
            w = list(p2_bar) if s == 0 else [("V", p3_freeV[-1])]
            for dc in range(NDC):
                sch.dma(chYL, lambda e, dc=dc, l0=l0, l1=l1:
                        e.dma_start(out=ygsp[:, dc * Ts:(dc + 1) * Ts], in_=YG[dc, :, l0:l1]),
                        waits=(w if dc == 0 else []))
            ygl_S = sch.now(chYL)
            for kc in range(KC):
                wxl = []
                if s >= 2 and kc == 0:
                    wxl = [("V", p3_freeV[-2]), xas_tots[s - 2]]
                sch.dma(chX, lambda e, kc=kc, l0=l0, l1=l1, xv=xv, li=li:
                        e.dma_start(out=xv[:, kc * Ts:(kc + 1) * Ts], in_=x_src(li, kc, l0, l1)),
                        waits=wxl)
            xl_S = sch.now(chX)
            for kc in range(KC):
                gi = s * KC + kc
                par = kc % 2
                chW = f"wol{par}"
                wv = wowin[:, par * NDC * 128:(par + 1) * NDC * 128]
                w2 = list(wstage) if gi < 2 else [("T", wowin_freeT[gi - 2])]
                sch.dma(chW, lambda e, d=wv, kc=kc: e.dma_start(out=d, in_=WOUTB[kc]), waits=w2)
                wl_S = sch.now(chW)
                psel = ps_mm if par == 0 else ps_mm2
                for dc in range(NDC):
                    w3 = []
                    if dc == 0:
                        w3 = [(chYL, ygl_S), (chW, wl_S)]
                        if gi >= 2:
                            w3.append(("V", p3psum_freeV[gi - 2]))
                        else:
                            w3.append(("A", p1_A))
                    sch.emit("T", lambda e, dc=dc, wv=wv, psel=psel:
                             e.matmul(psel[:, 0:Ts], wv[:, dc * 128:(dc + 1) * 128],
                                      ygsp[:, dc * Ts:(dc + 1) * Ts],
                                      start=(dc == 0), stop=(dc == NDC - 1)),
                             waits=w3, inc=(1 if dc == NDC - 1 else 0))
                mm_T = sch.now("T")
                wowin_freeT.append(mm_T)
                sch.emit("V", lambda e, kc=kc, xv=xv, psel=psel:
                         e.tensor_add(xv[:, kc * Ts:(kc + 1) * Ts], xv[:, kc * Ts:(kc + 1) * Ts],
                                      psel[:, 0:Ts]),
                         waits=([("T", mm_T), (chX, xl_S)] if kc == 0 else [("T", mm_T)]), inc=1)
                p3psum_freeV.append(sch.now("V"))
                xa_V = sch.now("V")
                sch.dma(f"xas{s % 2}", lambda e, kc=kc, l0=l0, l1=l1, xv=xv:
                        e.dma_start(out=xA[kc, :, l0:l1], in_=xv[:, kc * Ts:(kc + 1) * Ts]),
                        waits=[("V", xa_V)], eng="G")
            p3_freeV.append(sch.now("V"))
            xas_tots.append((f"xas{s % 2}", sch.now(f"xas{s % 2}")))

        for s in range(nsp):
            p3_span(s)
        xas_tot = [sch.now("xas0"), sch.now("xas1")]

    p3end = [("xas0", xas_tot[0]), ("xas1", xas_tot[1])]
    wq_Alist = []
    for oc in range(KC):
        w = list(p3end) if oc == 0 else [("A", wq_Alist[-1])]
        sch.dma("misc", lambda e, oc=oc: e.dma_start(out=stg_f[:, 0:512], in_=fw[oc, :, 0:512]), waits=w)
        sch.dma("misc", lambda e, oc=oc: e.dma_start(out=stg_f[:, 512:1024], in_=fw[oc, :, 512:1024]))
        fw_S = sch.now("misc")
        sch.emit("V", lambda e: e.reduce_max(am_sb[:, 0:1], stg_f[:, 0:DM],
                                             axis=mybir.AxisListType.X, apply_absolute_value=True),
                 waits=[("misc", fw_S)])
        sch.emit("V", lambda e: e.tensor_scalar_max(am_sb[:, 0:1], am_sb[:, 0:1], FFN_EPS))
        sch.emit("V", lambda e: e.reciprocal(am_sb[:, 1:2], am_sb[:, 0:1]))
        sch.emit("V", lambda e, oc=oc: e.tensor_scalar_mul(wsc_sb[:, oc:oc + 1], am_sb[:, 0:1], 1.0 / QMAX))
        sch.emit("V", lambda e: e.tensor_scalar_mul(am_sb[:, 1:2], am_sb[:, 1:2], QMAX))
        sch.emit("V", lambda e: e.tensor_scalar(stg_f[:, 0:DM], stg_f[:, 0:DM],
                                                am_sb[:, 1:2], RND_M, ALU.mult, ALU.add))
        sch.emit("V", lambda e: e.tensor_scalar(stg_f[:, 0:DM], stg_f[:, 0:DM],
                                                RND_M, None, ALU.subtract), inc=1)
        wq_V = sch.now("V")
        for ic in range(KC):
            sch.emit("T", lambda e, ic=ic: e.transpose(ps_aux[:, 128:256], stg_f[:, ic * 128:(ic + 1) * 128],
                                                       identf[:]),
                     waits=([("V", wq_V)] if ic == 0 else [("A", wq_Alist[-1])]), inc=1)
            tr_T = sch.now("T")
            sch.emit("A", lambda e, oc=oc, ic=ic:
                     e.activation(out=wqt_sb[:, (ic * KC + oc) * 128:(ic * KC + oc + 1) * 128],
                                  in_=ps_aux[:, 128:256], func=AF.Copy),
                     waits=[("T", tr_T)] + ([("ygs0", sch.now("ygs0")), ("ygs1", sch.now("ygs1"))]
                                            if (oc == 0 and ic == 0) else []),
                     inc=1)
            wq_Alist.append(sch.now("A"))
    sch.dma("idl", lambda e: e.dma_start(out=fb_sb[:, 0:KC], in_=fb[:, :]))
    wq_A = sch.now("A")
    fb_S = sch.now("idl")

    ffn_freeV = []
    ffn_store = []

    def ln_span(src_sp, dst_sp, first_waits):
        for kc in range(KC):
            sch.emit("A", lambda e, kc=kc, s_=src_sp:
                     e.activation(out=ub_sp[:, kc * Ts:(kc + 1) * Ts], in_=s_[:, kc * Ts:(kc + 1) * Ts],
                                  func=AF.Copy),
                     waits=(list(first_waits) if kc == 0 else []), inc=(1 if kc == KC - 1 else 0))
        cast_A = sch.now("A")
        for kc in range(KC):
            sch.emit("T", lambda e, kc=kc:
                     e.matmul(ps_st[0:1, 0:Ts], ones_bc[:], ub_sp[:, kc * Ts:(kc + 1) * Ts],
                              start=(kc == 0), stop=(kc == KC - 1)),
                     waits=([("A", cast_A)] if kc == 0 else []), inc=(1 if kc == KC - 1 else 0))
        sum_T = sch.now("T")
        sch.emit("V", lambda e: e.tensor_scalar_mul(rowbuf[0:1, 0:Ts], ps_st[0:1, 0:Ts], 1.0 / DM),
                 waits=[("T", sum_T)], inc=1)
        m_V = sch.now("V")
        for kc in range(KC):
            sch.emit("A", lambda e, kc=kc, s_=src_sp:
                     e.activation(out=ub_sp[:, kc * Ts:(kc + 1) * Ts], in_=s_[:, kc * Ts:(kc + 1) * Ts],
                                  func=AF.Square),
                     waits=([("T", sum_T)] if kc == 0 else []), inc=(1 if kc == KC - 1 else 0))
        sq_A = sch.now("A")
        for kc in range(KC):
            sch.emit("T", lambda e, kc=kc:
                     e.matmul(ps_st[0:1, 0:Ts], ones_bc[:], ub_sp[:, kc * Ts:(kc + 1) * Ts],
                              start=(kc == 0), stop=(kc == KC - 1)),
                     waits=([("A", sq_A), ("V", m_V)] if kc == 0 else []), inc=(1 if kc == KC - 1 else 0))
        sq_T = sch.now("T")
        sch.emit("V", lambda e: e.tensor_scalar_mul(rowbuf[0:1, Ts:2 * Ts], ps_st[0:1, 0:Ts], 1.0 / DM),
                 waits=[("T", sq_T)])
        sch.emit("V", lambda e: e.tensor_mul(rowbuf[0:1, 2 * Ts:3 * Ts], rowbuf[0:1, 0:Ts], rowbuf[0:1, 0:Ts]))
        sch.emit("V", lambda e: e.tensor_sub(rowbuf[0:1, Ts:2 * Ts], rowbuf[0:1, Ts:2 * Ts],
                                             rowbuf[0:1, 2 * Ts:3 * Ts]))
        sch.emit("V", lambda e: e.tensor_scalar_add(rowbuf[0:1, Ts:2 * Ts], rowbuf[0:1, Ts:2 * Ts], LN_EPS),
                 inc=1)
        var_V = sch.now("V")
        sch.emit("A", lambda e: e.activation(out=rowbuf[0:1, Ts:2 * Ts], in_=rowbuf[0:1, Ts:2 * Ts], func=AF.Ln),
                 waits=[("V", var_V)])
        sch.emit("A", lambda e: e.activation(out=rowbuf[0:1, Ts:2 * Ts], in_=rowbuf[0:1, Ts:2 * Ts],
                                             func=AF.Exp, scale=-0.5), inc=1)
        rstd_A = sch.now("A")
        sch.emit("V", lambda e: e.tensor_mul(rowbuf[0:1, 2 * Ts:3 * Ts], rowbuf[0:1, 0:Ts],
                                             rowbuf[0:1, Ts:2 * Ts]),
                 waits=[("A", rstd_A)], inc=1)
        mr_V = sch.now("V")
        sch.emit("T", lambda e: e.matmul(ps_aux[:, 0:Ts], ones_f[:], rowbuf[0:1, Ts:2 * Ts],
                                         start=True, stop=True),
                 waits=[("V", mr_V)])
        sch.emit("T", lambda e: e.matmul(ps_bc[:, 0:Ts], ones_f[:], rowbuf[0:1, 2 * Ts:3 * Ts],
                                         start=True, stop=True), inc=1)
        bc_T = sch.now("T")
        for kc in range(KC):
            sch.emit("V", lambda e, kc=kc, s_=src_sp, d_=dst_sp:
                     e.tensor_mul(d_[:, kc * Ts:(kc + 1) * Ts], s_[:, kc * Ts:(kc + 1) * Ts], ps_aux[:, 0:Ts]),
                     waits=([("T", bc_T)] if kc == 0 else []))
            sch.emit("V", lambda e, kc=kc, d_=dst_sp:
                     e.tensor_sub(d_[:, kc * Ts:(kc + 1) * Ts], d_[:, kc * Ts:(kc + 1) * Ts], ps_bc[:, 0:Ts]),
                     inc=(1 if kc == KC - 1 else 0))
        return sch.now("V")

    def ffn_span(s):
        l0, l1 = s * Ts, (s + 1) * Ts
        if s == 0:
            w = [("A", wq_A), ("idl", fb_S)]
        else:
            w = [("V", ffn_freeV[-1]), ("fos", ffn_store[-1])]
        for kc in range(KC):
            sch.dma("ffl", lambda e, kc=kc, l0=l0, l1=l1:
                    e.dma_start(out=u_sp[:, kc * Ts:(kc + 1) * Ts], in_=x_in[kc * 128:(kc + 1) * 128, l0:l1]),
                    waits=(w if kc == 0 else []))
            sch.dma("ffl", lambda e, kc=kc, l0=l0, l1=l1:
                    e.dma_start(out=o1_sp[:, kc * Ts:(kc + 1) * Ts], in_=xA[kc, :, l0:l1]))
        ld_S = sch.now("ffl")
        for kc in range(KC):
            sch.emit("V", lambda e, kc=kc:
                     e.tensor_add(u_sp[:, kc * Ts:(kc + 1) * Ts], u_sp[:, kc * Ts:(kc + 1) * Ts],
                                  o1_sp[:, kc * Ts:(kc + 1) * Ts]),
                     waits=([("ffl", ld_S)] if kc == 0 else []), inc=(1 if kc == KC - 1 else 0))
        u_V = sch.now("V")
        ln1_V = ln_span(u_sp, o1_sp, [("V", u_V)])
        amred_V = None
        for tb in range(NT):
            for kc in range(KC):
                w_tr = []
                if kc == 0:
                    w_tr = [("V", ln1_V)] if tb == 0 else [("V", amred_V)]
                sch.emit("T", lambda e, kc=kc, tb=tb:
                         e.transpose(ps_bc[:, kc * 128:(kc + 1) * 128],
                                     o1_sp[:, kc * Ts + tb * 128:kc * Ts + (tb + 1) * 128], identf[:]),
                         waits=w_tr, inc=(1 if kc == KC - 1 else 0))
            tr_T = sch.now("T")
            sch.emit("V", lambda e, tb=tb:
                     e.reduce_max(am_sb[:, tb:tb + 1], ps_bc[:, 0:DM],
                                  axis=mybir.AxisListType.X, apply_absolute_value=True),
                     waits=[("T", tr_T)], inc=1)
            amred_V = sch.now("V")
        sch.emit("V", lambda e: e.tensor_scalar_max(am_sb[:, 0:NT], am_sb[:, 0:NT], FFN_EPS), inc=1)
        amc_V = sch.now("V")
        for c in range(NT):
            sch.emit("T", lambda e, c=c: e.transpose(ps_aux[0:1, c * 128:(c + 1) * 128],
                                                     am_sb[:, c:c + 1], identf[:]),
                     waits=([("V", amc_V)] if c == 0 else []), inc=(1 if c == NT - 1 else 0))
        amt_T = sch.now("T")
        sch.emit("A", lambda e: e.activation(out=rowbuf[0:1, 0:Ts], in_=ps_aux[0:1, 0:Ts],
                                             func=AF.Copy, scale=1.0 / QMAX),
                 waits=[("T", amt_T)], inc=1)
        srow_A = sch.now("A")
        sch.emit("V", lambda e: e.reciprocal(rowbuf[0:1, Ts:2 * Ts], rowbuf[0:1, 0:Ts]),
                 waits=[("A", srow_A)], inc=1)
        rrow_V = sch.now("V")
        sch.emit("T", lambda e: e.matmul(ps_aux[:, 0:Ts], ones_f[:], rowbuf[0:1, Ts:2 * Ts],
                                         start=True, stop=True),
                 waits=[("V", rrow_V)])
        sch.emit("T", lambda e: e.matmul(ps_bc[:, 0:Ts], ones_f[:], rowbuf[0:1, 0:Ts],
                                         start=True, stop=True), inc=1)
        scl_T = sch.now("T")
        sch.emit("A", lambda e: e.activation(out=srep_sp, in_=ps_bc[:, 0:Ts], func=AF.Copy),
                 waits=[("T", scl_T)], inc=1)
        srep_A = sch.now("A")
        for kc in range(KC):
            sch.emit("V", lambda e, kc=kc:
                     e.tensor_mul(u_sp[:, kc * Ts:(kc + 1) * Ts], o1_sp[:, kc * Ts:(kc + 1) * Ts],
                                  ps_aux[:, 0:Ts]),
                     waits=([("T", scl_T)] if kc == 0 else []))
            sch.emit("V", lambda e, kc=kc:
                     e.tensor_scalar(u_sp[:, kc * Ts:(kc + 1) * Ts], u_sp[:, kc * Ts:(kc + 1) * Ts],
                                     RND_M, RND_M, ALU.add, ALU.subtract))
            sch.emit("V", lambda e, kc=kc:
                     e.tensor_copy(xq_sp[:, kc * Ts:(kc + 1) * Ts], u_sp[:, kc * Ts:(kc + 1) * Ts]),
                     inc=(1 if kc == KC - 1 else 0))
        xq_V = sch.now("V")
        ffnq_freeV = []
        for oc in range(KC):
            psel = ps_mm if oc % 2 == 0 else ps_mm2
            for ic in range(KC):
                w_q = []
                if ic == 0:
                    w_q = [("V", xq_V)]
                    if oc >= 2:
                        w_q.append(("V", ffnq_freeV[oc - 2]))
                sch.emit("T", lambda e, oc=oc, ic=ic, psel=psel:
                         e.matmul(psel[:, 0:Ts], wqt_sb[:, (ic * KC + oc) * 128:(ic * KC + oc + 1) * 128],
                                  xq_sp[:, ic * Ts:(ic + 1) * Ts],
                                  start=(ic == 0), stop=(ic == KC - 1)),
                         waits=w_q, inc=(1 if ic == KC - 1 else 0))
            q_T = sch.now("T")
            sch.emit("V", lambda e, oc=oc, psel=psel:
                     e.tensor_mul(u_sp[:, oc * Ts:(oc + 1) * Ts], psel[:, 0:Ts], srep_sp),
                     waits=[("T", q_T)] + ([("A", srep_A)] if oc == 0 else []))
            sch.emit("V", lambda e, oc=oc:
                     e.tensor_scalar(u_sp[:, oc * Ts:(oc + 1) * Ts], u_sp[:, oc * Ts:(oc + 1) * Ts],
                                     wsc_sb[:, oc:oc + 1], fb_sb[:, oc:oc + 1], ALU.mult, ALU.add))
            sch.emit("V", lambda e, oc=oc:
                     e.tensor_add(u_sp[:, oc * Ts:(oc + 1) * Ts], u_sp[:, oc * Ts:(oc + 1) * Ts],
                                  o1_sp[:, oc * Ts:(oc + 1) * Ts]), inc=1)
            ffnq_freeV.append(sch.now("V"))
        res_V = sch.now("V")
        ln2_V = ln_span(u_sp, o1_sp, [("V", res_V)])
        for kc in range(KC):
            sch.dma("fos", lambda e, kc=kc, l0=l0, l1=l1:
                    e.dma_start(out=y_out[kc * 128:(kc + 1) * 128, l0:l1],
                                in_=o1_sp[:, kc * Ts:(kc + 1) * Ts]),
                    waits=([("V", ln2_V)] if kc == 0 else []), eng="G")
        ffn_freeV.append(ln2_V)
        ffn_store.append(sch.now("fos"))

    for s in range(nsp):
        ffn_span(s)
    out_S = sch.now("fos")
    sch.emit("V", lambda e: e.nop(), waits=[("fos", out_S)])

    block = ctx.enter_context(nc.Block())
    by_eng = {"T": [], "A": [], "V": [], "S": [], "G": []}
    for eng, waits, fn, inc_key, inc_amt in sch.items:
        by_eng[eng].append((waits, fn, inc_key, inc_amt))

    def runner(key):
        def _f(eng):
            for waits, fn, inc_key, inc_amt in by_eng[key]:
                for (wk, wv) in waits:
                    if wv > 0:
                        eng.wait_ge(SEMS[wk], wv)
                ins = fn(eng)
                if inc_key is not None:
                    assert ins is not None
                    ins.then_inc(SEMS[inc_key], inc_amt)
        return _f

    block.tensor(runner("T"))
    block.scalar(runner("A"))
    block.vector(runner("V"))
    block.sync(runner("S"))
    block.gpsimd(runner("G"))
    ctx.close()
    return nc


def pack_inputs(inp, b, L=8192, layers=4):
    import ml_dtypes
    x = np.ascontiguousarray(np.asarray(inp["L0_attn_x"])[b].T)
    W_in = np.asarray(inp["L0_attn_W_in"])[:layers]
    winT = np.ascontiguousarray(
        W_in.transpose(0, 2, 1).reshape(layers, KC, 128, NEC, 128).transpose(0, 3, 1, 2, 4))
    W_out = np.asarray(inp["L0_attn_W_out"])[:layers]
    woutT = np.ascontiguousarray(
        W_out.transpose(0, 2, 1).reshape(layers, NDC, 128, KC, 128).transpose(0, 3, 1, 2, 4))
    W_x = np.asarray(inp["L0_attn_W_x"])[:layers]
    wxT = np.ascontiguousarray(
        W_x.transpose(0, 2, 1).reshape(layers, NDC, 128, 96)
        .transpose(0, 2, 1, 3).reshape(layers, 128, NDC * 96))
    W_dt = np.asarray(inp["L0_attn_W_dt"])[:layers]
    wdtT = np.ascontiguousarray(W_dt.transpose(0, 2, 1))
    alog = np.ascontiguousarray(np.asarray(inp["L0_attn_A_log"])[:layers]
                                .reshape(layers, NDC, 128, NS).transpose(0, 2, 1, 3)
                                .reshape(layers, 128, NDC * NS))
    convw = np.ascontiguousarray(np.asarray(inp["L0_attn_conv_w"])[:layers]
                                 .reshape(layers, NDC, 128, DCV).transpose(0, 2, 1, 3)
                                 .reshape(layers, 128, NDC * DCV))
    convb = np.ascontiguousarray(np.asarray(inp["L0_attn_conv_b"])[:layers]
                                 .reshape(layers, NDC, 128).transpose(0, 2, 1))
    bdt = np.ascontiguousarray(np.asarray(inp["L0_attn_b_dt"])[:layers]
                               .reshape(layers, NDC, 128).transpose(0, 2, 1))
    dvec = np.ascontiguousarray(np.asarray(inp["L0_attn_D"])[:layers]
                                .reshape(layers, NDC, 128).transpose(0, 2, 1))
    normw = np.ascontiguousarray(np.asarray(inp["L0_attn_norm_w"])[:layers]
                                 .reshape(layers, KC, 128).transpose(0, 2, 1))
    fwt = np.ascontiguousarray(np.asarray(inp["L0_ffn_weight"]).reshape(KC, 128, DM))
    fbt = np.ascontiguousarray(np.asarray(inp["L0_ffn_bias"]).reshape(KC, 128).T)
    return {
        "x_in": x.astype(np.float32),
        "winT": winT.astype(np.float32), "woutT": woutT.astype(np.float32),
        "wxT": wxT.astype(np.float32), "wdtT": wdtT.astype(np.float32),
        "alog": alog.astype(np.float32), "convw": convw.astype(np.float32),
        "convb": convb.astype(np.float32), "bdt": bdt.astype(np.float32),
        "dvec": dvec.astype(np.float32), "normw": normw.astype(np.float32),
        "fw": fwt.astype(np.float32), "fb": fbt.astype(np.float32),
        "ident_in": np.eye(128, dtype=ml_dtypes.bfloat16),
        "identf_in": np.eye(128, dtype=np.float32),
    }


def kernel(**inputs):
    from concourse.bass_utils import run_bass_kernel_spmd
    L = 8192
    layers = 4
    nc = build_program(L=L, layers=layers, debug_internals=False)
    im = [pack_inputs(inputs, 0, L=L, layers=layers),
          pack_inputs(inputs, 1, L=L, layers=layers)]
    in_maps = [im[c] if c < 2 else im[0] for c in range(8)]
    res = run_bass_kernel_spmd(nc, in_maps, core_ids=list(range(8)))
    out0 = np.ascontiguousarray(res.results[0]["y_out"].T)
    out1 = np.ascontiguousarray(res.results[1]["y_out"].T)
    return np.stack([out0, out1]).astype(np.float32)


# version 08
# speedup vs baseline: 1.0068x; 1.0068x over previous
import numpy as np
from contextlib import ExitStack

import concourse.bass as bass
import concourse.mybir as mybir

F32 = mybir.dt.float32
BF16 = mybir.dt.bfloat16
AF = mybir.ActivationFunctionType
ALU = mybir.AluOpType

DM, DI, RK, NS, DCV = 1024, 2048, 64, 16, 4
KC, NDC, NEC = DM // 128, DI // 128, 2 * DI // 128
RMS_EPS = 1e-5
LN_EPS = 1e-5
FFN_EPS = 1e-8
QMAX = 127.0
RND_M = float(3 * 2 ** 22)

DMA_CH = ["idl", "misc", "stgl0", "stgl1", "stgs0", "stgs1", "xl0", "xl1",
          "wwl0", "wwl1", "wol0", "wol1", "xcl0", "xcl1", "brl0", "brl1",
          "xcs00", "xcs01", "xcs10", "xcs11", "szs00", "szs01", "szs10", "szs11",
          "ygs0", "ygs1", "bcs0", "bcs1", "drl",
          "xas0", "xas1", "ffl", "fos"]
ENG = ["T", "A", "V", "G"]


class Sched:
    def __init__(self):
        self.items = []
        self.cnt = {k: 0 for k in ENG + DMA_CH}

    SELF_SERIAL = ("A", "V", "G")

    def emit(self, eng, fn, waits=(), inc=0):
        waits = list(waits)
        if eng in self.SELF_SERIAL:
            waits.append((eng, self.cnt[eng]))
            inc = 1
        self.items.append((eng, tuple(waits), fn, eng if inc else None, inc))
        if inc:
            self.cnt[eng] += inc
        return self.cnt[eng]

    def dma(self, ch, fn, waits=(), eng="S"):
        assert ch in DMA_CH, ch
        self.items.append((eng, tuple(waits), fn, ch, 16))
        self.cnt[ch] += 16
        return self.cnt[ch]

    def now(self, key):
        return self.cnt[key]


def build_program(L=8192, layers=4, Ts=512, Lp=1024, debug_internals=False):
    nsp = L // Ts
    nlp = L // Lp
    HB = Lp // 512
    NT = Ts // 128
    nc = bass.Bass("TRN2", target_bir_lowering=False, debug=False)

    EI, EO, IN = "ExternalInput", "ExternalOutput", "Internal"
    x_in = nc.dram_tensor("x_in", [DM, L], F32, kind=EI).ap()
    winT = nc.dram_tensor("winT", [layers, NEC, KC, 128, 128], F32, kind=EI).ap()
    woutT = nc.dram_tensor("woutT", [layers, KC, NDC, 128, 128], F32, kind=EI).ap()
    wxT = nc.dram_tensor("wxT", [layers, 128, NDC * 96], F32, kind=EI).ap()
    wdtT = nc.dram_tensor("wdtT", [layers, 64, NDC * 128], F32, kind=EI).ap()
    alog = nc.dram_tensor("alog", [layers, 128, NDC * NS], F32, kind=EI).ap()
    convw = nc.dram_tensor("convw", [layers, 128, NDC * DCV], F32, kind=EI).ap()
    convb = nc.dram_tensor("convb", [layers, 128, NDC], F32, kind=EI).ap()
    bdt = nc.dram_tensor("bdt", [layers, 128, NDC], F32, kind=EI).ap()
    dvec = nc.dram_tensor("dvec", [layers, 128, NDC], F32, kind=EI).ap()
    normw = nc.dram_tensor("normw", [layers, 128, KC], F32, kind=EI).ap()
    fw = nc.dram_tensor("fw", [KC, 128, DM], F32, kind=EI).ap()
    fb = nc.dram_tensor("fb", [128, KC], F32, kind=EI).ap()
    ident_in = nc.dram_tensor("ident_in", [128, 128], BF16, kind=EI).ap()
    identf_in = nc.dram_tensor("identf_in", [128, 128], F32, kind=EI).ap()

    y_out = nc.dram_tensor("y_out", [DM, L], F32, kind=EO).ap()

    ikind = EO if debug_internals else IN
    xA = nc.dram_tensor("xA", [KC, 128, L], F32, kind=ikind).ap()
    XC = nc.dram_tensor("XCb", [NDC, 128, L], F32, kind=ikind).ap()
    SZ = nc.dram_tensor("SZb", [NDC, 128, L], BF16, kind=ikind).ap()
    YG = nc.dram_tensor("YGb", [NDC, 128, L], BF16, kind=ikind).ap()
    BCB = nc.dram_tensor("BCB", [2, NS, L], BF16, kind=ikind).ap()
    DR = nc.dram_tensor("DRb", [64, L], BF16, kind=IN).ap()
    WINB = nc.dram_tensor("WINB", [NEC, 128, KC * 128], BF16, kind=IN).ap()
    WOUTB = nc.dram_tensor("WOUTB", [KC, 128, NDC * 128], BF16, kind=IN).ap()

    ctx = ExitStack()
    sb = lambda name, shape, dt: ctx.enter_context(nc.sbuf_tensor(name, shape, dt))
    psb = lambda name, shape, dt: ctx.enter_context(nc.psum_tensor(name, shape, dt))

    dr_sb = sb("dr_sb", [64, 2 * Lp], BF16)
    stage96 = sb("stage96", [128, 512], BF16)
    browA = sb("browA", [1, 2 * Lp], BF16)
    browC = sb("browC", [1, 2 * Lp], BF16)
    states = sb("states", [128, NDC * NS], F32)
    halo = sb("halo", [128, NDC * (DCV - 1)], F32)
    wx_sb = sb("wx_sb", [128, NDC * 96], F32)
    wdt_sb = sb("wdt_sb", [64, NDC * 128], BF16)
    A_sb = sb("A_sb", [128, NDC * NS], F32)
    cw_sb = sb("cw_sb", [128, NDC * DCV], F32)
    cb_sb = sb("cb_sb", [128, NDC], F32)
    bdt_sb = sb("bdt_sb", [128, NDC], F32)
    dv_sb = sb("dv_sb", [128, NDC], F32)
    nw_sb = sb("nw_sb", [128, KC], F32)
    wsc_sb = sb("wsc_sb", [128, KC], F32)
    fb_sb = sb("fb_sb", [128, KC], F32)
    am_sb = sb("am_sb", [128, 2 * NT], F32)
    ident = sb("ident", [128, 128], BF16)
    identf = sb("identf", [128, 128], F32)
    ones_f = sb("ones_f", [1, 128], F32)
    ones_bf = sb("ones_bf", [1, 128], BF16)
    ones_bc = sb("ones_bc", [128, 1], BF16)
    wwin = sb("wwin", [128, 2 * KC * 128], BF16)
    wowin = sb("wowin", [128, 2 * NDC * 128], BF16)
    stg_f = sb("stg_f", [128, 2048], F32)
    stg_b = sb("stg_b", [128, 2048], BF16)
    rowbuf = sb("rowbuf", [1, 3 * Ts], F32)

    AFC = 13318
    ABC = 50176
    arena_f = sb("arena_f", [128, AFC], F32)
    arena_b = sb("arena_b", [128, ABC], BF16)

    def fsl(off, cols):
        assert off + cols <= AFC, (off, cols)
        return arena_f[:, off:off + cols]

    def bsl(off, cols):
        assert off + cols <= ABC, (off, cols)
        return arena_b[:, off:off + cols]

    delta_sl = [fsl(i * Lp, Lp) for i in range(2)]
    da_sl = [fsl(2 * Lp + i * Lp, Lp) for i in range(2)]
    xc2_sl = [fsl(4 * Lp + i * Lp, Lp) for i in range(2)]
    t2_sl = fsl(6 * Lp, Lp)
    _p1 = 7 * Lp
    xsp = fsl(_p1, KC * Ts)
    xi_buf = [fsl(_p1 + KC * Ts + i * (Ts + 3), Ts + 3) for i in range(2)]
    xc_sl = [fsl(_p1 + KC * Ts + 2 * (Ts + 3) + i * Ts, Ts) for i in range(2)]
    brep = bsl(0, NS * Lp)
    crep = bsl(NS * Lp, NS * Lp)
    _o = 2 * NS * Lp
    dx_sl = [bsl(_o + i * Lp, Lp) for i in range(2)]
    bx_sl = [bsl(_o + 2 * Lp + i * Lp, Lp) for i in range(2)]
    h_sl = [bsl(_o + 4 * Lp + i * Lp, Lp) for i in range(2)]
    t_sl = [bsl(_o + 6 * Lp + i * Lp, Lp) for i in range(2)]
    sz2_sl = [bsl(_o + 8 * Lp + i * Lp, Lp) for i in range(2)]
    yg2_sl = [bsl(_o + 10 * Lp + i * Lp, Lp) for i in range(2)]
    _p1b = _o + 12 * Lp
    hrms = bsl(_p1b, KC * Ts)
    sz_sl = [bsl(_p1b + KC * Ts + i * Ts, Ts) for i in range(2)]

    xsp3 = [fsl(i * KC * Ts, KC * Ts) for i in range(2)]
    ygsp = bsl(0, NDC * Ts)

    u_sp = fsl(0, KC * Ts)
    o1_sp = fsl(KC * Ts, KC * Ts)
    srep_sp = fsl(2 * KC * Ts, Ts)
    xq_sp = bsl(0, KC * Ts)
    ub_sp = bsl(KC * Ts, KC * Ts)
    wqt_sb = bsl(2 * KC * Ts, KC * KC * 128)

    ps_mm = psb("ps_mm", [128, 512], F32)
    ps_mm2 = psb("ps_mm2", [128, 512], F32)
    ps_y = psb("ps_y", [128, Lp], F32)
    ps_bc = psb("ps_bc", [128, 1024], F32)
    ps_st = psb("ps_st", [128, 512], F32)
    ps_aux = psb("ps_aux", [128, 512], F32)

    SEMS = {}
    for k in ENG + DMA_CH:
        SEMS[k] = ctx.enter_context(nc.semaphore("sem_" + k))

    sch = Sched()

    sch.dma("idl", lambda e: e.dma_start(out=ident[:], in_=ident_in[:, :]))
    sch.dma("idl", lambda e: e.dma_start(out=identf[:], in_=identf_in[:, :]))
    pre_S = sch.now("idl")
    sch.emit("V", lambda e: e.memset(ones_f[:], 1.0))
    sch.emit("V", lambda e: e.memset(ones_bf[:], 1.0))
    sch.emit("V", lambda e: e.memset(ones_bc[:], 1.0), inc=1)
    pre_V = sch.now("V")

    def x_src(li, kc, l0, l1):
        if li == 0:
            return x_in[kc * 128:(kc + 1) * 128, l0:l1]
        return xA[kc, :, l0:l1]

    xas_tot = [0, 0]
    psbc_prev = [0]

    for li in range(layers):
        sch.emit("V", lambda e: e.memset(states[:], 0.0))
        sch.emit("V", lambda e: e.memset(halo[:], 0.0), inc=1)
        zero_V = sch.now("V")

        for (dst, src) in ((cw_sb, convw[li]), (cb_sb, convb[li]), (bdt_sb, bdt[li]),
                           (dv_sb, dvec[li]), (nw_sb, normw[li]), (A_sb, alog[li])):
            sch.dma("misc", lambda e, d=dst, s=src: e.dma_start(out=d[:], in_=s))
        cA_S = sch.now("misc")
        sch.emit("A", lambda e: e.activation(out=A_sb[:], in_=A_sb[:], func=AF.Exp),
                 waits=[("misc", cA_S)], inc=1)
        expA_A = sch.now("A")
        sch.emit("V", lambda e: e.tensor_scalar_mul(A_sb[:], A_sb[:], -1.0),
                 waits=[("A", expA_A)], inc=1)
        negA_V = sch.now("V")
        wxw = [("V", negA_V), ("A", sch.now("A"))]
        if li > 0:
            wxw.append(("T", sch.now("T")))
        sch.dma("misc", lambda e, s=wxT[li]: e.dma_start(out=wx_sb[:], in_=s), waits=wxw)
        sch.dma("misc", lambda e, s=wdtT[li]: e.dma_start(out=stg_f[0:64, 0:NDC * 128], in_=s))
        cWd_S = sch.now("misc")
        sch.emit("A", lambda e: e.activation(out=wdt_sb[:], in_=stg_f[0:64, 0:NDC * 128], func=AF.Copy),
                 waits=[("misc", cWd_S)], inc=1)
        const_A = sch.now("A")

        stage_cast = []
        stage_store = {0: [], 1: []}
        units = []
        for ec in range(NEC):
            units.append((winT[li, ec], WINB[ec], 0))
        for kc in range(KC):
            units.append((woutT[li, kc, 0:8], WOUTB[kc], 0))
            units.append((woutT[li, kc, 8:16], WOUTB[kc], 1024))
        for u, (src, dstrow, dcol) in enumerate(units):
            par = u % 2
            chL, chS = f"stgl{par}", f"stgs{par}"
            f_s = stg_f[:, par * 1024:(par + 1) * 1024]
            b_s = stg_b[:, par * 1024:(par + 1) * 1024]
            w = [("A", const_A)] if u < 2 else [("A", stage_cast[u - 2])]
            sch.dma(chL, lambda e, d=f_s, src=src:
                    e.dma_start(out=d.rearrange("p (k c) -> p k c", k=8),
                                in_=src.rearrange("k p c -> p k c")),
                    waits=w)
            sL = sch.now(chL)
            w2 = [(chL, sL)]
            if len(stage_store[par]) >= 1:
                w2.append((chS, stage_store[par][-1]))
            sch.emit("A", lambda e, d=b_s, sf=f_s: e.activation(out=d, in_=sf, func=AF.Copy),
                     waits=w2, inc=1)
            stage_cast.append(sch.now("A"))
            sch.dma(chS, lambda e, d=dstrow, dcol=dcol, sbf=b_s:
                    e.dma_start(out=d[:, dcol:dcol + 1024], in_=sbf),
                    waits=[("A", stage_cast[-1])], eng="A")
            stage_store[par].append(sch.now(chS))
        wstage = [("stgs0", sch.now("stgs0")), ("stgs1", sch.now("stgs1"))]

        NPIECE = 37
        xsp_free = []
        wwin_freeT = []
        psum_freeA = []
        xbA = {0: 0, 1: 0}
        xbV = {0: 0, 1: 0}
        xcs_tot = []
        szs_tot = []
        bcs_tot = []
        wxmm_T = []
        xl_tot = {}
        hrv_prev = [0]
        drbar = [0]
        rep_evacA = []

        def emit_xload(s):
            l0, l1 = s * Ts, (s + 1) * Ts
            if s == 0:
                w = [("V", zero_V), ("xas0", xas_tot[0]), ("xas1", xas_tot[1])]
            else:
                w = [("V", xsp_free[s - 1])]
            for kc in range(KC):
                sch.dma("xl0", lambda e, kc=kc, l0=l0, l1=l1, li=li:
                        e.dma_start(out=xsp[:, kc * Ts:(kc + 1) * Ts], in_=x_src(li, kc, l0, l1)),
                        waits=(w if kc == 0 else []))
            xl_tot[s] = sch.now("xl0")

        def p1_gen():
            for s in range(nsp):
                l0, l1 = s * Ts, (s + 1) * Ts
                gp = (s // 2 - 1) % 2
                if s == 0:
                    emit_xload(0)
                w_sq = [("xl0", xl_tot[s])]
                if wwin_freeT:
                    w_sq.append(("T", wwin_freeT[-1]))
                for kc in range(KC):
                    sch.emit("A", lambda e, kc=kc:
                             e.activation(out=hrms[:, kc * Ts:(kc + 1) * Ts],
                                          in_=xsp[:, kc * Ts:(kc + 1) * Ts], func=AF.Square),
                             waits=(w_sq if kc == 0 else []), inc=(1 if kc == KC - 1 else 0))
                sq_A = sch.now("A")
                yield
                w_st = [("A", sq_A), ("V", pre_V)]
                if hrv_prev[0]:
                    w_st.append(("V", hrv_prev[0]))
                for kc in range(KC):
                    sch.emit("T", lambda e, kc=kc:
                             e.matmul(ps_aux[0:1, 0:Ts], ones_bc[:], hrms[:, kc * Ts:(kc + 1) * Ts],
                                      start=(kc == 0), stop=(kc == KC - 1)),
                             waits=(w_st if kc == 0 else []), inc=(1 if kc == KC - 1 else 0))
                st_T = sch.now("T")
                yield
                sch.emit("V", lambda e: e.tensor_scalar(rowbuf[0:1, 0:Ts], ps_aux[0:1, 0:Ts],
                                                        1.0 / DM, RMS_EPS, ALU.mult, ALU.add),
                         waits=[("T", st_T)], inc=1)
                ms_V = sch.now("V")
                sch.emit("A", lambda e: e.activation(out=rowbuf[0:1, Ts:2 * Ts], in_=rowbuf[0:1, 0:Ts],
                                                     func=AF.Ln),
                         waits=[("V", ms_V)])
                sch.emit("A", lambda e: e.activation(out=rowbuf[0:1, Ts:2 * Ts], in_=rowbuf[0:1, Ts:2 * Ts],
                                                     func=AF.Exp, scale=-0.5), inc=1)
                rstd_A = sch.now("A")
                sch.emit("T", lambda e: e.matmul(ps_aux[:, 0:Ts], ones_f[:], rowbuf[0:1, Ts:2 * Ts],
                                                 start=True, stop=True),
                         waits=[("A", rstd_A), ("idl", pre_S)], inc=1)
                bc_T = sch.now("T")
                yield
                for kc in range(KC):
                    sch.emit("V", lambda e, kc=kc:
                             e.scalar_tensor_tensor(hrms[:, kc * Ts:(kc + 1) * Ts],
                                                    xsp[:, kc * Ts:(kc + 1) * Ts],
                                                    nw_sb[:, kc:kc + 1], ps_aux[:, 0:Ts],
                                                    ALU.mult, ALU.mult),
                             waits=([("T", bc_T)] if kc == 0 else []), inc=(1 if kc == KC - 1 else 0))
                hr_V = sch.now("V")
                hrv_prev[0] = hr_V
                xsp_free.append(hr_V)
                yield
                silu_Vs = {}
                conv_Vs = {}

                def emit_wx(dc):
                    wmx = [("V", silu_Vs[dc])]
                    if dc == 0:
                        wmx.append(("A", const_A))
                        if drbar[0]:
                            wmx.append(("A", drbar[0]))
                    sch.emit("T", lambda e, dc=dc:
                             e.matmul(ps_st[0:96, 0:Ts], wx_sb[:, dc * 96:(dc + 1) * 96],
                                      xc_sl[dc % 2][:, 0:Ts],
                                      start=(dc == 0), stop=(dc == NDC - 1)),
                             waits=wmx, inc=1)
                    wxmm_T.append(sch.now("T"))

                def emit_sig_store(dc):
                    xcv = xc_sl[dc % 2]
                    sg = xi_buf[dc % 2]
                    sch.emit("A", lambda e, xcv=xcv, sg=sg:
                             e.activation(out=sg[:, 0:Ts], in_=xcv, func=AF.Exp, scale=-1.0),
                             waits=[("V", conv_Vs[dc])])
                    sch.emit("A", lambda e, sg=sg:
                             e.activation(out=sg[:, 0:Ts], in_=sg[:, 0:Ts], func=AF.Ln, bias=1.0))
                    sch.emit("A", lambda e, sg=sg:
                             e.activation(out=sg[:, 0:Ts], in_=sg[:, 0:Ts], func=AF.Exp, scale=-1.0),
                             inc=1)
                    xbA[dc % 2] = sch.now("A")
                    sch.emit("V", lambda e, xcv=xcv, sg=sg: e.tensor_mul(xcv, xcv, sg[:, 0:Ts]),
                             waits=[("A", xbA[dc % 2])], inc=1)
                    silu_Vs[dc] = sch.now("V")
                    xbV[dc % 2] = sch.now("V")
                    chXS = f"xcs{gp}{dc % 2}"
                    sch.dma(chXS, lambda e, dc=dc, l0=l0, l1=l1:
                            e.dma_start(out=XC[dc, :, l0:l1], in_=xc_sl[dc % 2]),
                            waits=[("V", silu_Vs[dc])], eng="G")
                    xcs_tot.append((chXS, sch.now(chXS)))

                for ec in range(NEC):
                    gi = s * NEC + ec
                    par = ec % 2
                    chW = f"wwl{par}"
                    wv = wwin[:, par * KC * 128:(par + 1) * KC * 128]
                    w2 = list(wstage) if gi < 2 else [("T", wwin_freeT[gi - 2])]
                    sch.dma(chW, lambda e, d=wv, ec=ec: e.dma_start(out=d, in_=WINB[ec]), waits=w2)
                    wl_S = sch.now(chW)
                    psel = ps_bc[:, 0:512] if ec % 2 == 0 else ps_bc[:, 512:1024]
                    for kc in range(KC):
                        w3 = []
                        if kc == 0:
                            w3 = [(chW, wl_S), ("V", hr_V)]
                            if gi >= 2:
                                w3.append(psum_freeA[gi - 2])
                            if rep_evacA:
                                w3.append(("A", rep_evacA[-1]))
                            elif psbc_prev[0]:
                                w3.append(("A", psbc_prev[0]))
                        sch.emit("T", lambda e, kc=kc, wv=wv, psel=psel:
                                 e.matmul(psel[:, 0:Ts], wv[:, kc * 128:(kc + 1) * 128],
                                          hrms[:, kc * Ts:(kc + 1) * Ts],
                                          start=(kc == 0), stop=(kc == KC - 1)),
                                 waits=w3, inc=(1 if kc == KC - 1 else 0))
                    mm_T = sch.now("T")
                    wwin_freeT.append(mm_T)
                    if ec < NDC:
                        dc = ec
                        if dc >= 2:
                            emit_wx(dc - 2)
                        xb = xi_buf[dc % 2]
                        w4 = [("A", xbA[dc % 2])] if xbA[dc % 2] else []
                        sch.emit("V", lambda e, xb=xb, dc=dc:
                                 e.tensor_copy(xb[:, 0:DCV - 1],
                                               halo[:, dc * (DCV - 1):(dc + 1) * (DCV - 1)]),
                                 waits=w4, inc=1)
                        hv_V = sch.now("V")
                        sch.emit("A", lambda e, xb=xb, psel=psel:
                                 e.activation(out=xb[:, DCV - 1:DCV - 1 + Ts], in_=psel[:, 0:Ts],
                                              func=AF.Copy),
                                 waits=[("T", mm_T), ("V", hv_V)], inc=1)
                        psum_freeA.append(("A", sch.now("A")))
                        ev_A = sch.now("A")
                        xcv = xc_sl[dc % 2]
                        w5 = [("A", ev_A)]
                        if dc == 0:
                            if len(xcs_tot) >= 2:
                                w5.append(xcs_tot[-2])
                        elif xcs_tot:
                            w5.append(xcs_tot[-1])
                        if dc >= 2 or (dc == 1 and wxmm_T):
                            w5.append(("T", wxmm_T[-1]))
                        elif dc == 0 and len(wxmm_T) >= 2:
                            w5.append(("T", wxmm_T[-2]))
                        sch.emit("V", lambda e, xb=xb, xcv=xcv, dc=dc:
                                 e.tensor_scalar(xcv, xb[:, 0:Ts], cw_sb[:, dc * DCV:dc * DCV + 1],
                                                 cb_sb[:, dc:dc + 1], ALU.mult, ALU.add),
                                 waits=w5)
                        for k in range(1, DCV):
                            sch.emit("V", lambda e, xb=xb, xcv=xcv, dc=dc, k=k:
                                     e.scalar_tensor_tensor(xcv, xb[:, k:k + Ts],
                                                            cw_sb[:, dc * DCV + k:dc * DCV + k + 1],
                                                            xcv, ALU.mult, ALU.add))
                        sch.emit("V", lambda e, xb=xb, dc=dc:
                                 e.tensor_copy(halo[:, dc * (DCV - 1):(dc + 1) * (DCV - 1)],
                                               xb[:, Ts:Ts + DCV - 1]),
                                 inc=1)
                        conv_Vs[dc] = sch.now("V")
                        if dc >= 1:
                            emit_sig_store(dc - 1)
                        if dc == 2 and s + 1 < nsp:
                            emit_xload(s + 1)
                    else:
                        if ec == NDC:
                            emit_sig_store(NDC - 1)
                        dcz = ec - NDC
                        par2 = dcz % 2
                        szv = sz_sl[par2]
                        sg = xi_buf[par2]
                        chSS = f"szs{gp}{par2}"
                        wz = [("T", mm_T)]
                        if xbV[par2]:
                            wz.append(("V", xbV[par2]))
                        sch.emit("A", lambda e, sg=sg, psel=psel:
                                 e.activation(out=sg[:, 0:Ts], in_=psel[:, 0:Ts], func=AF.Exp,
                                              scale=-1.0),
                                 waits=wz)
                        sch.emit("A", lambda e, sg=sg:
                                 e.activation(out=sg[:, 0:Ts], in_=sg[:, 0:Ts], func=AF.Ln, bias=1.0))
                        sch.emit("A", lambda e, sg=sg:
                                 e.activation(out=sg[:, 0:Ts], in_=sg[:, 0:Ts], func=AF.Exp,
                                              scale=-1.0), inc=1)
                        xbA[par2] = sch.now("A")
                        wzm = [("A", xbA[par2])]
                        if len(szs_tot) >= 2:
                            wzm.append(szs_tot[-2])
                        sch.emit("V", lambda e, szv=szv, sg=sg, psel=psel:
                                 e.tensor_mul(szv, psel[:, 0:Ts], sg[:, 0:Ts]),
                                 waits=wzm, inc=1)
                        psum_freeA.append(("V", sch.now("V")))
                        xbV[par2] = sch.now("V")
                        sz_V = sch.now("V")
                        sch.dma(chSS, lambda e, dcz=dcz, l0=l0, l1=l1, szv=szv:
                                e.dma_start(out=SZ[dcz, :, l0:l1], in_=szv),
                                waits=[("V", sz_V)], eng="G")
                        szs_tot.append((chSS, sch.now(chSS)))
                    yield
                emit_wx(NDC - 2)
                emit_wx(NDC - 1)
                wx_T = sch.now("T")
                w7 = [("T", wx_T)]
                if bcs_tot:
                    w7.append(bcs_tot[-1])
                sch.emit("A", lambda e:
                         e.activation(out=stage96[0:64, 0:Ts], in_=ps_st[0:64, 0:Ts], func=AF.Copy),
                         waits=w7)
                sch.emit("A", lambda e: e.activation(out=stage96[64:96, 0:Ts], in_=ps_st[64:96, 0:Ts],
                                                     func=AF.Copy), inc=1)
                bc_A = sch.now("A")
                chBC = f"bcs{gp}"
                sch.dma(chBC, lambda e, l0=l0, l1=l1:
                        e.dma_start(out=DR[:, l0:l1], in_=stage96[0:64, 0:Ts]),
                        waits=[("A", bc_A)], eng="A")
                sch.dma(chBC, lambda e, l0=l0, l1=l1:
                        e.dma_start(out=BCB[0, :, l0:l1], in_=stage96[64:80, 0:Ts]),
                        eng="A")
                sch.dma(chBC, lambda e, l0=l0, l1=l1:
                        e.dma_start(out=BCB[1, :, l0:l1], in_=stage96[80:96, 0:Ts]),
                        eng="A")
                bcs_tot.append((chBC, sch.now(chBC)))
                drbar[0] = bc_A
                yield

        gen_bar = {}

        def record_bar(lp, gp):
            gen_bar[lp] = {
                "xc": [(f"xcs{gp}0", sch.now(f"xcs{gp}0")), (f"xcs{gp}1", sch.now(f"xcs{gp}1")),
                       (f"szs{gp}0", sch.now(f"szs{gp}0")), (f"szs{gp}1", sch.now(f"szs{gp}1"))],
                "bc": [(f"bcs{gp}", sch.now(f"bcs{gp}"))],
            }

        p2_ld_freeV = []
        ygs_tot = []
        psmm_freeA = []
        da_freeV = []
        tmm_T = []
        psy_freeV = []
        brow_T = []

        bevac = {}
        cevac = {}
        vend_at_lp = [0]

        def p2_rep_build(lp, n):
            g0, g1 = lp * Lp, (lp + 1) * Lp
            par = n % 2
            chB = f"brl{par}"
            rsA = browA[0:1, par * Lp:(par + 1) * Lp]
            rsC = browC[0:1, par * Lp:(par + 1) * Lp]
            w = list(gen_bar[lp]["bc"]) if n == 0 else []
            if len(brow_T) >= 2:
                w.append(("T", brow_T[-2]))
            sch.dma(chB, lambda e, d=rsA, n=n, g0=g0, g1=g1:
                    e.dma_start(out=d, in_=BCB[0, n:n + 1, g0:g1]), waits=w)
            sch.dma(chB, lambda e, d=rsC, n=n, g0=g0, g1=g1:
                    e.dma_start(out=d, in_=BCB[1, n:n + 1, g0:g1]))
            rl_S = sch.now(chB)
            for bi, (row, rep, evl) in enumerate(((rsA, brep, bevac), (rsC, crep, cevac))):
                w2 = [(chB, rl_S)]
                if rep_evacA:
                    w2.append(("A", rep_evacA[-1]))
                if psum_freeA:
                    w2.append(psum_freeA[-1])
                for hb in range(HB):
                    sch.emit("T", lambda e, d=row, hb=hb:
                             e.matmul(ps_bc[:, hb * 512:(hb + 1) * 512], ones_bf[:],
                                      d[0:1, hb * 512:(hb + 1) * 512], start=True, stop=True),
                             waits=(w2 if hb == 0 else []), inc=(1 if hb == HB - 1 else 0))
                rep_T = sch.now("T")
                wE = [("T", rep_T)]
                if lp > 0:
                    wE.append(("V", vend_at_lp[0]))
                sch.emit("A", lambda e, rep=rep, n=n:
                         e.activation(out=rep[:, n * Lp:(n + 1) * Lp], in_=ps_bc[:, 0:Lp], func=AF.Copy),
                         waits=wE, inc=1)
                rep_evacA.append(sch.now("A"))
                evl[n] = sch.now("A")
            brow_T.append(sch.now("T"))

        sp_As = {}
        dx_Vs = {}
        ld_Ss = {}
        wdt_last = {}
        drl_tot = {}

        def p2_prolog(lp, dc):
            g0, g1 = lp * Lp, (lp + 1) * Lp
            dsl = dc % 2
            chL = f"xcl{dsl}"
            dls = delta_sl[dsl]
            dxs = dx_sl[dsl]
            xcv = xc2_sl[dsl]
            szv = sz2_sl[dsl]
            gidx = lp * NDC + dc
            w = list(gen_bar[lp]["xc"]) if dc == 0 else []
            if gidx >= 2:
                w.append(("V", p2_ld_freeV[gidx - 2]))
            sch.dma(chL, lambda e, dc=dc, g0=g0, g1=g1, d=xcv:
                    e.dma_start(out=d, in_=XC[dc, :, g0:g1]), waits=w)
            sch.dma(chL, lambda e, dc=dc, g0=g0, g1=g1, d=szv:
                    e.dma_start(out=d, in_=SZ[dc, :, g0:g1]))
            ld_Ss[dc] = sch.now(chL)
            for hb in range(HB):
                psel = ps_mm if hb % 2 == 0 else ps_mm2
                w2 = []
                if dc == 0 and hb == 0:
                    w2.append(("drl", drl_tot[lp]))
                if len(psmm_freeA) >= 2:
                    w2.append(("A", psmm_freeA[-2]))
                sch.emit("T", lambda e, dc=dc, hb=hb, psel=psel, lp=lp:
                         e.matmul(psel[:, 0:512], wdt_sb[:, dc * 128:(dc + 1) * 128],
                                  dr_sb[:, (lp % 2) * Lp + hb * 512:(lp % 2) * Lp + (hb + 1) * 512],
                                  start=True, stop=True),
                         waits=w2, inc=1)
                dt_T = sch.now("T")
                sch.emit("A", lambda e, d=dls, hb=hb, dc=dc, psel=psel:
                         e.activation(out=d[:, hb * 512:(hb + 1) * 512], in_=psel[:, 0:512],
                                      func=AF.Exp, bias=bdt_sb[:, dc:dc + 1]),
                         waits=[("T", dt_T)])
                sch.emit("A", lambda e, d=dls, hb=hb:
                         e.activation(out=d[:, hb * 512:(hb + 1) * 512], in_=d[:, hb * 512:(hb + 1) * 512],
                                      func=AF.Ln, bias=1.0), inc=1)
                psmm_freeA.append(sch.now("A"))
            wdt_last[lp] = sch.now("T")
            sp_As[dc] = sch.now("A")
            sch.emit("V", lambda e, d=dxs, a=dls, b=xcv: e.tensor_mul(d, a, b),
                     waits=[("A", sp_As[dc]), (chL, ld_Ss[dc])], inc=1)
            dx_Vs[dc] = sch.now("V")

        def p2_lp(lp, gen=None, bud=None):
            g0, g1 = lp * Lp, (lp + 1) * Lp
            vend_at_lp[0] = sch.now("V")
            drv = dr_sb[:, (lp % 2) * Lp:(lp % 2 + 1) * Lp]
            wdr = list(gen_bar[lp]["bc"])
            if lp >= 2:
                wdr.append(("T", wdt_last[lp - 2]))
            sch.dma("drl", lambda e, d=drv, g0=g0, g1=g1:
                    e.dma_start(out=d, in_=DR[:, g0:g1]), waits=wdr)
            drl_tot[lp] = sch.now("drl")
            p2_rep_build(lp, 0)
            p2_rep_build(lp, 1)
            p2_prolog(lp, 0)

            for dci in range(NDC):
                dc = dci
                dsl = dci % 2
                dls = delta_sl[dsl]
                dxs = dx_sl[dsl]
                xcv = xc2_sl[dsl]
                szv = sz2_sl[dsl]
                ygv = yg2_sl[dsl]
                sp_A = sp_As[dc]
                dx_V = dx_Vs[dc]
                for n in range(NS):
                    it = len(da_freeV)
                    asl = it % 2
                    dav = da_sl[asl]
                    bxv = bx_sl[asl]
                    hv = h_sl[asl]
                    tv = t_sl[asl]
                    w3 = [("A", sp_A)] if n == 0 else []
                    if it == 0 and lp == 0 and li == 0:
                        w3.append(("V", negA_V))
                    if it >= 2:
                        w3.append(("V", da_freeV[it - 2]))
                    sch.emit("A", lambda e, d=dav, s_=dls, dc=dc, n=n:
                             e.activation(out=d, in_=s_, func=AF.Exp,
                                          scale=A_sb[:, dc * NS + n:dc * NS + n + 1]),
                             waits=w3, inc=1)
                    exp_A = sch.now("A")
                    w4 = [("V", dx_V)] if n == 0 else []
                    if dci == 0:
                        w4.append(("A", bevac[n]))
                    sch.emit("V", lambda e, d=bxv, a=dxs, n=n:
                             e.tensor_mul(d, a, brep[:, n * Lp:(n + 1) * Lp]), waits=w4)
                    sidx = dc * NS + n
                    sch.emit("V", lambda e, d=hv, a=dav, b=bxv, sidx=sidx:
                             e.tensor_tensor_scan(d, a, b, states[:, sidx:sidx + 1], ALU.mult, ALU.add),
                             waits=[("A", exp_A)])
                    sch.emit("V", lambda e, d=hv, sidx=sidx:
                             e.tensor_copy(states[:, sidx:sidx + 1], d[:, Lp - 1:Lp]))
                    w4b = [("A", cevac[n])] if dci == 0 else []
                    if it >= 2:
                        w4b.append(("T", tmm_T[it - 2]))
                    sch.emit("V", lambda e, d=tv, a=hv, n=n:
                             e.tensor_mul(d, a, crep[:, n * Lp:(n + 1) * Lp]), waits=w4b, inc=1)
                    da_freeV.append(sch.now("V"))
                    t_V = sch.now("V")
                    for hb in range(HB):
                        w5 = [("V", t_V)] if hb == 0 else []
                        if n == 0 and hb == 0 and psy_freeV:
                            w5.append(("V", psy_freeV[-1]))
                        sch.emit("T", lambda e, d=tv, hb=hb, n=n:
                                 e.matmul(ps_y[:, hb * 512:(hb + 1) * 512], ident[:],
                                          d[:, hb * 512:(hb + 1) * 512],
                                          start=(n == 0), stop=(n == NS - 1)),
                                 waits=w5, inc=(1 if hb == HB - 1 else 0))
                    tmm_T.append(sch.now("T"))
                    if dci == 0 and n < NS - 2:
                        p2_rep_build(lp, n + 2)
                    if n == 4 and dci + 1 < NDC:
                        p2_prolog(lp, dci + 1)
                    if gen is not None and bud[0] > 0 and dci >= 1 and n % 2 == 1:
                        next(gen)
                        bud[0] -= 1
                y_T = sch.now("T")
                sch.emit("V", lambda e, a=xcv, dc=dc:
                         e.scalar_tensor_tensor(t2_sl, a, dv_sb[:, dc:dc + 1], ps_y[:, 0:Lp],
                                                ALU.mult, ALU.add),
                         waits=[("T", y_T)], inc=1)
                psy_freeV.append(sch.now("V"))
                chY = f"ygs{dsl}"
                wyg = []
                if len(ygs_tot) >= 2:
                    wyg.append((chY, ygs_tot[-2]))
                sch.emit("V", lambda e, d=ygv, b=szv: e.tensor_mul(d, t2_sl, b), waits=wyg, inc=1)
                yg_V = sch.now("V")
                p2_ld_freeV.append(yg_V)
                sch.dma(chY, lambda e, dc=dc, g0=g0, g1=g1, d=ygv:
                        e.dma_start(out=YG[dc, :, g0:g1], in_=d),
                        waits=[("V", yg_V)], eng="G")
                ygs_tot.append(sch.now(chY))

        gen = p1_gen()
        for _ in range(2 * NPIECE):
            next(gen)
        record_bar(0, 1)
        for lp in range(nlp):
            if lp < nlp - 1:
                bud = [2 * NPIECE]
                p2_lp(lp, gen, bud)
                assert bud[0] == 0, (li, lp, bud[0])
                record_bar(lp + 1, lp % 2)
            else:
                p2_lp(lp)
        exhausted = False
        try:
            next(gen)
        except StopIteration:
            exhausted = True
        assert exhausted, "p1 generator not exhausted"
        psbc_prev[0] = rep_evacA[-1]
        p1_A = sch.now("A")
        p2_bar = [("ygs0", sch.now("ygs0")), ("ygs1", sch.now("ygs1"))]

        p3_freeV = []
        wowin_freeT = []
        p3psum_freeV = []
        xas_tots = []

        def p3_span(s):
            l0, l1 = s * Ts, (s + 1) * Ts
            xv = xsp3[s % 2]
            chX = f"xl{s % 2}"
            chYL = f"xcl{s % 2}"
            w = list(p2_bar) if s == 0 else [("V", p3_freeV[-1])]
            for dc in range(NDC):
                sch.dma(chYL, lambda e, dc=dc, l0=l0, l1=l1:
                        e.dma_start(out=ygsp[:, dc * Ts:(dc + 1) * Ts], in_=YG[dc, :, l0:l1]),
                        waits=(w if dc == 0 else []))
            ygl_S = sch.now(chYL)
            for kc in range(KC):
                wxl = []
                if s >= 2 and kc == 0:
                    wxl = [("V", p3_freeV[-2]), xas_tots[s - 2]]
                sch.dma(chX, lambda e, kc=kc, l0=l0, l1=l1, xv=xv, li=li:
                        e.dma_start(out=xv[:, kc * Ts:(kc + 1) * Ts], in_=x_src(li, kc, l0, l1)),
                        waits=wxl)
            xl_S = sch.now(chX)
            for kc in range(KC):
                gi = s * KC + kc
                par = kc % 2
                chW = f"wol{par}"
                wv = wowin[:, par * NDC * 128:(par + 1) * NDC * 128]
                w2 = list(wstage) if gi < 2 else [("T", wowin_freeT[gi - 2])]
                sch.dma(chW, lambda e, d=wv, kc=kc: e.dma_start(out=d, in_=WOUTB[kc]), waits=w2)
                wl_S = sch.now(chW)
                psel = ps_mm if par == 0 else ps_mm2
                for dc in range(NDC):
                    w3 = []
                    if dc == 0:
                        w3 = [(chYL, ygl_S), (chW, wl_S)]
                        if gi >= 2:
                            w3.append(("V", p3psum_freeV[gi - 2]))
                        else:
                            w3.append(("A", p1_A))
                    sch.emit("T", lambda e, dc=dc, wv=wv, psel=psel:
                             e.matmul(psel[:, 0:Ts], wv[:, dc * 128:(dc + 1) * 128],
                                      ygsp[:, dc * Ts:(dc + 1) * Ts],
                                      start=(dc == 0), stop=(dc == NDC - 1)),
                             waits=w3, inc=(1 if dc == NDC - 1 else 0))
                mm_T = sch.now("T")
                wowin_freeT.append(mm_T)
                sch.emit("V", lambda e, kc=kc, xv=xv, psel=psel:
                         e.tensor_add(xv[:, kc * Ts:(kc + 1) * Ts], xv[:, kc * Ts:(kc + 1) * Ts],
                                      psel[:, 0:Ts]),
                         waits=([("T", mm_T), (chX, xl_S)] if kc == 0 else [("T", mm_T)]), inc=1)
                p3psum_freeV.append(sch.now("V"))
                xa_V = sch.now("V")
                sch.dma(f"xas{s % 2}", lambda e, kc=kc, l0=l0, l1=l1, xv=xv:
                        e.dma_start(out=xA[kc, :, l0:l1], in_=xv[:, kc * Ts:(kc + 1) * Ts]),
                        waits=[("V", xa_V)], eng="G")
            p3_freeV.append(sch.now("V"))
            xas_tots.append((f"xas{s % 2}", sch.now(f"xas{s % 2}")))

        for s in range(nsp):
            p3_span(s)
        xas_tot = [sch.now("xas0"), sch.now("xas1")]

    p3end = [("xas0", xas_tot[0]), ("xas1", xas_tot[1])]
    wq_Alist = []
    for oc in range(KC):
        w = list(p3end) if oc == 0 else [("A", wq_Alist[-1])]
        sch.dma("misc", lambda e, oc=oc: e.dma_start(out=stg_f[:, 0:512], in_=fw[oc, :, 0:512]), waits=w)
        sch.dma("misc", lambda e, oc=oc: e.dma_start(out=stg_f[:, 512:1024], in_=fw[oc, :, 512:1024]))
        fw_S = sch.now("misc")
        sch.emit("V", lambda e: e.reduce_max(am_sb[:, 0:1], stg_f[:, 0:DM],
                                             axis=mybir.AxisListType.X, apply_absolute_value=True),
                 waits=[("misc", fw_S)])
        sch.emit("V", lambda e: e.tensor_scalar_max(am_sb[:, 0:1], am_sb[:, 0:1], FFN_EPS))
        sch.emit("V", lambda e: e.reciprocal(am_sb[:, 1:2], am_sb[:, 0:1]))
        sch.emit("V", lambda e, oc=oc: e.tensor_scalar_mul(wsc_sb[:, oc:oc + 1], am_sb[:, 0:1], 1.0 / QMAX))
        sch.emit("V", lambda e: e.tensor_scalar_mul(am_sb[:, 1:2], am_sb[:, 1:2], QMAX))
        sch.emit("V", lambda e: e.tensor_scalar(stg_f[:, 0:DM], stg_f[:, 0:DM],
                                                am_sb[:, 1:2], RND_M, ALU.mult, ALU.add))
        sch.emit("V", lambda e: e.tensor_scalar(stg_f[:, 0:DM], stg_f[:, 0:DM],
                                                RND_M, None, ALU.subtract), inc=1)
        wq_V = sch.now("V")
        for ic in range(KC):
            sch.emit("T", lambda e, ic=ic: e.transpose(ps_aux[:, 128:256], stg_f[:, ic * 128:(ic + 1) * 128],
                                                       identf[:]),
                     waits=([("V", wq_V)] if ic == 0 else [("A", wq_Alist[-1])]), inc=1)
            tr_T = sch.now("T")
            sch.emit("A", lambda e, oc=oc, ic=ic:
                     e.activation(out=wqt_sb[:, (ic * KC + oc) * 128:(ic * KC + oc + 1) * 128],
                                  in_=ps_aux[:, 128:256], func=AF.Copy),
                     waits=[("T", tr_T)] + ([("ygs0", sch.now("ygs0")), ("ygs1", sch.now("ygs1"))]
                                            if (oc == 0 and ic == 0) else []),
                     inc=1)
            wq_Alist.append(sch.now("A"))
    sch.dma("idl", lambda e: e.dma_start(out=fb_sb[:, 0:KC], in_=fb[:, :]))
    wq_A = sch.now("A")
    fb_S = sch.now("idl")

    ffn_freeV = []
    ffn_store = []

    def ln_span(src_sp, dst_sp, first_waits):
        for kc in range(KC):
            sch.emit("A", lambda e, kc=kc, s_=src_sp:
                     e.activation(out=ub_sp[:, kc * Ts:(kc + 1) * Ts], in_=s_[:, kc * Ts:(kc + 1) * Ts],
                                  func=AF.Copy),
                     waits=(list(first_waits) if kc == 0 else []), inc=(1 if kc == KC - 1 else 0))
        cast_A = sch.now("A")
        for kc in range(KC):
            sch.emit("T", lambda e, kc=kc:
                     e.matmul(ps_st[0:1, 0:Ts], ones_bc[:], ub_sp[:, kc * Ts:(kc + 1) * Ts],
                              start=(kc == 0), stop=(kc == KC - 1)),
                     waits=([("A", cast_A)] if kc == 0 else []), inc=(1 if kc == KC - 1 else 0))
        sum_T = sch.now("T")
        sch.emit("V", lambda e: e.tensor_scalar_mul(rowbuf[0:1, 0:Ts], ps_st[0:1, 0:Ts], 1.0 / DM),
                 waits=[("T", sum_T)], inc=1)
        m_V = sch.now("V")
        for kc in range(KC):
            sch.emit("A", lambda e, kc=kc, s_=src_sp:
                     e.activation(out=ub_sp[:, kc * Ts:(kc + 1) * Ts], in_=s_[:, kc * Ts:(kc + 1) * Ts],
                                  func=AF.Square),
                     waits=([("T", sum_T)] if kc == 0 else []), inc=(1 if kc == KC - 1 else 0))
        sq_A = sch.now("A")
        for kc in range(KC):
            sch.emit("T", lambda e, kc=kc:
                     e.matmul(ps_st[0:1, 0:Ts], ones_bc[:], ub_sp[:, kc * Ts:(kc + 1) * Ts],
                              start=(kc == 0), stop=(kc == KC - 1)),
                     waits=([("A", sq_A), ("V", m_V)] if kc == 0 else []), inc=(1 if kc == KC - 1 else 0))
        sq_T = sch.now("T")
        sch.emit("V", lambda e: e.tensor_scalar_mul(rowbuf[0:1, Ts:2 * Ts], ps_st[0:1, 0:Ts], 1.0 / DM),
                 waits=[("T", sq_T)])
        sch.emit("V", lambda e: e.tensor_mul(rowbuf[0:1, 2 * Ts:3 * Ts], rowbuf[0:1, 0:Ts], rowbuf[0:1, 0:Ts]))
        sch.emit("V", lambda e: e.tensor_sub(rowbuf[0:1, Ts:2 * Ts], rowbuf[0:1, Ts:2 * Ts],
                                             rowbuf[0:1, 2 * Ts:3 * Ts]))
        sch.emit("V", lambda e: e.tensor_scalar_add(rowbuf[0:1, Ts:2 * Ts], rowbuf[0:1, Ts:2 * Ts], LN_EPS),
                 inc=1)
        var_V = sch.now("V")
        sch.emit("A", lambda e: e.activation(out=rowbuf[0:1, Ts:2 * Ts], in_=rowbuf[0:1, Ts:2 * Ts], func=AF.Ln),
                 waits=[("V", var_V)])
        sch.emit("A", lambda e: e.activation(out=rowbuf[0:1, Ts:2 * Ts], in_=rowbuf[0:1, Ts:2 * Ts],
                                             func=AF.Exp, scale=-0.5), inc=1)
        rstd_A = sch.now("A")
        sch.emit("V", lambda e: e.tensor_mul(rowbuf[0:1, 2 * Ts:3 * Ts], rowbuf[0:1, 0:Ts],
                                             rowbuf[0:1, Ts:2 * Ts]),
                 waits=[("A", rstd_A)], inc=1)
        mr_V = sch.now("V")
        sch.emit("T", lambda e: e.matmul(ps_aux[:, 0:Ts], ones_f[:], rowbuf[0:1, Ts:2 * Ts],
                                         start=True, stop=True),
                 waits=[("V", mr_V)])
        sch.emit("T", lambda e: e.matmul(ps_bc[:, 0:Ts], ones_f[:], rowbuf[0:1, 2 * Ts:3 * Ts],
                                         start=True, stop=True), inc=1)
        bc_T = sch.now("T")
        for kc in range(KC):
            sch.emit("V", lambda e, kc=kc, s_=src_sp, d_=dst_sp:
                     e.tensor_mul(d_[:, kc * Ts:(kc + 1) * Ts], s_[:, kc * Ts:(kc + 1) * Ts], ps_aux[:, 0:Ts]),
                     waits=([("T", bc_T)] if kc == 0 else []))
            sch.emit("V", lambda e, kc=kc, d_=dst_sp:
                     e.tensor_sub(d_[:, kc * Ts:(kc + 1) * Ts], d_[:, kc * Ts:(kc + 1) * Ts], ps_bc[:, 0:Ts]),
                     inc=(1 if kc == KC - 1 else 0))
        return sch.now("V")

    def ffn_span(s):
        l0, l1 = s * Ts, (s + 1) * Ts
        if s == 0:
            w = [("A", wq_A), ("idl", fb_S)]
        else:
            w = [("V", ffn_freeV[-1]), ("fos", ffn_store[-1])]
        for kc in range(KC):
            sch.dma("ffl", lambda e, kc=kc, l0=l0, l1=l1:
                    e.dma_start(out=u_sp[:, kc * Ts:(kc + 1) * Ts], in_=x_in[kc * 128:(kc + 1) * 128, l0:l1]),
                    waits=(w if kc == 0 else []))
            sch.dma("ffl", lambda e, kc=kc, l0=l0, l1=l1:
                    e.dma_start(out=o1_sp[:, kc * Ts:(kc + 1) * Ts], in_=xA[kc, :, l0:l1]))
        ld_S = sch.now("ffl")
        for kc in range(KC):
            sch.emit("V", lambda e, kc=kc:
                     e.tensor_add(u_sp[:, kc * Ts:(kc + 1) * Ts], u_sp[:, kc * Ts:(kc + 1) * Ts],
                                  o1_sp[:, kc * Ts:(kc + 1) * Ts]),
                     waits=([("ffl", ld_S)] if kc == 0 else []), inc=(1 if kc == KC - 1 else 0))
        u_V = sch.now("V")
        ln1_V = ln_span(u_sp, o1_sp, [("V", u_V)])
        amred_V = None
        for tb in range(NT):
            for kc in range(KC):
                w_tr = []
                if kc == 0:
                    w_tr = [("V", ln1_V)] if tb == 0 else [("V", amred_V)]
                sch.emit("T", lambda e, kc=kc, tb=tb:
                         e.transpose(ps_bc[:, kc * 128:(kc + 1) * 128],
                                     o1_sp[:, kc * Ts + tb * 128:kc * Ts + (tb + 1) * 128], identf[:]),
                         waits=w_tr, inc=(1 if kc == KC - 1 else 0))
            tr_T = sch.now("T")
            sch.emit("V", lambda e, tb=tb:
                     e.reduce_max(am_sb[:, tb:tb + 1], ps_bc[:, 0:DM],
                                  axis=mybir.AxisListType.X, apply_absolute_value=True),
                     waits=[("T", tr_T)], inc=1)
            amred_V = sch.now("V")
        sch.emit("V", lambda e: e.tensor_scalar_max(am_sb[:, 0:NT], am_sb[:, 0:NT], FFN_EPS), inc=1)
        amc_V = sch.now("V")
        for c in range(NT):
            sch.emit("T", lambda e, c=c: e.transpose(ps_aux[0:1, c * 128:(c + 1) * 128],
                                                     am_sb[:, c:c + 1], identf[:]),
                     waits=([("V", amc_V)] if c == 0 else []), inc=(1 if c == NT - 1 else 0))
        amt_T = sch.now("T")
        sch.emit("A", lambda e: e.activation(out=rowbuf[0:1, 0:Ts], in_=ps_aux[0:1, 0:Ts],
                                             func=AF.Copy, scale=1.0 / QMAX),
                 waits=[("T", amt_T)], inc=1)
        srow_A = sch.now("A")
        sch.emit("V", lambda e: e.reciprocal(rowbuf[0:1, Ts:2 * Ts], rowbuf[0:1, 0:Ts]),
                 waits=[("A", srow_A)], inc=1)
        rrow_V = sch.now("V")
        sch.emit("T", lambda e: e.matmul(ps_aux[:, 0:Ts], ones_f[:], rowbuf[0:1, Ts:2 * Ts],
                                         start=True, stop=True),
                 waits=[("V", rrow_V)])
        sch.emit("T", lambda e: e.matmul(ps_bc[:, 0:Ts], ones_f[:], rowbuf[0:1, 0:Ts],
                                         start=True, stop=True), inc=1)
        scl_T = sch.now("T")
        sch.emit("A", lambda e: e.activation(out=srep_sp, in_=ps_bc[:, 0:Ts], func=AF.Copy),
                 waits=[("T", scl_T)], inc=1)
        srep_A = sch.now("A")
        for kc in range(KC):
            sch.emit("V", lambda e, kc=kc:
                     e.tensor_mul(u_sp[:, kc * Ts:(kc + 1) * Ts], o1_sp[:, kc * Ts:(kc + 1) * Ts],
                                  ps_aux[:, 0:Ts]),
                     waits=([("T", scl_T)] if kc == 0 else []))
            sch.emit("V", lambda e, kc=kc:
                     e.tensor_scalar(u_sp[:, kc * Ts:(kc + 1) * Ts], u_sp[:, kc * Ts:(kc + 1) * Ts],
                                     RND_M, RND_M, ALU.add, ALU.subtract))
            sch.emit("V", lambda e, kc=kc:
                     e.tensor_copy(xq_sp[:, kc * Ts:(kc + 1) * Ts], u_sp[:, kc * Ts:(kc + 1) * Ts]),
                     inc=(1 if kc == KC - 1 else 0))
        xq_V = sch.now("V")
        ffnq_freeV = []
        for oc in range(KC):
            psel = ps_mm if oc % 2 == 0 else ps_mm2
            for ic in range(KC):
                w_q = []
                if ic == 0:
                    w_q = [("V", xq_V)]
                    if oc >= 2:
                        w_q.append(("V", ffnq_freeV[oc - 2]))
                sch.emit("T", lambda e, oc=oc, ic=ic, psel=psel:
                         e.matmul(psel[:, 0:Ts], wqt_sb[:, (ic * KC + oc) * 128:(ic * KC + oc + 1) * 128],
                                  xq_sp[:, ic * Ts:(ic + 1) * Ts],
                                  start=(ic == 0), stop=(ic == KC - 1)),
                         waits=w_q, inc=(1 if ic == KC - 1 else 0))
            q_T = sch.now("T")
            sch.emit("V", lambda e, oc=oc, psel=psel:
                     e.tensor_mul(u_sp[:, oc * Ts:(oc + 1) * Ts], psel[:, 0:Ts], srep_sp),
                     waits=[("T", q_T)] + ([("A", srep_A)] if oc == 0 else []))
            sch.emit("V", lambda e, oc=oc:
                     e.tensor_scalar(u_sp[:, oc * Ts:(oc + 1) * Ts], u_sp[:, oc * Ts:(oc + 1) * Ts],
                                     wsc_sb[:, oc:oc + 1], fb_sb[:, oc:oc + 1], ALU.mult, ALU.add))
            sch.emit("V", lambda e, oc=oc:
                     e.tensor_add(u_sp[:, oc * Ts:(oc + 1) * Ts], u_sp[:, oc * Ts:(oc + 1) * Ts],
                                  o1_sp[:, oc * Ts:(oc + 1) * Ts]), inc=1)
            ffnq_freeV.append(sch.now("V"))
        res_V = sch.now("V")
        ln2_V = ln_span(u_sp, o1_sp, [("V", res_V)])
        for kc in range(KC):
            sch.dma("fos", lambda e, kc=kc, l0=l0, l1=l1:
                    e.dma_start(out=y_out[kc * 128:(kc + 1) * 128, l0:l1],
                                in_=o1_sp[:, kc * Ts:(kc + 1) * Ts]),
                    waits=([("V", ln2_V)] if kc == 0 else []), eng="G")
        ffn_freeV.append(ln2_V)
        ffn_store.append(sch.now("fos"))

    for s in range(nsp):
        ffn_span(s)
    out_S = sch.now("fos")
    sch.emit("V", lambda e: e.nop(), waits=[("fos", out_S)])

    block = ctx.enter_context(nc.Block())
    by_eng = {"T": [], "A": [], "V": [], "S": [], "G": []}
    for eng, waits, fn, inc_key, inc_amt in sch.items:
        by_eng[eng].append((waits, fn, inc_key, inc_amt))

    def runner(key):
        def _f(eng):
            for waits, fn, inc_key, inc_amt in by_eng[key]:
                for (wk, wv) in waits:
                    if wv > 0:
                        eng.wait_ge(SEMS[wk], wv)
                ins = fn(eng)
                if inc_key is not None:
                    assert ins is not None
                    ins.then_inc(SEMS[inc_key], inc_amt)
        return _f

    block.tensor(runner("T"))
    block.scalar(runner("A"))
    block.vector(runner("V"))
    block.sync(runner("S"))
    block.gpsimd(runner("G"))
    ctx.close()
    return nc


def pack_inputs(inp, b, L=8192, layers=4):
    import ml_dtypes
    x = np.ascontiguousarray(np.asarray(inp["L0_attn_x"])[b].T)
    W_in = np.asarray(inp["L0_attn_W_in"])[:layers]
    winT = np.ascontiguousarray(
        W_in.transpose(0, 2, 1).reshape(layers, KC, 128, NEC, 128).transpose(0, 3, 1, 2, 4))
    W_out = np.asarray(inp["L0_attn_W_out"])[:layers]
    woutT = np.ascontiguousarray(
        W_out.transpose(0, 2, 1).reshape(layers, NDC, 128, KC, 128).transpose(0, 3, 1, 2, 4))
    W_x = np.asarray(inp["L0_attn_W_x"])[:layers]
    wxT = np.ascontiguousarray(
        W_x.transpose(0, 2, 1).reshape(layers, NDC, 128, 96)
        .transpose(0, 2, 1, 3).reshape(layers, 128, NDC * 96))
    W_dt = np.asarray(inp["L0_attn_W_dt"])[:layers]
    wdtT = np.ascontiguousarray(W_dt.transpose(0, 2, 1))
    alog = np.ascontiguousarray(np.asarray(inp["L0_attn_A_log"])[:layers]
                                .reshape(layers, NDC, 128, NS).transpose(0, 2, 1, 3)
                                .reshape(layers, 128, NDC * NS))
    convw = np.ascontiguousarray(np.asarray(inp["L0_attn_conv_w"])[:layers]
                                 .reshape(layers, NDC, 128, DCV).transpose(0, 2, 1, 3)
                                 .reshape(layers, 128, NDC * DCV))
    convb = np.ascontiguousarray(np.asarray(inp["L0_attn_conv_b"])[:layers]
                                 .reshape(layers, NDC, 128).transpose(0, 2, 1))
    bdt = np.ascontiguousarray(np.asarray(inp["L0_attn_b_dt"])[:layers]
                               .reshape(layers, NDC, 128).transpose(0, 2, 1))
    dvec = np.ascontiguousarray(np.asarray(inp["L0_attn_D"])[:layers]
                                .reshape(layers, NDC, 128).transpose(0, 2, 1))
    normw = np.ascontiguousarray(np.asarray(inp["L0_attn_norm_w"])[:layers]
                                 .reshape(layers, KC, 128).transpose(0, 2, 1))
    fwt = np.ascontiguousarray(np.asarray(inp["L0_ffn_weight"]).reshape(KC, 128, DM))
    fbt = np.ascontiguousarray(np.asarray(inp["L0_ffn_bias"]).reshape(KC, 128).T)
    return {
        "x_in": x.astype(np.float32),
        "winT": winT.astype(np.float32), "woutT": woutT.astype(np.float32),
        "wxT": wxT.astype(np.float32), "wdtT": wdtT.astype(np.float32),
        "alog": alog.astype(np.float32), "convw": convw.astype(np.float32),
        "convb": convb.astype(np.float32), "bdt": bdt.astype(np.float32),
        "dvec": dvec.astype(np.float32), "normw": normw.astype(np.float32),
        "fw": fwt.astype(np.float32), "fb": fbt.astype(np.float32),
        "ident_in": np.eye(128, dtype=ml_dtypes.bfloat16),
        "identf_in": np.eye(128, dtype=np.float32),
    }


def kernel(**inputs):
    from concourse.bass_utils import run_bass_kernel_spmd
    L = 8192
    layers = 4
    nc = build_program(L=L, layers=layers, debug_internals=False)
    im = [pack_inputs(inputs, 0, L=L, layers=layers),
          pack_inputs(inputs, 1, L=L, layers=layers)]
    in_maps = [im[c] if c < 2 else im[0] for c in range(8)]
    res = run_bass_kernel_spmd(nc, in_maps, core_ids=list(range(8)))
    out0 = np.ascontiguousarray(res.results[0]["y_out"].T)
    out1 = np.ascontiguousarray(res.results[1]["y_out"].T)
    return np.stack([out0, out1]).astype(np.float32)


# version 09
# speedup vs baseline: 1.1396x; 1.1319x over previous
import numpy as np
from contextlib import ExitStack

import concourse.bass as bass
import concourse.mybir as mybir

F32 = mybir.dt.float32
BF16 = mybir.dt.bfloat16
AF = mybir.ActivationFunctionType
ALU = mybir.AluOpType

DM, DI, RK, NS, DCV = 1024, 2048, 64, 16, 4
KC, NDC, NEC = DM // 128, DI // 128, 2 * DI // 128
RMS_EPS = 1e-5
LN_EPS = 1e-5
FFN_EPS = 1e-8
QMAX = 127.0
RND_M = float(3 * 2 ** 22)

DMA_CH = ["idl", "misc", "stgl0", "stgl1", "stgs0", "stgs1", "xl0", "xl1",
          "wwl0", "wwl1", "wol0", "wol1", "xcl0", "xcl1", "brl0", "brl1",
          "xcs00", "xcs01", "xcs10", "xcs11", "szs00", "szs01", "szs10", "szs11",
          "ygs0", "ygs1", "bcs0", "bcs1", "drl",
          "xas0", "xas1", "ffl", "fos"]
ENG = ["T", "A", "V", "G"]


class Sched:
    def __init__(self):
        self.items = []
        self.cnt = {k: 0 for k in ENG + DMA_CH}

    SELF_SERIAL = ("A", "V", "G")

    def emit(self, eng, fn, waits=(), inc=0):
        waits = list(waits)
        if eng in self.SELF_SERIAL:
            waits.append((eng, self.cnt[eng]))
            inc = 1
        self.items.append((eng, tuple(waits), fn, eng if inc else None, inc))
        if inc:
            self.cnt[eng] += inc
        return self.cnt[eng]

    def dma(self, ch, fn, waits=(), eng="S"):
        assert ch in DMA_CH, ch
        self.items.append((eng, tuple(waits), fn, ch, 16))
        self.cnt[ch] += 16
        return self.cnt[ch]

    def now(self, key):
        return self.cnt[key]


def build_program(L=8192, layers=4, Ts=512, Lp=1024, debug_internals=False):
    nsp = L // Ts
    nlp = L // Lp
    HB = Lp // 512
    NT = Ts // 128
    nc = bass.Bass("TRN2", target_bir_lowering=False, debug=False)

    EI, EO, IN = "ExternalInput", "ExternalOutput", "Internal"
    x_in = nc.dram_tensor("x_in", [DM, L], F32, kind=EI).ap()
    winT = nc.dram_tensor("winT", [layers, NEC, KC, 128, 128], F32, kind=EI).ap()
    woutT = nc.dram_tensor("woutT", [layers, KC, NDC, 128, 128], F32, kind=EI).ap()
    wxT = nc.dram_tensor("wxT", [layers, 128, NDC * 96], F32, kind=EI).ap()
    wdtT = nc.dram_tensor("wdtT", [layers, 64, NDC * 128], F32, kind=EI).ap()
    alog = nc.dram_tensor("alog", [layers, 128, NDC * NS], F32, kind=EI).ap()
    convw = nc.dram_tensor("convw", [layers, 128, NDC * DCV], F32, kind=EI).ap()
    convb = nc.dram_tensor("convb", [layers, 128, NDC], F32, kind=EI).ap()
    bdt = nc.dram_tensor("bdt", [layers, 128, NDC], F32, kind=EI).ap()
    dvec = nc.dram_tensor("dvec", [layers, 128, NDC], F32, kind=EI).ap()
    normw = nc.dram_tensor("normw", [layers, 128, KC], F32, kind=EI).ap()
    fw = nc.dram_tensor("fw", [KC, 128, DM], F32, kind=EI).ap()
    fb = nc.dram_tensor("fb", [128, KC], F32, kind=EI).ap()
    ident_in = nc.dram_tensor("ident_in", [128, 128], BF16, kind=EI).ap()
    identf_in = nc.dram_tensor("identf_in", [128, 128], F32, kind=EI).ap()

    y_out = nc.dram_tensor("y_out", [DM, L], F32, kind=EO).ap()

    ikind = EO if debug_internals else IN
    xA = nc.dram_tensor("xA", [KC, 128, L], F32, kind=ikind).ap()
    XC = nc.dram_tensor("XCb", [NDC, 128, L], F32, kind=ikind).ap()
    SZ = nc.dram_tensor("SZb", [NDC, 128, L], BF16, kind=ikind).ap()
    YG = nc.dram_tensor("YGb", [NDC, 128, L], BF16, kind=ikind).ap()
    BCB = nc.dram_tensor("BCB", [2, NS, L], BF16, kind=ikind).ap()
    DR = nc.dram_tensor("DRb", [64, L], BF16, kind=IN).ap()
    WINB = nc.dram_tensor("WINB", [NEC, 128, KC * 128], BF16, kind=IN).ap()
    WOUTB = nc.dram_tensor("WOUTB", [KC, 128, NDC * 128], BF16, kind=IN).ap()

    ctx = ExitStack()
    sb = lambda name, shape, dt: ctx.enter_context(nc.sbuf_tensor(name, shape, dt))
    psb = lambda name, shape, dt: ctx.enter_context(nc.psum_tensor(name, shape, dt))

    dr_sb = sb("dr_sb", [64, 2 * Lp], BF16)
    stage96 = sb("stage96", [128, 512], BF16)
    browA = sb("browA", [1, 2 * Lp], BF16)
    browC = sb("browC", [1, 2 * Lp], BF16)
    states = sb("states", [128, NDC * NS], F32)
    halo = sb("halo", [128, NDC * (DCV - 1)], F32)
    wx_sb = sb("wx_sb", [128, NDC * 96], F32)
    wdt_sb = sb("wdt_sb", [64, NDC * 128], BF16)
    A_sb = sb("A_sb", [128, NDC * NS], F32)
    cw_sb = sb("cw_sb", [128, NDC * DCV], F32)
    cb_sb = sb("cb_sb", [128, NDC], F32)
    bdt_sb = sb("bdt_sb", [128, NDC], F32)
    dv_sb = sb("dv_sb", [128, NDC], F32)
    nw_sb = sb("nw_sb", [128, KC], F32)
    wsc_sb = sb("wsc_sb", [128, KC], F32)
    fb_sb = sb("fb_sb", [128, KC], F32)
    am_sb = sb("am_sb", [128, 2 * NT], F32)
    ident = sb("ident", [128, 128], BF16)
    identf = sb("identf", [128, 128], F32)
    ones_f = sb("ones_f", [1, 128], F32)
    ones_bf = sb("ones_bf", [1, 128], BF16)
    ones_bc = sb("ones_bc", [128, 1], BF16)
    wwin = sb("wwin", [128, 2 * KC * 128], BF16)
    wowin = sb("wowin", [128, 2 * NDC * 128], BF16)
    stg_f = sb("stg_f", [128, 2048], F32)
    stg_b = sb("stg_b", [128, 2048], BF16)
    rowbuf = sb("rowbuf", [1, 3 * Ts], F32)

    AFC = 13318
    ABC = 50176
    arena_f = sb("arena_f", [128, AFC], F32)
    arena_b = sb("arena_b", [128, ABC], BF16)

    def fsl(off, cols):
        assert off + cols <= AFC, (off, cols)
        return arena_f[:, off:off + cols]

    def bsl(off, cols):
        assert off + cols <= ABC, (off, cols)
        return arena_b[:, off:off + cols]

    delta_sl = [fsl(i * Lp, Lp) for i in range(2)]
    da_sl = [fsl(2 * Lp + i * Lp, Lp) for i in range(2)]
    xc2_sl = [fsl(4 * Lp + i * Lp, Lp) for i in range(2)]
    t2_sl = fsl(6 * Lp, Lp)
    _p1 = 7 * Lp
    xsp = fsl(_p1, KC * Ts)
    xi_buf = [fsl(_p1 + KC * Ts + i * (Ts + 3), Ts + 3) for i in range(2)]
    xc_sl = [fsl(_p1 + KC * Ts + 2 * (Ts + 3) + i * Ts, Ts) for i in range(2)]
    brep = bsl(0, NS * Lp)
    crep = bsl(NS * Lp, NS * Lp)
    _o = 2 * NS * Lp
    dx_sl = [bsl(_o + i * Lp, Lp) for i in range(2)]
    bx_sl = [bsl(_o + 2 * Lp + i * Lp, Lp) for i in range(2)]
    h_sl = [bsl(_o + 4 * Lp + i * Lp, Lp) for i in range(2)]
    t_sl = [bsl(_o + 6 * Lp + i * Lp, Lp) for i in range(2)]
    sz2_sl = [bsl(_o + 8 * Lp + i * Lp, Lp) for i in range(2)]
    yg2_sl = [bsl(_o + 10 * Lp + i * Lp, Lp) for i in range(2)]
    _p1b = _o + 12 * Lp
    hrms = bsl(_p1b, KC * Ts)
    sz_sl = [bsl(_p1b + KC * Ts + i * Ts, Ts) for i in range(2)]

    xsp3 = [fsl(i * KC * Ts, KC * Ts) for i in range(2)]
    ygsp = bsl(0, NDC * Ts)

    u_sp = fsl(0, KC * Ts)
    o1_sp = fsl(KC * Ts, KC * Ts)
    srep_sp = fsl(2 * KC * Ts, Ts)
    xq_sp = bsl(0, KC * Ts)
    ub_sp = bsl(KC * Ts, KC * Ts)
    wqt_sb = bsl(2 * KC * Ts, KC * KC * 128)

    ps_mm = psb("ps_mm", [128, 512], F32)
    ps_mm2 = psb("ps_mm2", [128, 512], F32)
    ps_y = psb("ps_y", [128, Lp], F32)
    ps_bc = psb("ps_bc", [128, 1024], F32)
    ps_st = psb("ps_st", [128, 512], F32)
    ps_aux = psb("ps_aux", [128, 512], F32)

    SEMS = {}
    for k in ENG + DMA_CH:
        SEMS[k] = ctx.enter_context(nc.semaphore("sem_" + k))

    sch = Sched()

    sch.dma("idl", lambda e: e.dma_start(out=ident[:], in_=ident_in[:, :]))
    sch.dma("idl", lambda e: e.dma_start(out=identf[:], in_=identf_in[:, :]))
    pre_S = sch.now("idl")
    sch.emit("V", lambda e: e.memset(ones_f[:], 1.0))
    sch.emit("V", lambda e: e.memset(ones_bf[:], 1.0))
    sch.emit("V", lambda e: e.memset(ones_bc[:], 1.0), inc=1)
    pre_V = sch.now("V")

    def x_src(li, kc, l0, l1):
        if li == 0:
            return x_in[kc * 128:(kc + 1) * 128, l0:l1]
        return xA[kc, :, l0:l1]

    xas_tot = [0, 0]
    psbc_prev = [0]

    for li in range(layers):
        sch.emit("V", lambda e: e.memset(states[:], 0.0))
        sch.emit("V", lambda e: e.memset(halo[:], 0.0), inc=1)
        zero_V = sch.now("V")

        for (dst, src) in ((cw_sb, convw[li]), (cb_sb, convb[li]), (bdt_sb, bdt[li]),
                           (dv_sb, dvec[li]), (nw_sb, normw[li]), (A_sb, alog[li])):
            sch.dma("misc", lambda e, d=dst, s=src: e.dma_start(out=d[:], in_=s))
        cA_S = sch.now("misc")
        sch.emit("A", lambda e: e.activation(out=A_sb[:], in_=A_sb[:], func=AF.Exp),
                 waits=[("misc", cA_S)], inc=1)
        expA_A = sch.now("A")
        sch.emit("V", lambda e: e.tensor_scalar_mul(A_sb[:], A_sb[:], -1.0),
                 waits=[("A", expA_A)], inc=1)
        negA_V = sch.now("V")
        wxw = [("V", negA_V), ("A", sch.now("A"))]
        if li > 0:
            wxw.append(("T", sch.now("T")))
        sch.dma("misc", lambda e, s=wxT[li]: e.dma_start(out=wx_sb[:], in_=s), waits=wxw)
        sch.dma("misc", lambda e, s=wdtT[li]: e.dma_start(out=stg_f[0:64, 0:NDC * 128], in_=s))
        cWd_S = sch.now("misc")
        sch.emit("A", lambda e: e.activation(out=wdt_sb[:], in_=stg_f[0:64, 0:NDC * 128], func=AF.Copy),
                 waits=[("misc", cWd_S)], inc=1)
        const_A = sch.now("A")

        stage_cast = []
        stage_store = {0: [], 1: []}
        units = []
        for ec in range(NEC):
            units.append((winT[li, ec], WINB[ec], 0))
        for kc in range(KC):
            units.append((woutT[li, kc, 0:8], WOUTB[kc], 0))
            units.append((woutT[li, kc, 8:16], WOUTB[kc], 1024))
        for u, (src, dstrow, dcol) in enumerate(units):
            par = u % 2
            chL, chS = f"stgl{par}", f"stgs{par}"
            f_s = stg_f[:, par * 1024:(par + 1) * 1024]
            b_s = stg_b[:, par * 1024:(par + 1) * 1024]
            w = [("A", const_A)] if u < 2 else [("A", stage_cast[u - 2])]
            sch.dma(chL, lambda e, d=f_s, src=src:
                    e.dma_start(out=d.rearrange("p (k c) -> p k c", k=8),
                                in_=src.rearrange("k p c -> p k c")),
                    waits=w)
            sL = sch.now(chL)
            w2 = [(chL, sL)]
            if len(stage_store[par]) >= 1:
                w2.append((chS, stage_store[par][-1]))
            sch.emit("A", lambda e, d=b_s, sf=f_s: e.activation(out=d, in_=sf, func=AF.Copy),
                     waits=w2, inc=1)
            stage_cast.append(sch.now("A"))
            sch.dma(chS, lambda e, d=dstrow, dcol=dcol, sbf=b_s:
                    e.dma_start(out=d[:, dcol:dcol + 1024], in_=sbf),
                    waits=[("A", stage_cast[-1])], eng="A")
            stage_store[par].append(sch.now(chS))
        wstage = [("stgs0", sch.now("stgs0")), ("stgs1", sch.now("stgs1"))]

        NPIECE = 37
        xsp_free = []
        wwin_freeT = []
        psum_freeA = []
        xbA = {0: 0, 1: 0}
        xbV = {0: 0, 1: 0}
        xcs_tot = []
        szs_tot = []
        bcs_tot = []
        wxmm_T = []
        xl_tot = {}
        hrv_prev = [0]
        drbar = [0]
        rep_evacA = []

        def emit_xload(s):
            l0, l1 = s * Ts, (s + 1) * Ts
            if s == 0:
                w = [("V", zero_V), ("xas0", xas_tot[0]), ("xas1", xas_tot[1])]
            else:
                w = [("V", xsp_free[s - 1])]
            for kc in range(KC):
                sch.dma("xl0", lambda e, kc=kc, l0=l0, l1=l1, li=li:
                        e.dma_start(out=xsp[:, kc * Ts:(kc + 1) * Ts], in_=x_src(li, kc, l0, l1)),
                        waits=(w if kc == 0 else []))
            xl_tot[s] = sch.now("xl0")

        def p1_gen():
            for s in range(nsp):
                l0, l1 = s * Ts, (s + 1) * Ts
                gp = (s // 2 - 1) % 2
                if s == 0:
                    emit_xload(0)
                w_sq = [("xl0", xl_tot[s])]
                if wwin_freeT:
                    w_sq.append(("T", wwin_freeT[-1]))
                for kc in range(KC):
                    sch.emit("A", lambda e, kc=kc:
                             e.activation(out=hrms[:, kc * Ts:(kc + 1) * Ts],
                                          in_=xsp[:, kc * Ts:(kc + 1) * Ts], func=AF.Square),
                             waits=(w_sq if kc == 0 else []), inc=(1 if kc == KC - 1 else 0))
                sq_A = sch.now("A")
                yield
                w_st = [("A", sq_A), ("V", pre_V)]
                if hrv_prev[0]:
                    w_st.append(("V", hrv_prev[0]))
                for kc in range(KC):
                    sch.emit("T", lambda e, kc=kc:
                             e.matmul(ps_aux[0:1, 0:Ts], ones_bc[:], hrms[:, kc * Ts:(kc + 1) * Ts],
                                      start=(kc == 0), stop=(kc == KC - 1)),
                             waits=(w_st if kc == 0 else []), inc=(1 if kc == KC - 1 else 0))
                st_T = sch.now("T")
                yield
                sch.emit("V", lambda e: e.tensor_scalar(rowbuf[0:1, 0:Ts], ps_aux[0:1, 0:Ts],
                                                        1.0 / DM, RMS_EPS, ALU.mult, ALU.add),
                         waits=[("T", st_T)], inc=1)
                ms_V = sch.now("V")
                sch.emit("A", lambda e: e.activation(out=rowbuf[0:1, Ts:2 * Ts], in_=rowbuf[0:1, 0:Ts],
                                                     func=AF.Ln),
                         waits=[("V", ms_V)])
                sch.emit("A", lambda e: e.activation(out=rowbuf[0:1, Ts:2 * Ts], in_=rowbuf[0:1, Ts:2 * Ts],
                                                     func=AF.Exp, scale=-0.5), inc=1)
                rstd_A = sch.now("A")
                sch.emit("T", lambda e: e.matmul(ps_aux[:, 0:Ts], ones_f[:], rowbuf[0:1, Ts:2 * Ts],
                                                 start=True, stop=True),
                         waits=[("A", rstd_A), ("idl", pre_S)], inc=1)
                bc_T = sch.now("T")
                yield
                for kc in range(KC):
                    sch.emit("V", lambda e, kc=kc:
                             e.scalar_tensor_tensor(hrms[:, kc * Ts:(kc + 1) * Ts],
                                                    xsp[:, kc * Ts:(kc + 1) * Ts],
                                                    nw_sb[:, kc:kc + 1], ps_aux[:, 0:Ts],
                                                    ALU.mult, ALU.mult),
                             waits=([("T", bc_T)] if kc == 0 else []), inc=(1 if kc == KC - 1 else 0))
                hr_V = sch.now("V")
                hrv_prev[0] = hr_V
                xsp_free.append(hr_V)
                yield
                silu_Vs = {}
                conv_Vs = {}

                def emit_wx(dc):
                    wmx = [("V", silu_Vs[dc])]
                    if dc == 0:
                        wmx.append(("A", const_A))
                        if drbar[0]:
                            wmx.append(("A", drbar[0]))
                    sch.emit("T", lambda e, dc=dc:
                             e.matmul(ps_st[0:96, 0:Ts], wx_sb[:, dc * 96:(dc + 1) * 96],
                                      xc_sl[dc % 2][:, 0:Ts],
                                      start=(dc == 0), stop=(dc == NDC - 1)),
                             waits=wmx, inc=1)
                    wxmm_T.append(sch.now("T"))

                def emit_sig_store(dc):
                    xcv = xc_sl[dc % 2]
                    sg = xi_buf[dc % 2]
                    sch.emit("A", lambda e, xcv=xcv, sg=sg:
                             e.activation(out=sg[:, 0:Ts], in_=xcv, func=AF.Exp, scale=-1.0),
                             waits=[("V", conv_Vs[dc])])
                    sch.emit("A", lambda e, sg=sg:
                             e.activation(out=sg[:, 0:Ts], in_=sg[:, 0:Ts], func=AF.Ln, bias=1.0))
                    sch.emit("A", lambda e, sg=sg:
                             e.activation(out=sg[:, 0:Ts], in_=sg[:, 0:Ts], func=AF.Exp, scale=-1.0),
                             inc=1)
                    xbA[dc % 2] = sch.now("A")
                    sch.emit("V", lambda e, xcv=xcv, sg=sg: e.tensor_mul(xcv, xcv, sg[:, 0:Ts]),
                             waits=[("A", xbA[dc % 2])], inc=1)
                    silu_Vs[dc] = sch.now("V")
                    xbV[dc % 2] = sch.now("V")
                    chXS = f"xcs{gp}{dc % 2}"
                    sch.dma(chXS, lambda e, dc=dc, l0=l0, l1=l1:
                            e.dma_start(out=XC[dc, :, l0:l1], in_=xc_sl[dc % 2]),
                            waits=[("V", silu_Vs[dc])], eng="G")
                    xcs_tot.append((chXS, sch.now(chXS)))

                for ec in range(NEC):
                    gi = s * NEC + ec
                    par = ec % 2
                    chW = f"wwl{par}"
                    wv = wwin[:, par * KC * 128:(par + 1) * KC * 128]
                    w2 = list(wstage) if gi < 2 else [("T", wwin_freeT[gi - 2])]
                    sch.dma(chW, lambda e, d=wv, ec=ec: e.dma_start(out=d, in_=WINB[ec]), waits=w2)
                    wl_S = sch.now(chW)
                    psel = ps_bc[:, 0:512] if ec % 2 == 0 else ps_bc[:, 512:1024]
                    for kc in range(KC):
                        w3 = []
                        if kc == 0:
                            w3 = [(chW, wl_S), ("V", hr_V)]
                            if gi >= 2:
                                w3.append(psum_freeA[gi - 2])
                            if rep_evacA:
                                w3.append(("A", rep_evacA[-1]))
                            elif psbc_prev[0]:
                                w3.append(("A", psbc_prev[0]))
                        sch.emit("T", lambda e, kc=kc, wv=wv, psel=psel:
                                 e.matmul(psel[:, 0:Ts], wv[:, kc * 128:(kc + 1) * 128],
                                          hrms[:, kc * Ts:(kc + 1) * Ts],
                                          start=(kc == 0), stop=(kc == KC - 1)),
                                 waits=w3, inc=(1 if kc == KC - 1 else 0))
                    mm_T = sch.now("T")
                    wwin_freeT.append(mm_T)
                    if ec < NDC:
                        dc = ec
                        if dc >= 2:
                            emit_wx(dc - 2)
                        xb = xi_buf[dc % 2]
                        w4 = [("A", xbA[dc % 2])] if xbA[dc % 2] else []
                        sch.emit("V", lambda e, xb=xb, dc=dc:
                                 e.tensor_copy(xb[:, 0:DCV - 1],
                                               halo[:, dc * (DCV - 1):(dc + 1) * (DCV - 1)]),
                                 waits=w4, inc=1)
                        hv_V = sch.now("V")
                        sch.emit("A", lambda e, xb=xb, psel=psel:
                                 e.activation(out=xb[:, DCV - 1:DCV - 1 + Ts], in_=psel[:, 0:Ts],
                                              func=AF.Copy),
                                 waits=[("T", mm_T), ("V", hv_V)], inc=1)
                        psum_freeA.append(("A", sch.now("A")))
                        ev_A = sch.now("A")
                        xcv = xc_sl[dc % 2]
                        w5 = [("A", ev_A)]
                        if dc == 0:
                            if len(xcs_tot) >= 2:
                                w5.append(xcs_tot[-2])
                        elif xcs_tot:
                            w5.append(xcs_tot[-1])
                        if dc >= 2 or (dc == 1 and wxmm_T):
                            w5.append(("T", wxmm_T[-1]))
                        elif dc == 0 and len(wxmm_T) >= 2:
                            w5.append(("T", wxmm_T[-2]))
                        sch.emit("V", lambda e, xb=xb, xcv=xcv, dc=dc:
                                 e.tensor_scalar(xcv, xb[:, 0:Ts], cw_sb[:, dc * DCV:dc * DCV + 1],
                                                 cb_sb[:, dc:dc + 1], ALU.mult, ALU.add),
                                 waits=w5)
                        for k in range(1, DCV):
                            sch.emit("V", lambda e, xb=xb, xcv=xcv, dc=dc, k=k:
                                     e.scalar_tensor_tensor(xcv, xb[:, k:k + Ts],
                                                            cw_sb[:, dc * DCV + k:dc * DCV + k + 1],
                                                            xcv, ALU.mult, ALU.add))
                        sch.emit("V", lambda e, xb=xb, dc=dc:
                                 e.tensor_copy(halo[:, dc * (DCV - 1):(dc + 1) * (DCV - 1)],
                                               xb[:, Ts:Ts + DCV - 1]),
                                 inc=1)
                        conv_Vs[dc] = sch.now("V")
                        if dc >= 1:
                            emit_sig_store(dc - 1)
                        if dc == 2 and s + 1 < nsp:
                            emit_xload(s + 1)
                    else:
                        if ec == NDC:
                            emit_sig_store(NDC - 1)
                        dcz = ec - NDC
                        par2 = dcz % 2
                        szv = sz_sl[par2]
                        sg = xi_buf[par2]
                        chSS = f"szs{gp}{par2}"
                        wz = [("T", mm_T)]
                        if xbV[par2]:
                            wz.append(("V", xbV[par2]))
                        sch.emit("A", lambda e, sg=sg, psel=psel:
                                 e.activation(out=sg[:, 0:Ts], in_=psel[:, 0:Ts], func=AF.Exp,
                                              scale=-1.0),
                                 waits=wz)
                        sch.emit("A", lambda e, sg=sg:
                                 e.activation(out=sg[:, 0:Ts], in_=sg[:, 0:Ts], func=AF.Ln, bias=1.0))
                        sch.emit("A", lambda e, sg=sg:
                                 e.activation(out=sg[:, 0:Ts], in_=sg[:, 0:Ts], func=AF.Exp,
                                              scale=-1.0), inc=1)
                        xbA[par2] = sch.now("A")
                        wzm = [("A", xbA[par2])]
                        if len(szs_tot) >= 2:
                            wzm.append(szs_tot[-2])
                        sch.emit("V", lambda e, szv=szv, sg=sg, psel=psel:
                                 e.tensor_mul(szv, psel[:, 0:Ts], sg[:, 0:Ts]),
                                 waits=wzm, inc=1)
                        psum_freeA.append(("V", sch.now("V")))
                        xbV[par2] = sch.now("V")
                        sz_V = sch.now("V")
                        sch.dma(chSS, lambda e, dcz=dcz, l0=l0, l1=l1, szv=szv:
                                e.dma_start(out=SZ[dcz, :, l0:l1], in_=szv),
                                waits=[("V", sz_V)], eng="G")
                        szs_tot.append((chSS, sch.now(chSS)))
                    yield
                emit_wx(NDC - 2)
                emit_wx(NDC - 1)
                wx_T = sch.now("T")
                w7 = [("T", wx_T)]
                if bcs_tot:
                    w7.append(bcs_tot[-1])
                sch.emit("A", lambda e:
                         e.activation(out=stage96[0:64, 0:Ts], in_=ps_st[0:64, 0:Ts], func=AF.Copy),
                         waits=w7)
                sch.emit("A", lambda e: e.activation(out=stage96[64:96, 0:Ts], in_=ps_st[64:96, 0:Ts],
                                                     func=AF.Copy), inc=1)
                bc_A = sch.now("A")
                chBC = f"bcs{gp}"
                sch.dma(chBC, lambda e, l0=l0, l1=l1:
                        e.dma_start(out=DR[:, l0:l1], in_=stage96[0:64, 0:Ts]),
                        waits=[("A", bc_A)], eng="A")
                sch.dma(chBC, lambda e, l0=l0, l1=l1:
                        e.dma_start(out=BCB[0, :, l0:l1], in_=stage96[64:80, 0:Ts]),
                        eng="A")
                sch.dma(chBC, lambda e, l0=l0, l1=l1:
                        e.dma_start(out=BCB[1, :, l0:l1], in_=stage96[80:96, 0:Ts]),
                        eng="A")
                bcs_tot.append((chBC, sch.now(chBC)))
                drbar[0] = bc_A
                yield

        gen_bar = {}

        def record_bar(lp, gp):
            gen_bar[lp] = {
                "xc": [(f"xcs{gp}0", sch.now(f"xcs{gp}0")), (f"xcs{gp}1", sch.now(f"xcs{gp}1")),
                       (f"szs{gp}0", sch.now(f"szs{gp}0")), (f"szs{gp}1", sch.now(f"szs{gp}1"))],
                "bc": [(f"bcs{gp}", sch.now(f"bcs{gp}"))],
            }

        p2_ld_freeV = []
        ygs_tot = []
        psmm_freeA = []
        da_freeV = []
        tmm_T = []
        psy_freeV = []
        brow_T = []

        bevac = {}
        cevac = {}
        vend_at_lp = [0]

        def p2_rep_build(lp, n):
            g0, g1 = lp * Lp, (lp + 1) * Lp
            par = n % 2
            chB = f"brl{par}"
            rsA = browA[0:1, par * Lp:(par + 1) * Lp]
            rsC = browC[0:1, par * Lp:(par + 1) * Lp]
            w = list(gen_bar[lp]["bc"]) if n == 0 else []
            if len(brow_T) >= 2:
                w.append(("T", brow_T[-2]))
            sch.dma(chB, lambda e, d=rsA, n=n, g0=g0, g1=g1:
                    e.dma_start(out=d, in_=BCB[0, n:n + 1, g0:g1]), waits=w)
            sch.dma(chB, lambda e, d=rsC, n=n, g0=g0, g1=g1:
                    e.dma_start(out=d, in_=BCB[1, n:n + 1, g0:g1]))
            rl_S = sch.now(chB)
            for bi, (row, rep, evl) in enumerate(((rsA, brep, bevac), (rsC, crep, cevac))):
                w2 = [(chB, rl_S)]
                if rep_evacA:
                    w2.append(("A", rep_evacA[-1]))
                if psum_freeA:
                    w2.append(psum_freeA[-1])
                for hb in range(HB):
                    sch.emit("T", lambda e, d=row, hb=hb:
                             e.matmul(ps_bc[:, hb * 512:(hb + 1) * 512], ones_bf[:],
                                      d[0:1, hb * 512:(hb + 1) * 512], start=True, stop=True),
                             waits=(w2 if hb == 0 else []), inc=(1 if hb == HB - 1 else 0))
                rep_T = sch.now("T")
                wE = [("T", rep_T)]
                if lp > 0:
                    wE.append(("V", vend_at_lp[0]))
                sch.emit("A", lambda e, rep=rep, n=n:
                         e.activation(out=rep[:, n * Lp:(n + 1) * Lp], in_=ps_bc[:, 0:Lp], func=AF.Copy),
                         waits=wE, inc=1)
                rep_evacA.append(sch.now("A"))
                evl[n] = sch.now("A")
            brow_T.append(sch.now("T"))

        sp_As = {}
        dx_Vs = {}
        ld_Ss = {}
        wdt_last = {}
        drl_tot = {}

        def p2_prolog(lp, dc):
            g0, g1 = lp * Lp, (lp + 1) * Lp
            dsl = dc % 2
            chL = f"xcl{dsl}"
            dls = delta_sl[dsl]
            dxs = dx_sl[dsl]
            xcv = xc2_sl[dsl]
            szv = sz2_sl[dsl]
            gidx = lp * NDC + dc
            w = list(gen_bar[lp]["xc"]) if dc == 0 else []
            if gidx >= 2:
                w.append(("V", p2_ld_freeV[gidx - 2]))
            sch.dma(chL, lambda e, dc=dc, g0=g0, g1=g1, d=xcv:
                    e.dma_start(out=d, in_=XC[dc, :, g0:g1]), waits=w)
            sch.dma(chL, lambda e, dc=dc, g0=g0, g1=g1, d=szv:
                    e.dma_start(out=d, in_=SZ[dc, :, g0:g1]))
            ld_Ss[dc] = sch.now(chL)
            for hb in range(HB):
                psel = ps_mm if hb % 2 == 0 else ps_mm2
                w2 = []
                if dc == 0 and hb == 0:
                    w2.append(("drl", drl_tot[lp]))
                if len(psmm_freeA) >= 2:
                    w2.append(("A", psmm_freeA[-2]))
                sch.emit("T", lambda e, dc=dc, hb=hb, psel=psel, lp=lp:
                         e.matmul(psel[:, 0:512], wdt_sb[:, dc * 128:(dc + 1) * 128],
                                  dr_sb[:, (lp % 2) * Lp + hb * 512:(lp % 2) * Lp + (hb + 1) * 512],
                                  start=True, stop=True),
                         waits=w2, inc=1)
                dt_T = sch.now("T")
                sch.emit("A", lambda e, d=dls, hb=hb, dc=dc, psel=psel:
                         e.activation(out=d[:, hb * 512:(hb + 1) * 512], in_=psel[:, 0:512],
                                      func=AF.Exp, bias=bdt_sb[:, dc:dc + 1]),
                         waits=[("T", dt_T)])
                sch.emit("A", lambda e, d=dls, hb=hb:
                         e.activation(out=d[:, hb * 512:(hb + 1) * 512], in_=d[:, hb * 512:(hb + 1) * 512],
                                      func=AF.Ln, bias=1.0), inc=1)
                psmm_freeA.append(sch.now("A"))
            wdt_last[lp] = sch.now("T")
            sp_As[dc] = sch.now("A")
            sch.emit("V", lambda e, d=dxs, a=dls, b=xcv: e.tensor_mul(d, a, b),
                     waits=[("A", sp_As[dc]), (chL, ld_Ss[dc])], inc=1)
            dx_Vs[dc] = sch.now("V")

        def p2_lp(lp, gen=None, bud=None):
            g0, g1 = lp * Lp, (lp + 1) * Lp
            vend_at_lp[0] = sch.now("V")
            drv = dr_sb[:, (lp % 2) * Lp:(lp % 2 + 1) * Lp]
            wdr = list(gen_bar[lp]["bc"])
            if lp >= 2:
                wdr.append(("T", wdt_last[lp - 2]))
            sch.dma("drl", lambda e, d=drv, g0=g0, g1=g1:
                    e.dma_start(out=d, in_=DR[:, g0:g1]), waits=wdr)
            drl_tot[lp] = sch.now("drl")
            p2_rep_build(lp, 0)
            p2_rep_build(lp, 1)
            p2_prolog(lp, 0)

            for dci in range(NDC):
                dc = dci
                dsl = dci % 2
                dls = delta_sl[dsl]
                dxs = dx_sl[dsl]
                xcv = xc2_sl[dsl]
                szv = sz2_sl[dsl]
                ygv = yg2_sl[dsl]
                sp_A = sp_As[dc]
                dx_V = dx_Vs[dc]
                for n in range(NS):
                    it = len(da_freeV)
                    asl = it % 2
                    dav = da_sl[asl]
                    bxv = bx_sl[asl]
                    hv = h_sl[asl]
                    tv = t_sl[asl]
                    w3 = [("A", sp_A)] if n == 0 else []
                    if it == 0 and lp == 0 and li == 0:
                        w3.append(("V", negA_V))
                    if it >= 2:
                        w3.append(("V", da_freeV[it - 2]))
                    sch.emit("A", lambda e, d=dav, s_=dls, dc=dc, n=n:
                             e.activation(out=d, in_=s_, func=AF.Exp,
                                          scale=A_sb[:, dc * NS + n:dc * NS + n + 1]),
                             waits=w3, inc=1)
                    exp_A = sch.now("A")
                    w4 = [("V", dx_V)] if n == 0 else []
                    if dci == 0:
                        w4.append(("A", bevac[n]))
                    sch.emit("V", lambda e, d=bxv, a=dxs, n=n:
                             e.tensor_mul(d, a, brep[:, n * Lp:(n + 1) * Lp]), waits=w4)
                    sidx = dc * NS + n
                    sch.emit("V", lambda e, d=hv, a=dav, b=bxv, sidx=sidx:
                             e.tensor_tensor_scan(d, a, b, states[:, sidx:sidx + 1], ALU.mult, ALU.add),
                             waits=[("A", exp_A)])
                    sch.emit("V", lambda e, d=hv, sidx=sidx:
                             e.tensor_copy(states[:, sidx:sidx + 1], d[:, Lp - 1:Lp]))
                    w4b = [("A", cevac[n])] if dci == 0 else []
                    if it >= 2:
                        w4b.append(("T", tmm_T[it - 2]))
                    sch.emit("V", lambda e, d=tv, a=hv, n=n:
                             e.tensor_mul(d, a, crep[:, n * Lp:(n + 1) * Lp]), waits=w4b, inc=1)
                    da_freeV.append(sch.now("V"))
                    t_V = sch.now("V")
                    if gen is not None and bud[0] > 0 and dci >= 1 and n % 2 == 1:
                        next(gen)
                        bud[0] -= 1
                    for hb in range(HB):
                        w5 = [("V", t_V)] if hb == 0 else []
                        if n == 0 and hb == 0 and psy_freeV:
                            w5.append(("V", psy_freeV[-1]))
                        sch.emit("T", lambda e, d=tv, hb=hb, n=n:
                                 e.matmul(ps_y[:, hb * 512:(hb + 1) * 512], ident[:],
                                          d[:, hb * 512:(hb + 1) * 512],
                                          start=(n == 0), stop=(n == NS - 1)),
                                 waits=w5, inc=(1 if hb == HB - 1 else 0))
                    tmm_T.append(sch.now("T"))
                    if dci == 0 and n < NS - 2:
                        p2_rep_build(lp, n + 2)
                    if n == 4 and dci + 1 < NDC:
                        p2_prolog(lp, dci + 1)
                y_T = sch.now("T")
                sch.emit("V", lambda e, a=xcv, dc=dc:
                         e.scalar_tensor_tensor(t2_sl, a, dv_sb[:, dc:dc + 1], ps_y[:, 0:Lp],
                                                ALU.mult, ALU.add),
                         waits=[("T", y_T)], inc=1)
                psy_freeV.append(sch.now("V"))
                chY = f"ygs{dsl}"
                wyg = []
                if len(ygs_tot) >= 2:
                    wyg.append((chY, ygs_tot[-2]))
                sch.emit("V", lambda e, d=ygv, b=szv: e.tensor_mul(d, t2_sl, b), waits=wyg, inc=1)
                yg_V = sch.now("V")
                p2_ld_freeV.append(yg_V)
                sch.dma(chY, lambda e, dc=dc, g0=g0, g1=g1, d=ygv:
                        e.dma_start(out=YG[dc, :, g0:g1], in_=d),
                        waits=[("V", yg_V)], eng="G")
                ygs_tot.append(sch.now(chY))

        gen = p1_gen()
        for _ in range(2 * NPIECE):
            next(gen)
        record_bar(0, 1)
        for lp in range(nlp):
            if lp < nlp - 1:
                bud = [2 * NPIECE]
                p2_lp(lp, gen, bud)
                assert bud[0] == 0, (li, lp, bud[0])
                record_bar(lp + 1, lp % 2)
            else:
                p2_lp(lp)
        exhausted = False
        try:
            next(gen)
        except StopIteration:
            exhausted = True
        assert exhausted, "p1 generator not exhausted"
        psbc_prev[0] = rep_evacA[-1]
        p1_A = sch.now("A")
        p2_bar = [("ygs0", sch.now("ygs0")), ("ygs1", sch.now("ygs1"))]

        p3_freeV = []
        wowin_freeT = []
        p3psum_freeV = []
        xas_tots = []

        def p3_span(s):
            l0, l1 = s * Ts, (s + 1) * Ts
            xv = xsp3[s % 2]
            chX = f"xl{s % 2}"
            chYL = f"xcl{s % 2}"
            w = list(p2_bar) if s == 0 else [("V", p3_freeV[-1])]
            for dc in range(NDC):
                sch.dma(chYL, lambda e, dc=dc, l0=l0, l1=l1:
                        e.dma_start(out=ygsp[:, dc * Ts:(dc + 1) * Ts], in_=YG[dc, :, l0:l1]),
                        waits=(w if dc == 0 else []))
            ygl_S = sch.now(chYL)
            for kc in range(KC):
                wxl = []
                if s >= 2 and kc == 0:
                    wxl = [("V", p3_freeV[-2]), xas_tots[s - 2]]
                sch.dma(chX, lambda e, kc=kc, l0=l0, l1=l1, xv=xv, li=li:
                        e.dma_start(out=xv[:, kc * Ts:(kc + 1) * Ts], in_=x_src(li, kc, l0, l1)),
                        waits=wxl)
            xl_S = sch.now(chX)
            for kc in range(KC):
                gi = s * KC + kc
                par = kc % 2
                chW = f"wol{par}"
                wv = wowin[:, par * NDC * 128:(par + 1) * NDC * 128]
                w2 = list(wstage) if gi < 2 else [("T", wowin_freeT[gi - 2])]
                sch.dma(chW, lambda e, d=wv, kc=kc: e.dma_start(out=d, in_=WOUTB[kc]), waits=w2)
                wl_S = sch.now(chW)
                psel = ps_mm if par == 0 else ps_mm2
                for dc in range(NDC):
                    w3 = []
                    if dc == 0:
                        w3 = [(chYL, ygl_S), (chW, wl_S)]
                        if gi >= 2:
                            w3.append(("V", p3psum_freeV[gi - 2]))
                        else:
                            w3.append(("A", p1_A))
                    sch.emit("T", lambda e, dc=dc, wv=wv, psel=psel:
                             e.matmul(psel[:, 0:Ts], wv[:, dc * 128:(dc + 1) * 128],
                                      ygsp[:, dc * Ts:(dc + 1) * Ts],
                                      start=(dc == 0), stop=(dc == NDC - 1)),
                             waits=w3, inc=(1 if dc == NDC - 1 else 0))
                mm_T = sch.now("T")
                wowin_freeT.append(mm_T)
                sch.emit("V", lambda e, kc=kc, xv=xv, psel=psel:
                         e.tensor_add(xv[:, kc * Ts:(kc + 1) * Ts], xv[:, kc * Ts:(kc + 1) * Ts],
                                      psel[:, 0:Ts]),
                         waits=([("T", mm_T), (chX, xl_S)] if kc == 0 else [("T", mm_T)]), inc=1)
                p3psum_freeV.append(sch.now("V"))
                xa_V = sch.now("V")
                sch.dma(f"xas{s % 2}", lambda e, kc=kc, l0=l0, l1=l1, xv=xv:
                        e.dma_start(out=xA[kc, :, l0:l1], in_=xv[:, kc * Ts:(kc + 1) * Ts]),
                        waits=[("V", xa_V)], eng="G")
            p3_freeV.append(sch.now("V"))
            xas_tots.append((f"xas{s % 2}", sch.now(f"xas{s % 2}")))

        for s in range(nsp):
            p3_span(s)
        xas_tot = [sch.now("xas0"), sch.now("xas1")]

    p3end = [("xas0", xas_tot[0]), ("xas1", xas_tot[1])]
    wq_Alist = []
    for oc in range(KC):
        w = list(p3end) if oc == 0 else [("A", wq_Alist[-1])]
        sch.dma("misc", lambda e, oc=oc: e.dma_start(out=stg_f[:, 0:512], in_=fw[oc, :, 0:512]), waits=w)
        sch.dma("misc", lambda e, oc=oc: e.dma_start(out=stg_f[:, 512:1024], in_=fw[oc, :, 512:1024]))
        fw_S = sch.now("misc")
        sch.emit("V", lambda e: e.reduce_max(am_sb[:, 0:1], stg_f[:, 0:DM],
                                             axis=mybir.AxisListType.X, apply_absolute_value=True),
                 waits=[("misc", fw_S)])
        sch.emit("V", lambda e: e.tensor_scalar_max(am_sb[:, 0:1], am_sb[:, 0:1], FFN_EPS))
        sch.emit("V", lambda e: e.reciprocal(am_sb[:, 1:2], am_sb[:, 0:1]))
        sch.emit("V", lambda e, oc=oc: e.tensor_scalar_mul(wsc_sb[:, oc:oc + 1], am_sb[:, 0:1], 1.0 / QMAX))
        sch.emit("V", lambda e: e.tensor_scalar_mul(am_sb[:, 1:2], am_sb[:, 1:2], QMAX))
        sch.emit("V", lambda e: e.tensor_scalar(stg_f[:, 0:DM], stg_f[:, 0:DM],
                                                am_sb[:, 1:2], RND_M, ALU.mult, ALU.add))
        sch.emit("V", lambda e: e.tensor_scalar(stg_f[:, 0:DM], stg_f[:, 0:DM],
                                                RND_M, None, ALU.subtract), inc=1)
        wq_V = sch.now("V")
        for ic in range(KC):
            sch.emit("T", lambda e, ic=ic: e.transpose(ps_aux[:, 128:256], stg_f[:, ic * 128:(ic + 1) * 128],
                                                       identf[:]),
                     waits=([("V", wq_V)] if ic == 0 else [("A", wq_Alist[-1])]), inc=1)
            tr_T = sch.now("T")
            sch.emit("A", lambda e, oc=oc, ic=ic:
                     e.activation(out=wqt_sb[:, (ic * KC + oc) * 128:(ic * KC + oc + 1) * 128],
                                  in_=ps_aux[:, 128:256], func=AF.Copy),
                     waits=[("T", tr_T)] + ([("ygs0", sch.now("ygs0")), ("ygs1", sch.now("ygs1"))]
                                            if (oc == 0 and ic == 0) else []),
                     inc=1)
            wq_Alist.append(sch.now("A"))
    sch.dma("idl", lambda e: e.dma_start(out=fb_sb[:, 0:KC], in_=fb[:, :]))
    wq_A = sch.now("A")
    fb_S = sch.now("idl")

    ffn_freeV = []
    ffn_store = []

    def ln_span(src_sp, dst_sp, first_waits):
        for kc in range(KC):
            sch.emit("A", lambda e, kc=kc, s_=src_sp:
                     e.activation(out=ub_sp[:, kc * Ts:(kc + 1) * Ts], in_=s_[:, kc * Ts:(kc + 1) * Ts],
                                  func=AF.Copy),
                     waits=(list(first_waits) if kc == 0 else []), inc=(1 if kc == KC - 1 else 0))
        cast_A = sch.now("A")
        for kc in range(KC):
            sch.emit("T", lambda e, kc=kc:
                     e.matmul(ps_st[0:1, 0:Ts], ones_bc[:], ub_sp[:, kc * Ts:(kc + 1) * Ts],
                              start=(kc == 0), stop=(kc == KC - 1)),
                     waits=([("A", cast_A)] if kc == 0 else []), inc=(1 if kc == KC - 1 else 0))
        sum_T = sch.now("T")
        sch.emit("V", lambda e: e.tensor_scalar_mul(rowbuf[0:1, 0:Ts], ps_st[0:1, 0:Ts], 1.0 / DM),
                 waits=[("T", sum_T)], inc=1)
        m_V = sch.now("V")
        for kc in range(KC):
            sch.emit("A", lambda e, kc=kc, s_=src_sp:
                     e.activation(out=ub_sp[:, kc * Ts:(kc + 1) * Ts], in_=s_[:, kc * Ts:(kc + 1) * Ts],
                                  func=AF.Square),
                     waits=([("T", sum_T)] if kc == 0 else []), inc=(1 if kc == KC - 1 else 0))
        sq_A = sch.now("A")
        for kc in range(KC):
            sch.emit("T", lambda e, kc=kc:
                     e.matmul(ps_st[0:1, 0:Ts], ones_bc[:], ub_sp[:, kc * Ts:(kc + 1) * Ts],
                              start=(kc == 0), stop=(kc == KC - 1)),
                     waits=([("A", sq_A), ("V", m_V)] if kc == 0 else []), inc=(1 if kc == KC - 1 else 0))
        sq_T = sch.now("T")
        sch.emit("V", lambda e: e.tensor_scalar_mul(rowbuf[0:1, Ts:2 * Ts], ps_st[0:1, 0:Ts], 1.0 / DM),
                 waits=[("T", sq_T)])
        sch.emit("V", lambda e: e.tensor_mul(rowbuf[0:1, 2 * Ts:3 * Ts], rowbuf[0:1, 0:Ts], rowbuf[0:1, 0:Ts]))
        sch.emit("V", lambda e: e.tensor_sub(rowbuf[0:1, Ts:2 * Ts], rowbuf[0:1, Ts:2 * Ts],
                                             rowbuf[0:1, 2 * Ts:3 * Ts]))
        sch.emit("V", lambda e: e.tensor_scalar_add(rowbuf[0:1, Ts:2 * Ts], rowbuf[0:1, Ts:2 * Ts], LN_EPS),
                 inc=1)
        var_V = sch.now("V")
        sch.emit("A", lambda e: e.activation(out=rowbuf[0:1, Ts:2 * Ts], in_=rowbuf[0:1, Ts:2 * Ts], func=AF.Ln),
                 waits=[("V", var_V)])
        sch.emit("A", lambda e: e.activation(out=rowbuf[0:1, Ts:2 * Ts], in_=rowbuf[0:1, Ts:2 * Ts],
                                             func=AF.Exp, scale=-0.5), inc=1)
        rstd_A = sch.now("A")
        sch.emit("V", lambda e: e.tensor_mul(rowbuf[0:1, 2 * Ts:3 * Ts], rowbuf[0:1, 0:Ts],
                                             rowbuf[0:1, Ts:2 * Ts]),
                 waits=[("A", rstd_A)], inc=1)
        mr_V = sch.now("V")
        sch.emit("T", lambda e: e.matmul(ps_aux[:, 0:Ts], ones_f[:], rowbuf[0:1, Ts:2 * Ts],
                                         start=True, stop=True),
                 waits=[("V", mr_V)])
        sch.emit("T", lambda e: e.matmul(ps_bc[:, 0:Ts], ones_f[:], rowbuf[0:1, 2 * Ts:3 * Ts],
                                         start=True, stop=True), inc=1)
        bc_T = sch.now("T")
        for kc in range(KC):
            sch.emit("V", lambda e, kc=kc, s_=src_sp, d_=dst_sp:
                     e.tensor_mul(d_[:, kc * Ts:(kc + 1) * Ts], s_[:, kc * Ts:(kc + 1) * Ts], ps_aux[:, 0:Ts]),
                     waits=([("T", bc_T)] if kc == 0 else []))
            sch.emit("V", lambda e, kc=kc, d_=dst_sp:
                     e.tensor_sub(d_[:, kc * Ts:(kc + 1) * Ts], d_[:, kc * Ts:(kc + 1) * Ts], ps_bc[:, 0:Ts]),
                     inc=(1 if kc == KC - 1 else 0))
        return sch.now("V")

    def ffn_span(s):
        l0, l1 = s * Ts, (s + 1) * Ts
        if s == 0:
            w = [("A", wq_A), ("idl", fb_S)]
        else:
            w = [("V", ffn_freeV[-1]), ("fos", ffn_store[-1])]
        for kc in range(KC):
            sch.dma("ffl", lambda e, kc=kc, l0=l0, l1=l1:
                    e.dma_start(out=u_sp[:, kc * Ts:(kc + 1) * Ts], in_=x_in[kc * 128:(kc + 1) * 128, l0:l1]),
                    waits=(w if kc == 0 else []))
            sch.dma("ffl", lambda e, kc=kc, l0=l0, l1=l1:
                    e.dma_start(out=o1_sp[:, kc * Ts:(kc + 1) * Ts], in_=xA[kc, :, l0:l1]))
        ld_S = sch.now("ffl")
        for kc in range(KC):
            sch.emit("V", lambda e, kc=kc:
                     e.tensor_add(u_sp[:, kc * Ts:(kc + 1) * Ts], u_sp[:, kc * Ts:(kc + 1) * Ts],
                                  o1_sp[:, kc * Ts:(kc + 1) * Ts]),
                     waits=([("ffl", ld_S)] if kc == 0 else []), inc=(1 if kc == KC - 1 else 0))
        u_V = sch.now("V")
        ln1_V = ln_span(u_sp, o1_sp, [("V", u_V)])
        amred_V = None
        for tb in range(NT):
            for kc in range(KC):
                w_tr = []
                if kc == 0:
                    w_tr = [("V", ln1_V)] if tb == 0 else [("V", amred_V)]
                sch.emit("T", lambda e, kc=kc, tb=tb:
                         e.transpose(ps_bc[:, kc * 128:(kc + 1) * 128],
                                     o1_sp[:, kc * Ts + tb * 128:kc * Ts + (tb + 1) * 128], identf[:]),
                         waits=w_tr, inc=(1 if kc == KC - 1 else 0))
            tr_T = sch.now("T")
            sch.emit("V", lambda e, tb=tb:
                     e.reduce_max(am_sb[:, tb:tb + 1], ps_bc[:, 0:DM],
                                  axis=mybir.AxisListType.X, apply_absolute_value=True),
                     waits=[("T", tr_T)], inc=1)
            amred_V = sch.now("V")
        sch.emit("V", lambda e: e.tensor_scalar_max(am_sb[:, 0:NT], am_sb[:, 0:NT], FFN_EPS), inc=1)
        amc_V = sch.now("V")
        for c in range(NT):
            sch.emit("T", lambda e, c=c: e.transpose(ps_aux[0:1, c * 128:(c + 1) * 128],
                                                     am_sb[:, c:c + 1], identf[:]),
                     waits=([("V", amc_V)] if c == 0 else []), inc=(1 if c == NT - 1 else 0))
        amt_T = sch.now("T")
        sch.emit("A", lambda e: e.activation(out=rowbuf[0:1, 0:Ts], in_=ps_aux[0:1, 0:Ts],
                                             func=AF.Copy, scale=1.0 / QMAX),
                 waits=[("T", amt_T)], inc=1)
        srow_A = sch.now("A")
        sch.emit("V", lambda e: e.reciprocal(rowbuf[0:1, Ts:2 * Ts], rowbuf[0:1, 0:Ts]),
                 waits=[("A", srow_A)], inc=1)
        rrow_V = sch.now("V")
        sch.emit("T", lambda e: e.matmul(ps_aux[:, 0:Ts], ones_f[:], rowbuf[0:1, Ts:2 * Ts],
                                         start=True, stop=True),
                 waits=[("V", rrow_V)])
        sch.emit("T", lambda e: e.matmul(ps_bc[:, 0:Ts], ones_f[:], rowbuf[0:1, 0:Ts],
                                         start=True, stop=True), inc=1)
        scl_T = sch.now("T")
        sch.emit("A", lambda e: e.activation(out=srep_sp, in_=ps_bc[:, 0:Ts], func=AF.Copy),
                 waits=[("T", scl_T)], inc=1)
        srep_A = sch.now("A")
        for kc in range(KC):
            sch.emit("V", lambda e, kc=kc:
                     e.tensor_mul(u_sp[:, kc * Ts:(kc + 1) * Ts], o1_sp[:, kc * Ts:(kc + 1) * Ts],
                                  ps_aux[:, 0:Ts]),
                     waits=([("T", scl_T)] if kc == 0 else []))
            sch.emit("V", lambda e, kc=kc:
                     e.tensor_scalar(u_sp[:, kc * Ts:(kc + 1) * Ts], u_sp[:, kc * Ts:(kc + 1) * Ts],
                                     RND_M, RND_M, ALU.add, ALU.subtract))
            sch.emit("V", lambda e, kc=kc:
                     e.tensor_copy(xq_sp[:, kc * Ts:(kc + 1) * Ts], u_sp[:, kc * Ts:(kc + 1) * Ts]),
                     inc=(1 if kc == KC - 1 else 0))
        xq_V = sch.now("V")
        ffnq_freeV = []
        for oc in range(KC):
            psel = ps_mm if oc % 2 == 0 else ps_mm2
            for ic in range(KC):
                w_q = []
                if ic == 0:
                    w_q = [("V", xq_V)]
                    if oc >= 2:
                        w_q.append(("V", ffnq_freeV[oc - 2]))
                sch.emit("T", lambda e, oc=oc, ic=ic, psel=psel:
                         e.matmul(psel[:, 0:Ts], wqt_sb[:, (ic * KC + oc) * 128:(ic * KC + oc + 1) * 128],
                                  xq_sp[:, ic * Ts:(ic + 1) * Ts],
                                  start=(ic == 0), stop=(ic == KC - 1)),
                         waits=w_q, inc=(1 if ic == KC - 1 else 0))
            q_T = sch.now("T")
            sch.emit("V", lambda e, oc=oc, psel=psel:
                     e.tensor_mul(u_sp[:, oc * Ts:(oc + 1) * Ts], psel[:, 0:Ts], srep_sp),
                     waits=[("T", q_T)] + ([("A", srep_A)] if oc == 0 else []))
            sch.emit("V", lambda e, oc=oc:
                     e.tensor_scalar(u_sp[:, oc * Ts:(oc + 1) * Ts], u_sp[:, oc * Ts:(oc + 1) * Ts],
                                     wsc_sb[:, oc:oc + 1], fb_sb[:, oc:oc + 1], ALU.mult, ALU.add))
            sch.emit("V", lambda e, oc=oc:
                     e.tensor_add(u_sp[:, oc * Ts:(oc + 1) * Ts], u_sp[:, oc * Ts:(oc + 1) * Ts],
                                  o1_sp[:, oc * Ts:(oc + 1) * Ts]), inc=1)
            ffnq_freeV.append(sch.now("V"))
        res_V = sch.now("V")
        ln2_V = ln_span(u_sp, o1_sp, [("V", res_V)])
        for kc in range(KC):
            sch.dma("fos", lambda e, kc=kc, l0=l0, l1=l1:
                    e.dma_start(out=y_out[kc * 128:(kc + 1) * 128, l0:l1],
                                in_=o1_sp[:, kc * Ts:(kc + 1) * Ts]),
                    waits=([("V", ln2_V)] if kc == 0 else []), eng="G")
        ffn_freeV.append(ln2_V)
        ffn_store.append(sch.now("fos"))

    for s in range(nsp):
        ffn_span(s)
    out_S = sch.now("fos")
    sch.emit("V", lambda e: e.nop(), waits=[("fos", out_S)])

    block = ctx.enter_context(nc.Block())
    by_eng = {"T": [], "A": [], "V": [], "S": [], "G": []}
    for eng, waits, fn, inc_key, inc_amt in sch.items:
        by_eng[eng].append((waits, fn, inc_key, inc_amt))

    def runner(key):
        def _f(eng):
            for waits, fn, inc_key, inc_amt in by_eng[key]:
                for (wk, wv) in waits:
                    if wv > 0:
                        eng.wait_ge(SEMS[wk], wv)
                ins = fn(eng)
                if inc_key is not None:
                    assert ins is not None
                    ins.then_inc(SEMS[inc_key], inc_amt)
        return _f

    block.tensor(runner("T"))
    block.scalar(runner("A"))
    block.vector(runner("V"))
    block.sync(runner("S"))
    block.gpsimd(runner("G"))
    ctx.close()
    return nc


def pack_inputs(inp, b, L=8192, layers=4):
    import ml_dtypes
    x = np.ascontiguousarray(np.asarray(inp["L0_attn_x"])[b].T)
    W_in = np.asarray(inp["L0_attn_W_in"])[:layers]
    winT = np.ascontiguousarray(
        W_in.transpose(0, 2, 1).reshape(layers, KC, 128, NEC, 128).transpose(0, 3, 1, 2, 4))
    W_out = np.asarray(inp["L0_attn_W_out"])[:layers]
    woutT = np.ascontiguousarray(
        W_out.transpose(0, 2, 1).reshape(layers, NDC, 128, KC, 128).transpose(0, 3, 1, 2, 4))
    W_x = np.asarray(inp["L0_attn_W_x"])[:layers]
    wxT = np.ascontiguousarray(
        W_x.transpose(0, 2, 1).reshape(layers, NDC, 128, 96)
        .transpose(0, 2, 1, 3).reshape(layers, 128, NDC * 96))
    W_dt = np.asarray(inp["L0_attn_W_dt"])[:layers]
    wdtT = np.ascontiguousarray(W_dt.transpose(0, 2, 1))
    alog = np.ascontiguousarray(np.asarray(inp["L0_attn_A_log"])[:layers]
                                .reshape(layers, NDC, 128, NS).transpose(0, 2, 1, 3)
                                .reshape(layers, 128, NDC * NS))
    convw = np.ascontiguousarray(np.asarray(inp["L0_attn_conv_w"])[:layers]
                                 .reshape(layers, NDC, 128, DCV).transpose(0, 2, 1, 3)
                                 .reshape(layers, 128, NDC * DCV))
    convb = np.ascontiguousarray(np.asarray(inp["L0_attn_conv_b"])[:layers]
                                 .reshape(layers, NDC, 128).transpose(0, 2, 1))
    bdt = np.ascontiguousarray(np.asarray(inp["L0_attn_b_dt"])[:layers]
                               .reshape(layers, NDC, 128).transpose(0, 2, 1))
    dvec = np.ascontiguousarray(np.asarray(inp["L0_attn_D"])[:layers]
                                .reshape(layers, NDC, 128).transpose(0, 2, 1))
    normw = np.ascontiguousarray(np.asarray(inp["L0_attn_norm_w"])[:layers]
                                 .reshape(layers, KC, 128).transpose(0, 2, 1))
    fwt = np.ascontiguousarray(np.asarray(inp["L0_ffn_weight"]).reshape(KC, 128, DM))
    fbt = np.ascontiguousarray(np.asarray(inp["L0_ffn_bias"]).reshape(KC, 128).T)
    return {
        "x_in": x.astype(np.float32),
        "winT": winT.astype(np.float32), "woutT": woutT.astype(np.float32),
        "wxT": wxT.astype(np.float32), "wdtT": wdtT.astype(np.float32),
        "alog": alog.astype(np.float32), "convw": convw.astype(np.float32),
        "convb": convb.astype(np.float32), "bdt": bdt.astype(np.float32),
        "dvec": dvec.astype(np.float32), "normw": normw.astype(np.float32),
        "fw": fwt.astype(np.float32), "fb": fbt.astype(np.float32),
        "ident_in": np.eye(128, dtype=ml_dtypes.bfloat16),
        "identf_in": np.eye(128, dtype=np.float32),
    }


def kernel(**inputs):
    from concourse.bass_utils import run_bass_kernel_spmd
    L = 8192
    layers = 4
    nc = build_program(L=L, layers=layers, debug_internals=False)
    im = [pack_inputs(inputs, 0, L=L, layers=layers),
          pack_inputs(inputs, 1, L=L, layers=layers)]
    in_maps = [im[c] if c < 2 else im[0] for c in range(8)]
    res = run_bass_kernel_spmd(nc, in_maps, core_ids=list(range(8)))
    out0 = np.ascontiguousarray(res.results[0]["y_out"].T)
    out1 = np.ascontiguousarray(res.results[1]["y_out"].T)
    return np.stack([out0, out1]).astype(np.float32)


# version 10
# speedup vs baseline: 1.1493x; 1.0085x over previous
import numpy as np
from contextlib import ExitStack

import concourse.bass as bass
import concourse.mybir as mybir

F32 = mybir.dt.float32
BF16 = mybir.dt.bfloat16
AF = mybir.ActivationFunctionType
ALU = mybir.AluOpType

DM, DI, RK, NS, DCV = 1024, 2048, 64, 16, 4
KC, NDC, NEC = DM // 128, DI // 128, 2 * DI // 128
RMS_EPS = 1e-5
LN_EPS = 1e-5
FFN_EPS = 1e-8
QMAX = 127.0
RND_M = float(3 * 2 ** 22)

DMA_CH = ["idl", "misc", "stgl0", "stgl1", "stgs0", "stgs1", "xl0", "xl1",
          "wwl0", "wwl1", "wol0", "wol1", "xcl0", "xcl1", "brl0", "brl1",
          "xcs00", "xcs01", "xcs10", "xcs11", "szs00", "szs01", "szs10", "szs11",
          "ygs0", "ygs1", "bcs0", "bcs1", "drl",
          "xas0", "xas1", "ffl", "fos"]
ENG = ["T", "A", "V", "G"]


class Sched:
    def __init__(self):
        self.items = []
        self.cnt = {k: 0 for k in ENG + DMA_CH}

    SELF_SERIAL = ("A", "V", "G")

    def emit(self, eng, fn, waits=(), inc=0):
        waits = list(waits)
        if eng in self.SELF_SERIAL:
            waits.append((eng, self.cnt[eng]))
            inc = 1
        self.items.append((eng, tuple(waits), fn, eng if inc else None, inc))
        if inc:
            self.cnt[eng] += inc
        return self.cnt[eng]

    def dma(self, ch, fn, waits=(), eng="S"):
        assert ch in DMA_CH, ch
        self.items.append((eng, tuple(waits), fn, ch, 16))
        self.cnt[ch] += 16
        return self.cnt[ch]

    def now(self, key):
        return self.cnt[key]


def build_program(L=8192, layers=4, Ts=512, Lp=1024, debug_internals=False):
    nsp = L // Ts
    nlp = L // Lp
    HB = Lp // 512
    NT = Ts // 128
    nc = bass.Bass("TRN2", target_bir_lowering=False, debug=False)

    EI, EO, IN = "ExternalInput", "ExternalOutput", "Internal"
    x_in = nc.dram_tensor("x_in", [DM, L], F32, kind=EI).ap()
    winT = nc.dram_tensor("winT", [layers, NEC, KC, 128, 128], F32, kind=EI).ap()
    woutT = nc.dram_tensor("woutT", [layers, KC, NDC, 128, 128], F32, kind=EI).ap()
    wxT = nc.dram_tensor("wxT", [layers, 128, NDC * 96], F32, kind=EI).ap()
    wdtT = nc.dram_tensor("wdtT", [layers, 64, NDC * 128], F32, kind=EI).ap()
    alog = nc.dram_tensor("alog", [layers, 128, NDC * NS], F32, kind=EI).ap()
    convw = nc.dram_tensor("convw", [layers, 128, NDC * DCV], F32, kind=EI).ap()
    convb = nc.dram_tensor("convb", [layers, 128, NDC], F32, kind=EI).ap()
    bdt = nc.dram_tensor("bdt", [layers, 128, NDC], F32, kind=EI).ap()
    dvec = nc.dram_tensor("dvec", [layers, 128, NDC], F32, kind=EI).ap()
    normw = nc.dram_tensor("normw", [layers, 128, KC], F32, kind=EI).ap()
    fw = nc.dram_tensor("fw", [KC, 128, DM], F32, kind=EI).ap()
    fb = nc.dram_tensor("fb", [128, KC], F32, kind=EI).ap()
    ident_in = nc.dram_tensor("ident_in", [128, 128], BF16, kind=EI).ap()
    identf_in = nc.dram_tensor("identf_in", [128, 128], F32, kind=EI).ap()

    y_out = nc.dram_tensor("y_out", [DM, L], F32, kind=EO).ap()

    ikind = EO if debug_internals else IN
    xA = nc.dram_tensor("xA", [KC, 128, L], F32, kind=ikind).ap()
    XC = nc.dram_tensor("XCb", [NDC, 128, L], F32, kind=ikind).ap()
    SZ = nc.dram_tensor("SZb", [NDC, 128, L], BF16, kind=ikind).ap()
    YG = nc.dram_tensor("YGb", [NDC, 128, L], BF16, kind=ikind).ap()
    BCB = nc.dram_tensor("BCB", [2, NS, L], BF16, kind=ikind).ap()
    DR = nc.dram_tensor("DRb", [64, L], BF16, kind=IN).ap()
    WINB = nc.dram_tensor("WINB", [NEC, 128, KC * 128], BF16, kind=IN).ap()
    WOUTB = nc.dram_tensor("WOUTB", [KC, 128, NDC * 128], BF16, kind=IN).ap()

    ctx = ExitStack()
    sb = lambda name, shape, dt: ctx.enter_context(nc.sbuf_tensor(name, shape, dt))
    psb = lambda name, shape, dt: ctx.enter_context(nc.psum_tensor(name, shape, dt))

    dr_sb = sb("dr_sb", [64, 2 * Lp], BF16)
    stage96 = sb("stage96", [128, 512], BF16)
    browA = sb("browA", [1, 2 * Lp], BF16)
    browC = sb("browC", [1, 2 * Lp], BF16)
    states = sb("states", [128, NDC * NS], F32)
    halo = sb("halo", [128, NDC * (DCV - 1)], F32)
    wx_sb = sb("wx_sb", [128, NDC * 96], F32)
    wdt_sb = sb("wdt_sb", [64, NDC * 128], BF16)
    A_sb = sb("A_sb", [128, NDC * NS], F32)
    cw_sb = sb("cw_sb", [128, NDC * DCV], F32)
    cb_sb = sb("cb_sb", [128, NDC], F32)
    bdt_sb = sb("bdt_sb", [128, NDC], F32)
    dv_sb = sb("dv_sb", [128, NDC], F32)
    nw_sb = sb("nw_sb", [128, KC], F32)
    wsc_sb = sb("wsc_sb", [128, KC], F32)
    fb_sb = sb("fb_sb", [128, KC], F32)
    am_sb = sb("am_sb", [128, 2 * NT], F32)
    ident = sb("ident", [128, 128], BF16)
    identf = sb("identf", [128, 128], F32)
    ones_f = sb("ones_f", [1, 128], F32)
    ones_bf = sb("ones_bf", [1, 128], BF16)
    ones_bc = sb("ones_bc", [128, 1], BF16)
    wwin = sb("wwin", [128, 2 * KC * 128], BF16)
    wowin = sb("wowin", [128, 2 * NDC * 128], BF16)
    stg_f = sb("stg_f", [128, 2048], F32)
    stg_b = sb("stg_b", [128, 2048], BF16)
    rowbuf = sb("rowbuf", [1, 3 * Ts], F32)

    AFC = 13318
    ABC = 50176
    arena_f = sb("arena_f", [128, AFC], F32)
    arena_b = sb("arena_b", [128, ABC], BF16)

    def fsl(off, cols):
        assert off + cols <= AFC, (off, cols)
        return arena_f[:, off:off + cols]

    def bsl(off, cols):
        assert off + cols <= ABC, (off, cols)
        return arena_b[:, off:off + cols]

    delta_sl = [fsl(i * Lp, Lp) for i in range(2)]
    da_sl = [fsl(2 * Lp + i * Lp, Lp) for i in range(2)]
    xc2_sl = [fsl(4 * Lp + i * Lp, Lp) for i in range(2)]
    t2_sl = fsl(6 * Lp, Lp)
    _p1 = 7 * Lp
    xsp = fsl(_p1, KC * Ts)
    xi_buf = [fsl(_p1 + KC * Ts + i * (Ts + 3), Ts + 3) for i in range(2)]
    xc_sl = [fsl(_p1 + KC * Ts + 2 * (Ts + 3) + i * Ts, Ts) for i in range(2)]
    brep = bsl(0, NS * Lp)
    crep = bsl(NS * Lp, NS * Lp)
    _o = 2 * NS * Lp
    dx_sl = [bsl(_o + i * Lp, Lp) for i in range(2)]
    bx_sl = [bsl(_o + 2 * Lp + i * Lp, Lp) for i in range(2)]
    h_sl = [bsl(_o + 4 * Lp + i * Lp, Lp) for i in range(2)]
    t_sl = [bsl(_o + 6 * Lp + i * Lp, Lp) for i in range(2)]
    sz2_sl = [bsl(_o + 8 * Lp + i * Lp, Lp) for i in range(2)]
    yg2_sl = [bsl(_o + 10 * Lp + i * Lp, Lp) for i in range(2)]
    _p1b = _o + 12 * Lp
    hrms = bsl(_p1b, KC * Ts)
    sz_sl = [bsl(_p1b + KC * Ts + i * Ts, Ts) for i in range(2)]

    xsp3 = [fsl(i * KC * Ts, KC * Ts) for i in range(2)]
    ygsp = bsl(0, NDC * Ts)

    u_sp = fsl(0, KC * Ts)
    o1_sp = fsl(KC * Ts, KC * Ts)
    srep_sp = fsl(2 * KC * Ts, Ts)
    xq_sp = bsl(0, KC * Ts)
    ub_sp = bsl(KC * Ts, KC * Ts)
    wqt_sb = bsl(2 * KC * Ts, KC * KC * 128)

    ps_mm = psb("ps_mm", [128, 512], F32)
    ps_mm2 = psb("ps_mm2", [128, 512], F32)
    ps_y = psb("ps_y", [128, Lp], F32)
    ps_bc = psb("ps_bc", [128, 1024], F32)
    ps_st = psb("ps_st", [128, 512], F32)
    ps_aux = psb("ps_aux", [128, 512], F32)

    SEMS = {}
    for k in ENG + DMA_CH:
        SEMS[k] = ctx.enter_context(nc.semaphore("sem_" + k))

    sch = Sched()

    sch.dma("idl", lambda e: e.dma_start(out=ident[:], in_=ident_in[:, :]))
    sch.dma("idl", lambda e: e.dma_start(out=identf[:], in_=identf_in[:, :]))
    pre_S = sch.now("idl")
    sch.emit("V", lambda e: e.memset(ones_f[:], 1.0))
    sch.emit("V", lambda e: e.memset(ones_bf[:], 1.0))
    sch.emit("V", lambda e: e.memset(ones_bc[:], 1.0), inc=1)
    pre_V = sch.now("V")

    def x_src(li, kc, l0, l1):
        if li == 0:
            return x_in[kc * 128:(kc + 1) * 128, l0:l1]
        return xA[kc, :, l0:l1]

    xas_tot = [0, 0]
    psbc_prev = [0]

    for li in range(layers):
        sch.emit("V", lambda e: e.memset(states[:], 0.0))
        sch.emit("V", lambda e: e.memset(halo[:], 0.0), inc=1)
        zero_V = sch.now("V")

        for (dst, src) in ((cw_sb, convw[li]), (cb_sb, convb[li]), (bdt_sb, bdt[li]),
                           (dv_sb, dvec[li]), (nw_sb, normw[li]), (A_sb, alog[li])):
            sch.dma("misc", lambda e, d=dst, s=src: e.dma_start(out=d[:], in_=s))
        cA_S = sch.now("misc")
        sch.emit("A", lambda e: e.activation(out=A_sb[:], in_=A_sb[:], func=AF.Exp),
                 waits=[("misc", cA_S)], inc=1)
        expA_A = sch.now("A")
        sch.emit("V", lambda e: e.tensor_scalar_mul(A_sb[:], A_sb[:], -1.0),
                 waits=[("A", expA_A)], inc=1)
        negA_V = sch.now("V")
        wxw = [("V", negA_V), ("A", sch.now("A"))]
        if li > 0:
            wxw.append(("T", sch.now("T")))
        sch.dma("misc", lambda e, s=wxT[li]: e.dma_start(out=wx_sb[:], in_=s), waits=wxw)
        sch.dma("misc", lambda e, s=wdtT[li]: e.dma_start(out=stg_f[0:64, 0:NDC * 128], in_=s))
        cWd_S = sch.now("misc")
        sch.emit("A", lambda e: e.activation(out=wdt_sb[:], in_=stg_f[0:64, 0:NDC * 128], func=AF.Copy),
                 waits=[("misc", cWd_S)], inc=1)
        const_A = sch.now("A")

        stage_cast = []
        stage_store = {0: [], 1: []}
        units = []
        for ec in range(NEC):
            units.append((winT[li, ec], WINB[ec], 0))
        for kc in range(KC):
            units.append((woutT[li, kc, 0:8], WOUTB[kc], 0))
            units.append((woutT[li, kc, 8:16], WOUTB[kc], 1024))
        for u, (src, dstrow, dcol) in enumerate(units):
            par = u % 2
            chL, chS = f"stgl{par}", f"stgs{par}"
            f_s = stg_f[:, par * 1024:(par + 1) * 1024]
            b_s = stg_b[:, par * 1024:(par + 1) * 1024]
            w = [("A", const_A)] if u < 2 else [("A", stage_cast[u - 2])]
            sch.dma(chL, lambda e, d=f_s, src=src:
                    e.dma_start(out=d.rearrange("p (k c) -> p k c", k=8),
                                in_=src.rearrange("k p c -> p k c")),
                    waits=w)
            sL = sch.now(chL)
            w2 = [(chL, sL)]
            if len(stage_store[par]) >= 1:
                w2.append((chS, stage_store[par][-1]))
            sch.emit("A", lambda e, d=b_s, sf=f_s: e.activation(out=d, in_=sf, func=AF.Copy),
                     waits=w2, inc=1)
            stage_cast.append(sch.now("A"))
            sch.dma(chS, lambda e, d=dstrow, dcol=dcol, sbf=b_s:
                    e.dma_start(out=d[:, dcol:dcol + 1024], in_=sbf),
                    waits=[("A", stage_cast[-1])], eng="A")
            stage_store[par].append(sch.now(chS))
        wstage = [("stgs0", sch.now("stgs0")), ("stgs1", sch.now("stgs1"))]

        NPIECE = 38
        xsp_free = []
        wwin_freeT = []
        psum_freeA = []
        xbA = {0: 0, 1: 0}
        xbV = {0: 0, 1: 0}
        xcs_tot = []
        szs_tot = []
        bcs_tot = []
        wxmm_T = []
        xl_tot = {}
        hrv_prev = [0]
        drbar = [0]
        rep_evacA = []

        def emit_xload(s):
            l0, l1 = s * Ts, (s + 1) * Ts
            if s == 0:
                w = [("V", zero_V), ("xas0", xas_tot[0]), ("xas1", xas_tot[1])]
            else:
                w = [("V", xsp_free[s - 1])]
            for kc in range(KC):
                sch.dma("xl0", lambda e, kc=kc, l0=l0, l1=l1, li=li:
                        e.dma_start(out=xsp[:, kc * Ts:(kc + 1) * Ts], in_=x_src(li, kc, l0, l1)),
                        waits=(w if kc == 0 else []))
            xl_tot[s] = sch.now("xl0")

        def p1_gen():
            for s in range(nsp):
                l0, l1 = s * Ts, (s + 1) * Ts
                gp = (s // 2 - 1) % 2
                if s == 0:
                    emit_xload(0)
                w_sq = [("xl0", xl_tot[s])]
                if wwin_freeT:
                    w_sq.append(("T", wwin_freeT[-1]))
                for kc in range(KC):
                    sch.emit("A", lambda e, kc=kc:
                             e.activation(out=hrms[:, kc * Ts:(kc + 1) * Ts],
                                          in_=xsp[:, kc * Ts:(kc + 1) * Ts], func=AF.Square),
                             waits=(w_sq if kc == 0 else []), inc=(1 if kc == KC - 1 else 0))
                sq_A = sch.now("A")
                yield
                w_st = [("A", sq_A), ("V", pre_V)]
                if hrv_prev[0]:
                    w_st.append(("V", hrv_prev[0]))
                for kc in range(KC):
                    sch.emit("T", lambda e, kc=kc:
                             e.matmul(ps_aux[0:1, 0:Ts], ones_bc[:], hrms[:, kc * Ts:(kc + 1) * Ts],
                                      start=(kc == 0), stop=(kc == KC - 1)),
                             waits=(w_st if kc == 0 else []), inc=(1 if kc == KC - 1 else 0))
                st_T = sch.now("T")
                yield
                sch.emit("V", lambda e: e.tensor_scalar(rowbuf[0:1, 0:Ts], ps_aux[0:1, 0:Ts],
                                                        1.0 / DM, RMS_EPS, ALU.mult, ALU.add),
                         waits=[("T", st_T)], inc=1)
                ms_V = sch.now("V")
                sch.emit("A", lambda e: e.activation(out=rowbuf[0:1, Ts:2 * Ts], in_=rowbuf[0:1, 0:Ts],
                                                     func=AF.Ln),
                         waits=[("V", ms_V)])
                sch.emit("A", lambda e: e.activation(out=rowbuf[0:1, Ts:2 * Ts], in_=rowbuf[0:1, Ts:2 * Ts],
                                                     func=AF.Exp, scale=-0.5), inc=1)
                rstd_A = sch.now("A")
                yield
                sch.emit("T", lambda e: e.matmul(ps_aux[:, 0:Ts], ones_f[:], rowbuf[0:1, Ts:2 * Ts],
                                                 start=True, stop=True),
                         waits=[("A", rstd_A), ("idl", pre_S)], inc=1)
                bc_T = sch.now("T")
                yield
                for kc in range(KC):
                    sch.emit("V", lambda e, kc=kc:
                             e.scalar_tensor_tensor(hrms[:, kc * Ts:(kc + 1) * Ts],
                                                    xsp[:, kc * Ts:(kc + 1) * Ts],
                                                    nw_sb[:, kc:kc + 1], ps_aux[:, 0:Ts],
                                                    ALU.mult, ALU.mult),
                             waits=([("T", bc_T)] if kc == 0 else []), inc=(1 if kc == KC - 1 else 0))
                hr_V = sch.now("V")
                hrv_prev[0] = hr_V
                xsp_free.append(hr_V)
                yield
                silu_Vs = {}
                conv_Vs = {}

                def emit_wx(dc):
                    wmx = [("V", silu_Vs[dc])]
                    if dc == 0:
                        wmx.append(("A", const_A))
                        if drbar[0]:
                            wmx.append(("A", drbar[0]))
                    sch.emit("T", lambda e, dc=dc:
                             e.matmul(ps_st[0:96, 0:Ts], wx_sb[:, dc * 96:(dc + 1) * 96],
                                      xc_sl[dc % 2][:, 0:Ts],
                                      start=(dc == 0), stop=(dc == NDC - 1)),
                             waits=wmx, inc=1)
                    wxmm_T.append(sch.now("T"))

                def emit_sig_store(dc):
                    xcv = xc_sl[dc % 2]
                    sg = xi_buf[dc % 2]
                    sch.emit("A", lambda e, xcv=xcv, sg=sg:
                             e.activation(out=sg[:, 0:Ts], in_=xcv, func=AF.Exp, scale=-1.0),
                             waits=[("V", conv_Vs[dc])])
                    sch.emit("A", lambda e, sg=sg:
                             e.activation(out=sg[:, 0:Ts], in_=sg[:, 0:Ts], func=AF.Ln, bias=1.0))
                    sch.emit("A", lambda e, sg=sg:
                             e.activation(out=sg[:, 0:Ts], in_=sg[:, 0:Ts], func=AF.Exp, scale=-1.0),
                             inc=1)
                    xbA[dc % 2] = sch.now("A")
                    sch.emit("V", lambda e, xcv=xcv, sg=sg: e.tensor_mul(xcv, xcv, sg[:, 0:Ts]),
                             waits=[("A", xbA[dc % 2])], inc=1)
                    silu_Vs[dc] = sch.now("V")
                    xbV[dc % 2] = sch.now("V")
                    chXS = f"xcs{gp}{dc % 2}"
                    sch.dma(chXS, lambda e, dc=dc, l0=l0, l1=l1:
                            e.dma_start(out=XC[dc, :, l0:l1], in_=xc_sl[dc % 2]),
                            waits=[("V", silu_Vs[dc])], eng="G")
                    xcs_tot.append((chXS, sch.now(chXS)))

                for ec in range(NEC):
                    gi = s * NEC + ec
                    par = ec % 2
                    chW = f"wwl{par}"
                    wv = wwin[:, par * KC * 128:(par + 1) * KC * 128]
                    w2 = list(wstage) if gi < 2 else [("T", wwin_freeT[gi - 2])]
                    sch.dma(chW, lambda e, d=wv, ec=ec: e.dma_start(out=d, in_=WINB[ec]), waits=w2)
                    wl_S = sch.now(chW)
                    psel = ps_bc[:, 0:512] if ec % 2 == 0 else ps_bc[:, 512:1024]
                    for kc in range(KC):
                        w3 = []
                        if kc == 0:
                            w3 = [(chW, wl_S), ("V", hr_V)]
                            if gi >= 2:
                                w3.append(psum_freeA[gi - 2])
                            if rep_evacA:
                                w3.append(("A", rep_evacA[-1]))
                            elif psbc_prev[0]:
                                w3.append(("A", psbc_prev[0]))
                        sch.emit("T", lambda e, kc=kc, wv=wv, psel=psel:
                                 e.matmul(psel[:, 0:Ts], wv[:, kc * 128:(kc + 1) * 128],
                                          hrms[:, kc * Ts:(kc + 1) * Ts],
                                          start=(kc == 0), stop=(kc == KC - 1)),
                                 waits=w3, inc=(1 if kc == KC - 1 else 0))
                    mm_T = sch.now("T")
                    wwin_freeT.append(mm_T)
                    if ec < NDC:
                        dc = ec
                        if dc >= 2:
                            emit_wx(dc - 2)
                        xb = xi_buf[dc % 2]
                        w4 = [("A", xbA[dc % 2])] if xbA[dc % 2] else []
                        sch.emit("V", lambda e, xb=xb, dc=dc:
                                 e.tensor_copy(xb[:, 0:DCV - 1],
                                               halo[:, dc * (DCV - 1):(dc + 1) * (DCV - 1)]),
                                 waits=w4, inc=1)
                        wev = [("T", mm_T)]
                        if xbV[dc % 2]:
                            wev.append(("V", xbV[dc % 2]))
                        sch.emit("A", lambda e, xb=xb, psel=psel:
                                 e.activation(out=xb[:, DCV - 1:DCV - 1 + Ts], in_=psel[:, 0:Ts],
                                              func=AF.Copy),
                                 waits=wev, inc=1)
                        psum_freeA.append(("A", sch.now("A")))
                        ev_A = sch.now("A")
                        xcv = xc_sl[dc % 2]
                        w5 = [("A", ev_A)]
                        if dc == 0:
                            if len(xcs_tot) >= 2:
                                w5.append(xcs_tot[-2])
                        elif xcs_tot:
                            w5.append(xcs_tot[-1])
                        if dc >= 2 or (dc == 1 and wxmm_T):
                            w5.append(("T", wxmm_T[-1]))
                        elif dc == 0 and len(wxmm_T) >= 2:
                            w5.append(("T", wxmm_T[-2]))
                        sch.emit("V", lambda e, xb=xb, xcv=xcv, dc=dc:
                                 e.tensor_scalar(xcv, xb[:, 0:Ts], cw_sb[:, dc * DCV:dc * DCV + 1],
                                                 cb_sb[:, dc:dc + 1], ALU.mult, ALU.add),
                                 waits=w5)
                        for k in range(1, DCV):
                            sch.emit("V", lambda e, xb=xb, xcv=xcv, dc=dc, k=k:
                                     e.scalar_tensor_tensor(xcv, xb[:, k:k + Ts],
                                                            cw_sb[:, dc * DCV + k:dc * DCV + k + 1],
                                                            xcv, ALU.mult, ALU.add))
                        sch.emit("V", lambda e, xb=xb, dc=dc:
                                 e.tensor_copy(halo[:, dc * (DCV - 1):(dc + 1) * (DCV - 1)],
                                               xb[:, Ts:Ts + DCV - 1]),
                                 inc=1)
                        conv_Vs[dc] = sch.now("V")
                        if dc >= 1:
                            emit_sig_store(dc - 1)
                        if dc == 2 and s + 1 < nsp:
                            emit_xload(s + 1)
                    else:
                        if ec == NDC:
                            emit_sig_store(NDC - 1)
                        dcz = ec - NDC
                        par2 = dcz % 2
                        szv = sz_sl[par2]
                        sg = xi_buf[par2]
                        chSS = f"szs{gp}{par2}"
                        wz = [("T", mm_T)]
                        if xbV[par2]:
                            wz.append(("V", xbV[par2]))
                        sch.emit("A", lambda e, sg=sg, psel=psel:
                                 e.activation(out=sg[:, 0:Ts], in_=psel[:, 0:Ts], func=AF.Exp,
                                              scale=-1.0),
                                 waits=wz)
                        sch.emit("A", lambda e, sg=sg:
                                 e.activation(out=sg[:, 0:Ts], in_=sg[:, 0:Ts], func=AF.Ln, bias=1.0))
                        sch.emit("A", lambda e, sg=sg:
                                 e.activation(out=sg[:, 0:Ts], in_=sg[:, 0:Ts], func=AF.Exp,
                                              scale=-1.0), inc=1)
                        xbA[par2] = sch.now("A")
                        wzm = [("A", xbA[par2])]
                        if len(szs_tot) >= 2:
                            wzm.append(szs_tot[-2])
                        sch.emit("V", lambda e, szv=szv, sg=sg, psel=psel:
                                 e.tensor_mul(szv, psel[:, 0:Ts], sg[:, 0:Ts]),
                                 waits=wzm, inc=1)
                        psum_freeA.append(("V", sch.now("V")))
                        xbV[par2] = sch.now("V")
                        sz_V = sch.now("V")
                        sch.dma(chSS, lambda e, dcz=dcz, l0=l0, l1=l1, szv=szv:
                                e.dma_start(out=SZ[dcz, :, l0:l1], in_=szv),
                                waits=[("V", sz_V)], eng="G")
                        szs_tot.append((chSS, sch.now(chSS)))
                    yield
                emit_wx(NDC - 2)
                emit_wx(NDC - 1)
                wx_T = sch.now("T")
                w7 = [("T", wx_T)]
                if bcs_tot:
                    w7.append(bcs_tot[-1])
                sch.emit("A", lambda e:
                         e.activation(out=stage96[0:64, 0:Ts], in_=ps_st[0:64, 0:Ts], func=AF.Copy),
                         waits=w7)
                sch.emit("A", lambda e: e.activation(out=stage96[64:96, 0:Ts], in_=ps_st[64:96, 0:Ts],
                                                     func=AF.Copy), inc=1)
                bc_A = sch.now("A")
                chBC = f"bcs{gp}"
                sch.dma(chBC, lambda e, l0=l0, l1=l1:
                        e.dma_start(out=DR[:, l0:l1], in_=stage96[0:64, 0:Ts]),
                        waits=[("A", bc_A)], eng="A")
                sch.dma(chBC, lambda e, l0=l0, l1=l1:
                        e.dma_start(out=BCB[0, :, l0:l1], in_=stage96[64:80, 0:Ts]),
                        eng="A")
                sch.dma(chBC, lambda e, l0=l0, l1=l1:
                        e.dma_start(out=BCB[1, :, l0:l1], in_=stage96[80:96, 0:Ts]),
                        eng="A")
                bcs_tot.append((chBC, sch.now(chBC)))
                drbar[0] = bc_A
                yield

        gen_bar = {}

        def record_bar(lp, gp):
            gen_bar[lp] = {
                "xc": [(f"xcs{gp}0", sch.now(f"xcs{gp}0")), (f"xcs{gp}1", sch.now(f"xcs{gp}1")),
                       (f"szs{gp}0", sch.now(f"szs{gp}0")), (f"szs{gp}1", sch.now(f"szs{gp}1"))],
                "bc": [(f"bcs{gp}", sch.now(f"bcs{gp}"))],
            }

        p2_ld_freeV = []
        ygs_tot = []
        psmm_freeA = []
        da_freeV = []
        tmm_T = []
        psy_freeV = []
        brow_T = []

        bevac = {}
        cevac = {}
        vend_at_lp = [0]

        def p2_rep_build(lp, n):
            g0, g1 = lp * Lp, (lp + 1) * Lp
            par = n % 2
            chB = f"brl{par}"
            rsA = browA[0:1, par * Lp:(par + 1) * Lp]
            rsC = browC[0:1, par * Lp:(par + 1) * Lp]
            w = list(gen_bar[lp]["bc"]) if n == 0 else []
            if len(brow_T) >= 2:
                w.append(("T", brow_T[-2]))
            sch.dma(chB, lambda e, d=rsA, n=n, g0=g0, g1=g1:
                    e.dma_start(out=d, in_=BCB[0, n:n + 1, g0:g1]), waits=w)
            sch.dma(chB, lambda e, d=rsC, n=n, g0=g0, g1=g1:
                    e.dma_start(out=d, in_=BCB[1, n:n + 1, g0:g1]))
            rl_S = sch.now(chB)
            for bi, (row, rep, evl) in enumerate(((rsA, brep, bevac), (rsC, crep, cevac))):
                w2 = [(chB, rl_S)]
                if rep_evacA:
                    w2.append(("A", rep_evacA[-1]))
                if psum_freeA:
                    w2.append(psum_freeA[-1])
                for hb in range(HB):
                    sch.emit("T", lambda e, d=row, hb=hb:
                             e.matmul(ps_bc[:, hb * 512:(hb + 1) * 512], ones_bf[:],
                                      d[0:1, hb * 512:(hb + 1) * 512], start=True, stop=True),
                             waits=(w2 if hb == 0 else []), inc=(1 if hb == HB - 1 else 0))
                rep_T = sch.now("T")
                wE = [("T", rep_T)]
                if lp > 0:
                    wE.append(("V", vend_at_lp[0]))
                sch.emit("A", lambda e, rep=rep, n=n:
                         e.activation(out=rep[:, n * Lp:(n + 1) * Lp], in_=ps_bc[:, 0:Lp], func=AF.Copy),
                         waits=wE, inc=1)
                rep_evacA.append(sch.now("A"))
                evl[n] = sch.now("A")
            brow_T.append(sch.now("T"))

        sp_As = {}
        dx_Vs = {}
        ld_Ss = {}
        wdt_last = {}
        drl_tot = {}

        def p2_prolog(lp, dc):
            g0, g1 = lp * Lp, (lp + 1) * Lp
            dsl = dc % 2
            chL = f"xcl{dsl}"
            dls = delta_sl[dsl]
            dxs = dx_sl[dsl]
            xcv = xc2_sl[dsl]
            szv = sz2_sl[dsl]
            gidx = lp * NDC + dc
            w = list(gen_bar[lp]["xc"]) if dc == 0 else []
            if gidx >= 2:
                w.append(("V", p2_ld_freeV[gidx - 2]))
            sch.dma(chL, lambda e, dc=dc, g0=g0, g1=g1, d=xcv:
                    e.dma_start(out=d, in_=XC[dc, :, g0:g1]), waits=w)
            sch.dma(chL, lambda e, dc=dc, g0=g0, g1=g1, d=szv:
                    e.dma_start(out=d, in_=SZ[dc, :, g0:g1]))
            ld_Ss[dc] = sch.now(chL)
            for hb in range(HB):
                psel = ps_mm if hb % 2 == 0 else ps_mm2
                w2 = []
                if dc == 0 and hb == 0:
                    w2.append(("drl", drl_tot[lp]))
                if len(psmm_freeA) >= 2:
                    w2.append(("A", psmm_freeA[-2]))
                sch.emit("T", lambda e, dc=dc, hb=hb, psel=psel, lp=lp:
                         e.matmul(psel[:, 0:512], wdt_sb[:, dc * 128:(dc + 1) * 128],
                                  dr_sb[:, (lp % 2) * Lp + hb * 512:(lp % 2) * Lp + (hb + 1) * 512],
                                  start=True, stop=True),
                         waits=w2, inc=1)
                dt_T = sch.now("T")
                sch.emit("A", lambda e, d=dls, hb=hb, dc=dc, psel=psel:
                         e.activation(out=d[:, hb * 512:(hb + 1) * 512], in_=psel[:, 0:512],
                                      func=AF.Exp, bias=bdt_sb[:, dc:dc + 1]),
                         waits=[("T", dt_T)])
                sch.emit("A", lambda e, d=dls, hb=hb:
                         e.activation(out=d[:, hb * 512:(hb + 1) * 512], in_=d[:, hb * 512:(hb + 1) * 512],
                                      func=AF.Ln, bias=1.0), inc=1)
                psmm_freeA.append(sch.now("A"))
            wdt_last[lp] = sch.now("T")
            sp_As[dc] = sch.now("A")
            sch.emit("V", lambda e, d=dxs, a=dls, b=xcv: e.tensor_mul(d, a, b),
                     waits=[("A", sp_As[dc]), (chL, ld_Ss[dc])], inc=1)
            dx_Vs[dc] = sch.now("V")

        def p2_lp(lp, gen=None, bud=None):
            g0, g1 = lp * Lp, (lp + 1) * Lp
            vend_at_lp[0] = sch.now("V")
            drv = dr_sb[:, (lp % 2) * Lp:(lp % 2 + 1) * Lp]
            wdr = list(gen_bar[lp]["bc"])
            if lp >= 2:
                wdr.append(("T", wdt_last[lp - 2]))
            sch.dma("drl", lambda e, d=drv, g0=g0, g1=g1:
                    e.dma_start(out=d, in_=DR[:, g0:g1]), waits=wdr)
            drl_tot[lp] = sch.now("drl")
            p2_rep_build(lp, 0)
            p2_rep_build(lp, 1)
            p2_prolog(lp, 0)

            for dci in range(NDC):
                dc = dci
                dsl = dci % 2
                dls = delta_sl[dsl]
                dxs = dx_sl[dsl]
                xcv = xc2_sl[dsl]
                szv = sz2_sl[dsl]
                ygv = yg2_sl[dsl]
                sp_A = sp_As[dc]
                dx_V = dx_Vs[dc]
                for n in range(NS):
                    it = len(da_freeV)
                    asl = it % 2
                    dav = da_sl[asl]
                    bxv = bx_sl[asl]
                    hv = h_sl[asl]
                    tv = t_sl[asl]
                    w3 = [("A", sp_A)] if n == 0 else []
                    if it == 0 and lp == 0 and li == 0:
                        w3.append(("V", negA_V))
                    if it >= 2:
                        w3.append(("V", da_freeV[it - 2]))
                    sch.emit("A", lambda e, d=dav, s_=dls, dc=dc, n=n:
                             e.activation(out=d, in_=s_, func=AF.Exp,
                                          scale=A_sb[:, dc * NS + n:dc * NS + n + 1]),
                             waits=w3, inc=1)
                    exp_A = sch.now("A")
                    w4 = [("V", dx_V)] if n == 0 else []
                    if dci == 0:
                        w4.append(("A", bevac[n]))
                    sch.emit("V", lambda e, d=bxv, a=dxs, n=n:
                             e.tensor_mul(d, a, brep[:, n * Lp:(n + 1) * Lp]), waits=w4)
                    sidx = dc * NS + n
                    sch.emit("V", lambda e, d=hv, a=dav, b=bxv, sidx=sidx:
                             e.tensor_tensor_scan(d, a, b, states[:, sidx:sidx + 1], ALU.mult, ALU.add),
                             waits=[("A", exp_A)])
                    sch.emit("V", lambda e, d=hv, sidx=sidx:
                             e.tensor_copy(states[:, sidx:sidx + 1], d[:, Lp - 1:Lp]))
                    w4b = [("A", cevac[n])] if dci == 0 else []
                    if it >= 2:
                        w4b.append(("T", tmm_T[it - 2]))
                    sch.emit("V", lambda e, d=tv, a=hv, n=n:
                             e.tensor_mul(d, a, crep[:, n * Lp:(n + 1) * Lp]), waits=w4b, inc=1)
                    da_freeV.append(sch.now("V"))
                    t_V = sch.now("V")
                    if gen is not None and bud[0] > 0 and dci >= 1 and n % 2 == 1:
                        next(gen)
                        bud[0] -= 1
                    for hb in range(HB):
                        w5 = [("V", t_V)] if hb == 0 else []
                        if n == 0 and hb == 0 and psy_freeV:
                            w5.append(("V", psy_freeV[-1]))
                        sch.emit("T", lambda e, d=tv, hb=hb, n=n:
                                 e.matmul(ps_y[:, hb * 512:(hb + 1) * 512], ident[:],
                                          d[:, hb * 512:(hb + 1) * 512],
                                          start=(n == 0), stop=(n == NS - 1)),
                                 waits=w5, inc=(1 if hb == HB - 1 else 0))
                    tmm_T.append(sch.now("T"))
                    if dci == 0 and n < NS - 2:
                        p2_rep_build(lp, n + 2)
                    if n == 4 and dci + 1 < NDC:
                        p2_prolog(lp, dci + 1)
                y_T = sch.now("T")
                sch.emit("V", lambda e, a=xcv, dc=dc:
                         e.scalar_tensor_tensor(t2_sl, a, dv_sb[:, dc:dc + 1], ps_y[:, 0:Lp],
                                                ALU.mult, ALU.add),
                         waits=[("T", y_T)], inc=1)
                psy_freeV.append(sch.now("V"))
                chY = f"ygs{dsl}"
                wyg = []
                if len(ygs_tot) >= 2:
                    wyg.append((chY, ygs_tot[-2]))
                sch.emit("V", lambda e, d=ygv, b=szv: e.tensor_mul(d, t2_sl, b), waits=wyg, inc=1)
                yg_V = sch.now("V")
                p2_ld_freeV.append(yg_V)
                sch.dma(chY, lambda e, dc=dc, g0=g0, g1=g1, d=ygv:
                        e.dma_start(out=YG[dc, :, g0:g1], in_=d),
                        waits=[("V", yg_V)], eng="G")
                ygs_tot.append(sch.now(chY))

        gen = p1_gen()
        for _ in range(2 * NPIECE):
            next(gen)
        record_bar(0, 1)
        for lp in range(nlp):
            if lp < nlp - 1:
                bud = [2 * NPIECE]
                p2_lp(lp, gen, bud)
                assert bud[0] == 0, (li, lp, bud[0])
                record_bar(lp + 1, lp % 2)
            else:
                p2_lp(lp)
        exhausted = False
        try:
            next(gen)
        except StopIteration:
            exhausted = True
        assert exhausted, "p1 generator not exhausted"
        psbc_prev[0] = rep_evacA[-1]
        p1_A = sch.now("A")
        p2_bar = [("ygs0", sch.now("ygs0")), ("ygs1", sch.now("ygs1"))]

        p3_freeV = []
        wowin_freeT = []
        p3psum_freeV = []
        xas_tots = []

        def p3_span(s):
            l0, l1 = s * Ts, (s + 1) * Ts
            xv = xsp3[s % 2]
            chX = f"xl{s % 2}"
            chYL = f"xcl{s % 2}"
            w = list(p2_bar) if s == 0 else [("V", p3_freeV[-1])]
            for dc in range(NDC):
                sch.dma(chYL, lambda e, dc=dc, l0=l0, l1=l1:
                        e.dma_start(out=ygsp[:, dc * Ts:(dc + 1) * Ts], in_=YG[dc, :, l0:l1]),
                        waits=(w if dc == 0 else []))
            ygl_S = sch.now(chYL)
            for kc in range(KC):
                wxl = []
                if s >= 2 and kc == 0:
                    wxl = [("V", p3_freeV[-2]), xas_tots[s - 2]]
                sch.dma(chX, lambda e, kc=kc, l0=l0, l1=l1, xv=xv, li=li:
                        e.dma_start(out=xv[:, kc * Ts:(kc + 1) * Ts], in_=x_src(li, kc, l0, l1)),
                        waits=wxl)
            xl_S = sch.now(chX)
            for kc in range(KC):
                gi = s * KC + kc
                par = kc % 2
                chW = f"wol{par}"
                wv = wowin[:, par * NDC * 128:(par + 1) * NDC * 128]
                w2 = list(wstage) if gi < 2 else [("T", wowin_freeT[gi - 2])]
                sch.dma(chW, lambda e, d=wv, kc=kc: e.dma_start(out=d, in_=WOUTB[kc]), waits=w2)
                wl_S = sch.now(chW)
                psel = ps_mm if par == 0 else ps_mm2
                for dc in range(NDC):
                    w3 = []
                    if dc == 0:
                        w3 = [(chYL, ygl_S), (chW, wl_S)]
                        if gi >= 2:
                            w3.append(("V", p3psum_freeV[gi - 2]))
                        else:
                            w3.append(("A", p1_A))
                    sch.emit("T", lambda e, dc=dc, wv=wv, psel=psel:
                             e.matmul(psel[:, 0:Ts], wv[:, dc * 128:(dc + 1) * 128],
                                      ygsp[:, dc * Ts:(dc + 1) * Ts],
                                      start=(dc == 0), stop=(dc == NDC - 1)),
                             waits=w3, inc=(1 if dc == NDC - 1 else 0))
                mm_T = sch.now("T")
                wowin_freeT.append(mm_T)
                sch.emit("V", lambda e, kc=kc, xv=xv, psel=psel:
                         e.tensor_add(xv[:, kc * Ts:(kc + 1) * Ts], xv[:, kc * Ts:(kc + 1) * Ts],
                                      psel[:, 0:Ts]),
                         waits=([("T", mm_T), (chX, xl_S)] if kc == 0 else [("T", mm_T)]), inc=1)
                p3psum_freeV.append(sch.now("V"))
                xa_V = sch.now("V")
                sch.dma(f"xas{s % 2}", lambda e, kc=kc, l0=l0, l1=l1, xv=xv:
                        e.dma_start(out=xA[kc, :, l0:l1], in_=xv[:, kc * Ts:(kc + 1) * Ts]),
                        waits=[("V", xa_V)], eng="G")
            p3_freeV.append(sch.now("V"))
            xas_tots.append((f"xas{s % 2}", sch.now(f"xas{s % 2}")))

        for s in range(nsp):
            p3_span(s)
        xas_tot = [sch.now("xas0"), sch.now("xas1")]

    p3end = [("xas0", xas_tot[0]), ("xas1", xas_tot[1])]
    wq_Alist = []
    for oc in range(KC):
        w = list(p3end) if oc == 0 else [("A", wq_Alist[-1])]
        sch.dma("misc", lambda e, oc=oc: e.dma_start(out=stg_f[:, 0:512], in_=fw[oc, :, 0:512]), waits=w)
        sch.dma("misc", lambda e, oc=oc: e.dma_start(out=stg_f[:, 512:1024], in_=fw[oc, :, 512:1024]))
        fw_S = sch.now("misc")
        sch.emit("V", lambda e: e.reduce_max(am_sb[:, 0:1], stg_f[:, 0:DM],
                                             axis=mybir.AxisListType.X, apply_absolute_value=True),
                 waits=[("misc", fw_S)])
        sch.emit("V", lambda e: e.tensor_scalar_max(am_sb[:, 0:1], am_sb[:, 0:1], FFN_EPS))
        sch.emit("V", lambda e: e.reciprocal(am_sb[:, 1:2], am_sb[:, 0:1]))
        sch.emit("V", lambda e, oc=oc: e.tensor_scalar_mul(wsc_sb[:, oc:oc + 1], am_sb[:, 0:1], 1.0 / QMAX))
        sch.emit("V", lambda e: e.tensor_scalar_mul(am_sb[:, 1:2], am_sb[:, 1:2], QMAX))
        sch.emit("V", lambda e: e.tensor_scalar(stg_f[:, 0:DM], stg_f[:, 0:DM],
                                                am_sb[:, 1:2], RND_M, ALU.mult, ALU.add))
        sch.emit("V", lambda e: e.tensor_scalar(stg_f[:, 0:DM], stg_f[:, 0:DM],
                                                RND_M, None, ALU.subtract), inc=1)
        wq_V = sch.now("V")
        for ic in range(KC):
            sch.emit("T", lambda e, ic=ic: e.transpose(ps_aux[:, 128:256], stg_f[:, ic * 128:(ic + 1) * 128],
                                                       identf[:]),
                     waits=([("V", wq_V)] if ic == 0 else [("A", wq_Alist[-1])]), inc=1)
            tr_T = sch.now("T")
            sch.emit("A", lambda e, oc=oc, ic=ic:
                     e.activation(out=wqt_sb[:, (ic * KC + oc) * 128:(ic * KC + oc + 1) * 128],
                                  in_=ps_aux[:, 128:256], func=AF.Copy),
                     waits=[("T", tr_T)] + ([("ygs0", sch.now("ygs0")), ("ygs1", sch.now("ygs1"))]
                                            if (oc == 0 and ic == 0) else []),
                     inc=1)
            wq_Alist.append(sch.now("A"))
    sch.dma("idl", lambda e: e.dma_start(out=fb_sb[:, 0:KC], in_=fb[:, :]))
    wq_A = sch.now("A")
    fb_S = sch.now("idl")

    ffn_freeV = []
    ffn_store = []

    def ln_span(src_sp, dst_sp, first_waits):
        for kc in range(KC):
            sch.emit("A", lambda e, kc=kc, s_=src_sp:
                     e.activation(out=ub_sp[:, kc * Ts:(kc + 1) * Ts], in_=s_[:, kc * Ts:(kc + 1) * Ts],
                                  func=AF.Copy),
                     waits=(list(first_waits) if kc == 0 else []), inc=(1 if kc == KC - 1 else 0))
        cast_A = sch.now("A")
        for kc in range(KC):
            sch.emit("T", lambda e, kc=kc:
                     e.matmul(ps_st[0:1, 0:Ts], ones_bc[:], ub_sp[:, kc * Ts:(kc + 1) * Ts],
                              start=(kc == 0), stop=(kc == KC - 1)),
                     waits=([("A", cast_A)] if kc == 0 else []), inc=(1 if kc == KC - 1 else 0))
        sum_T = sch.now("T")
        sch.emit("V", lambda e: e.tensor_scalar_mul(rowbuf[0:1, 0:Ts], ps_st[0:1, 0:Ts], 1.0 / DM),
                 waits=[("T", sum_T)], inc=1)
        m_V = sch.now("V")
        for kc in range(KC):
            sch.emit("A", lambda e, kc=kc, s_=src_sp:
                     e.activation(out=ub_sp[:, kc * Ts:(kc + 1) * Ts], in_=s_[:, kc * Ts:(kc + 1) * Ts],
                                  func=AF.Square),
                     waits=([("T", sum_T)] if kc == 0 else []), inc=(1 if kc == KC - 1 else 0))
        sq_A = sch.now("A")
        for kc in range(KC):
            sch.emit("T", lambda e, kc=kc:
                     e.matmul(ps_st[0:1, 0:Ts], ones_bc[:], ub_sp[:, kc * Ts:(kc + 1) * Ts],
                              start=(kc == 0), stop=(kc == KC - 1)),
                     waits=([("A", sq_A), ("V", m_V)] if kc == 0 else []), inc=(1 if kc == KC - 1 else 0))
        sq_T = sch.now("T")
        sch.emit("V", lambda e: e.tensor_scalar_mul(rowbuf[0:1, Ts:2 * Ts], ps_st[0:1, 0:Ts], 1.0 / DM),
                 waits=[("T", sq_T)])
        sch.emit("V", lambda e: e.tensor_mul(rowbuf[0:1, 2 * Ts:3 * Ts], rowbuf[0:1, 0:Ts], rowbuf[0:1, 0:Ts]))
        sch.emit("V", lambda e: e.tensor_sub(rowbuf[0:1, Ts:2 * Ts], rowbuf[0:1, Ts:2 * Ts],
                                             rowbuf[0:1, 2 * Ts:3 * Ts]))
        sch.emit("V", lambda e: e.tensor_scalar_add(rowbuf[0:1, Ts:2 * Ts], rowbuf[0:1, Ts:2 * Ts], LN_EPS),
                 inc=1)
        var_V = sch.now("V")
        sch.emit("A", lambda e: e.activation(out=rowbuf[0:1, Ts:2 * Ts], in_=rowbuf[0:1, Ts:2 * Ts], func=AF.Ln),
                 waits=[("V", var_V)])
        sch.emit("A", lambda e: e.activation(out=rowbuf[0:1, Ts:2 * Ts], in_=rowbuf[0:1, Ts:2 * Ts],
                                             func=AF.Exp, scale=-0.5), inc=1)
        rstd_A = sch.now("A")
        sch.emit("V", lambda e: e.tensor_mul(rowbuf[0:1, 2 * Ts:3 * Ts], rowbuf[0:1, 0:Ts],
                                             rowbuf[0:1, Ts:2 * Ts]),
                 waits=[("A", rstd_A)], inc=1)
        mr_V = sch.now("V")
        sch.emit("T", lambda e: e.matmul(ps_aux[:, 0:Ts], ones_f[:], rowbuf[0:1, Ts:2 * Ts],
                                         start=True, stop=True),
                 waits=[("V", mr_V)])
        sch.emit("T", lambda e: e.matmul(ps_bc[:, 0:Ts], ones_f[:], rowbuf[0:1, 2 * Ts:3 * Ts],
                                         start=True, stop=True), inc=1)
        bc_T = sch.now("T")
        for kc in range(KC):
            sch.emit("V", lambda e, kc=kc, s_=src_sp, d_=dst_sp:
                     e.tensor_mul(d_[:, kc * Ts:(kc + 1) * Ts], s_[:, kc * Ts:(kc + 1) * Ts], ps_aux[:, 0:Ts]),
                     waits=([("T", bc_T)] if kc == 0 else []))
            sch.emit("V", lambda e, kc=kc, d_=dst_sp:
                     e.tensor_sub(d_[:, kc * Ts:(kc + 1) * Ts], d_[:, kc * Ts:(kc + 1) * Ts], ps_bc[:, 0:Ts]),
                     inc=(1 if kc == KC - 1 else 0))
        return sch.now("V")

    def ffn_span(s):
        l0, l1 = s * Ts, (s + 1) * Ts
        if s == 0:
            w = [("A", wq_A), ("idl", fb_S)]
        else:
            w = [("V", ffn_freeV[-1]), ("fos", ffn_store[-1])]
        for kc in range(KC):
            sch.dma("ffl", lambda e, kc=kc, l0=l0, l1=l1:
                    e.dma_start(out=u_sp[:, kc * Ts:(kc + 1) * Ts], in_=x_in[kc * 128:(kc + 1) * 128, l0:l1]),
                    waits=(w if kc == 0 else []))
            sch.dma("ffl", lambda e, kc=kc, l0=l0, l1=l1:
                    e.dma_start(out=o1_sp[:, kc * Ts:(kc + 1) * Ts], in_=xA[kc, :, l0:l1]))
        ld_S = sch.now("ffl")
        for kc in range(KC):
            sch.emit("V", lambda e, kc=kc:
                     e.tensor_add(u_sp[:, kc * Ts:(kc + 1) * Ts], u_sp[:, kc * Ts:(kc + 1) * Ts],
                                  o1_sp[:, kc * Ts:(kc + 1) * Ts]),
                     waits=([("ffl", ld_S)] if kc == 0 else []), inc=(1 if kc == KC - 1 else 0))
        u_V = sch.now("V")
        ln1_V = ln_span(u_sp, o1_sp, [("V", u_V)])
        amred_V = None
        for tb in range(NT):
            for kc in range(KC):
                w_tr = []
                if kc == 0:
                    w_tr = [("V", ln1_V)] if tb == 0 else [("V", amred_V)]
                sch.emit("T", lambda e, kc=kc, tb=tb:
                         e.transpose(ps_bc[:, kc * 128:(kc + 1) * 128],
                                     o1_sp[:, kc * Ts + tb * 128:kc * Ts + (tb + 1) * 128], identf[:]),
                         waits=w_tr, inc=(1 if kc == KC - 1 else 0))
            tr_T = sch.now("T")
            sch.emit("V", lambda e, tb=tb:
                     e.reduce_max(am_sb[:, tb:tb + 1], ps_bc[:, 0:DM],
                                  axis=mybir.AxisListType.X, apply_absolute_value=True),
                     waits=[("T", tr_T)], inc=1)
            amred_V = sch.now("V")
        sch.emit("V", lambda e: e.tensor_scalar_max(am_sb[:, 0:NT], am_sb[:, 0:NT], FFN_EPS), inc=1)
        amc_V = sch.now("V")
        for c in range(NT):
            sch.emit("T", lambda e, c=c: e.transpose(ps_aux[0:1, c * 128:(c + 1) * 128],
                                                     am_sb[:, c:c + 1], identf[:]),
                     waits=([("V", amc_V)] if c == 0 else []), inc=(1 if c == NT - 1 else 0))
        amt_T = sch.now("T")
        sch.emit("A", lambda e: e.activation(out=rowbuf[0:1, 0:Ts], in_=ps_aux[0:1, 0:Ts],
                                             func=AF.Copy, scale=1.0 / QMAX),
                 waits=[("T", amt_T)], inc=1)
        srow_A = sch.now("A")
        sch.emit("V", lambda e: e.reciprocal(rowbuf[0:1, Ts:2 * Ts], rowbuf[0:1, 0:Ts]),
                 waits=[("A", srow_A)], inc=1)
        rrow_V = sch.now("V")
        sch.emit("T", lambda e: e.matmul(ps_aux[:, 0:Ts], ones_f[:], rowbuf[0:1, Ts:2 * Ts],
                                         start=True, stop=True),
                 waits=[("V", rrow_V)])
        sch.emit("T", lambda e: e.matmul(ps_bc[:, 0:Ts], ones_f[:], rowbuf[0:1, 0:Ts],
                                         start=True, stop=True), inc=1)
        scl_T = sch.now("T")
        sch.emit("A", lambda e: e.activation(out=srep_sp, in_=ps_bc[:, 0:Ts], func=AF.Copy),
                 waits=[("T", scl_T)], inc=1)
        srep_A = sch.now("A")
        for kc in range(KC):
            sch.emit("V", lambda e, kc=kc:
                     e.tensor_mul(u_sp[:, kc * Ts:(kc + 1) * Ts], o1_sp[:, kc * Ts:(kc + 1) * Ts],
                                  ps_aux[:, 0:Ts]),
                     waits=([("T", scl_T)] if kc == 0 else []))
            sch.emit("V", lambda e, kc=kc:
                     e.tensor_scalar(u_sp[:, kc * Ts:(kc + 1) * Ts], u_sp[:, kc * Ts:(kc + 1) * Ts],
                                     RND_M, RND_M, ALU.add, ALU.subtract))
            sch.emit("V", lambda e, kc=kc:
                     e.tensor_copy(xq_sp[:, kc * Ts:(kc + 1) * Ts], u_sp[:, kc * Ts:(kc + 1) * Ts]),
                     inc=(1 if kc == KC - 1 else 0))
        xq_V = sch.now("V")
        ffnq_freeV = []
        for oc in range(KC):
            psel = ps_mm if oc % 2 == 0 else ps_mm2
            for ic in range(KC):
                w_q = []
                if ic == 0:
                    w_q = [("V", xq_V)]
                    if oc >= 2:
                        w_q.append(("V", ffnq_freeV[oc - 2]))
                sch.emit("T", lambda e, oc=oc, ic=ic, psel=psel:
                         e.matmul(psel[:, 0:Ts], wqt_sb[:, (ic * KC + oc) * 128:(ic * KC + oc + 1) * 128],
                                  xq_sp[:, ic * Ts:(ic + 1) * Ts],
                                  start=(ic == 0), stop=(ic == KC - 1)),
                         waits=w_q, inc=(1 if ic == KC - 1 else 0))
            q_T = sch.now("T")
            sch.emit("V", lambda e, oc=oc, psel=psel:
                     e.tensor_mul(u_sp[:, oc * Ts:(oc + 1) * Ts], psel[:, 0:Ts], srep_sp),
                     waits=[("T", q_T)] + ([("A", srep_A)] if oc == 0 else []))
            sch.emit("V", lambda e, oc=oc:
                     e.tensor_scalar(u_sp[:, oc * Ts:(oc + 1) * Ts], u_sp[:, oc * Ts:(oc + 1) * Ts],
                                     wsc_sb[:, oc:oc + 1], fb_sb[:, oc:oc + 1], ALU.mult, ALU.add))
            sch.emit("V", lambda e, oc=oc:
                     e.tensor_add(u_sp[:, oc * Ts:(oc + 1) * Ts], u_sp[:, oc * Ts:(oc + 1) * Ts],
                                  o1_sp[:, oc * Ts:(oc + 1) * Ts]), inc=1)
            ffnq_freeV.append(sch.now("V"))
        res_V = sch.now("V")
        ln2_V = ln_span(u_sp, o1_sp, [("V", res_V)])
        for kc in range(KC):
            sch.dma("fos", lambda e, kc=kc, l0=l0, l1=l1:
                    e.dma_start(out=y_out[kc * 128:(kc + 1) * 128, l0:l1],
                                in_=o1_sp[:, kc * Ts:(kc + 1) * Ts]),
                    waits=([("V", ln2_V)] if kc == 0 else []), eng="G")
        ffn_freeV.append(ln2_V)
        ffn_store.append(sch.now("fos"))

    for s in range(nsp):
        ffn_span(s)
    out_S = sch.now("fos")
    sch.emit("V", lambda e: e.nop(), waits=[("fos", out_S)])

    block = ctx.enter_context(nc.Block())
    by_eng = {"T": [], "A": [], "V": [], "S": [], "G": []}
    for eng, waits, fn, inc_key, inc_amt in sch.items:
        by_eng[eng].append((waits, fn, inc_key, inc_amt))

    def runner(key):
        def _f(eng):
            for waits, fn, inc_key, inc_amt in by_eng[key]:
                for (wk, wv) in waits:
                    if wv > 0:
                        eng.wait_ge(SEMS[wk], wv)
                ins = fn(eng)
                if inc_key is not None:
                    assert ins is not None
                    ins.then_inc(SEMS[inc_key], inc_amt)
        return _f

    block.tensor(runner("T"))
    block.scalar(runner("A"))
    block.vector(runner("V"))
    block.sync(runner("S"))
    block.gpsimd(runner("G"))
    ctx.close()
    return nc


def pack_inputs(inp, b, L=8192, layers=4):
    import ml_dtypes
    x = np.ascontiguousarray(np.asarray(inp["L0_attn_x"])[b].T)
    W_in = np.asarray(inp["L0_attn_W_in"])[:layers]
    winT = np.ascontiguousarray(
        W_in.transpose(0, 2, 1).reshape(layers, KC, 128, NEC, 128).transpose(0, 3, 1, 2, 4))
    W_out = np.asarray(inp["L0_attn_W_out"])[:layers]
    woutT = np.ascontiguousarray(
        W_out.transpose(0, 2, 1).reshape(layers, NDC, 128, KC, 128).transpose(0, 3, 1, 2, 4))
    W_x = np.asarray(inp["L0_attn_W_x"])[:layers]
    wxT = np.ascontiguousarray(
        W_x.transpose(0, 2, 1).reshape(layers, NDC, 128, 96)
        .transpose(0, 2, 1, 3).reshape(layers, 128, NDC * 96))
    W_dt = np.asarray(inp["L0_attn_W_dt"])[:layers]
    wdtT = np.ascontiguousarray(W_dt.transpose(0, 2, 1))
    alog = np.ascontiguousarray(np.asarray(inp["L0_attn_A_log"])[:layers]
                                .reshape(layers, NDC, 128, NS).transpose(0, 2, 1, 3)
                                .reshape(layers, 128, NDC * NS))
    convw = np.ascontiguousarray(np.asarray(inp["L0_attn_conv_w"])[:layers]
                                 .reshape(layers, NDC, 128, DCV).transpose(0, 2, 1, 3)
                                 .reshape(layers, 128, NDC * DCV))
    convb = np.ascontiguousarray(np.asarray(inp["L0_attn_conv_b"])[:layers]
                                 .reshape(layers, NDC, 128).transpose(0, 2, 1))
    bdt = np.ascontiguousarray(np.asarray(inp["L0_attn_b_dt"])[:layers]
                               .reshape(layers, NDC, 128).transpose(0, 2, 1))
    dvec = np.ascontiguousarray(np.asarray(inp["L0_attn_D"])[:layers]
                                .reshape(layers, NDC, 128).transpose(0, 2, 1))
    normw = np.ascontiguousarray(np.asarray(inp["L0_attn_norm_w"])[:layers]
                                 .reshape(layers, KC, 128).transpose(0, 2, 1))
    fwt = np.ascontiguousarray(np.asarray(inp["L0_ffn_weight"]).reshape(KC, 128, DM))
    fbt = np.ascontiguousarray(np.asarray(inp["L0_ffn_bias"]).reshape(KC, 128).T)
    return {
        "x_in": x.astype(np.float32),
        "winT": winT.astype(np.float32), "woutT": woutT.astype(np.float32),
        "wxT": wxT.astype(np.float32), "wdtT": wdtT.astype(np.float32),
        "alog": alog.astype(np.float32), "convw": convw.astype(np.float32),
        "convb": convb.astype(np.float32), "bdt": bdt.astype(np.float32),
        "dvec": dvec.astype(np.float32), "normw": normw.astype(np.float32),
        "fw": fwt.astype(np.float32), "fb": fbt.astype(np.float32),
        "ident_in": np.eye(128, dtype=ml_dtypes.bfloat16),
        "identf_in": np.eye(128, dtype=np.float32),
    }


def kernel(**inputs):
    from concourse.bass_utils import run_bass_kernel_spmd
    L = 8192
    layers = 4
    nc = build_program(L=L, layers=layers, debug_internals=False)
    im = [pack_inputs(inputs, 0, L=L, layers=layers),
          pack_inputs(inputs, 1, L=L, layers=layers)]
    in_maps = [im[c] if c < 2 else im[0] for c in range(8)]
    res = run_bass_kernel_spmd(nc, in_maps, core_ids=list(range(8)))
    out0 = np.ascontiguousarray(res.results[0]["y_out"].T)
    out1 = np.ascontiguousarray(res.results[1]["y_out"].T)
    return np.stack([out0, out1]).astype(np.float32)


# version 13
# speedup vs baseline: 1.2436x; 1.0821x over previous
import numpy as np
from contextlib import ExitStack

import concourse.bass as bass
import concourse.mybir as mybir

F32 = mybir.dt.float32
BF16 = mybir.dt.bfloat16
AF = mybir.ActivationFunctionType
ALU = mybir.AluOpType

DM, DI, RK, NS, DCV = 1024, 2048, 64, 16, 4
KC, NDC, NEC = DM // 128, DI // 128, 2 * DI // 128
RMS_EPS = 1e-5
LN_EPS = 1e-5
FFN_EPS = 1e-8
QMAX = 127.0
RND_M = float(3 * 2 ** 22)

DMA_CH = ["idl", "misc", "stgl0", "stgl1", "stgs0", "stgs1", "xl0", "xl1",
          "wwl0", "wwl1", "wol0", "wol1", "xcl0", "xcl1", "brl0", "brl1",
          "xcs00", "xcs01", "xcs10", "xcs11", "szs00", "szs01", "szs10", "szs11",
          "ygs0", "ygs1", "bcs0", "bcs1", "drl",
          "xas0", "xas1", "ffl", "fos"]
ENG = ["T", "A", "V", "G"]


class Sched:
    def __init__(self):
        self.items = []
        self.cnt = {k: 0 for k in ENG + DMA_CH}

    SELF_SERIAL = ("A", "V", "G")

    def emit(self, eng, fn, waits=(), inc=0):
        waits = list(waits)
        if eng in self.SELF_SERIAL:
            waits.append((eng, self.cnt[eng]))
            inc = 1
        self.items.append((eng, tuple(waits), fn, eng if inc else None, inc))
        if inc:
            self.cnt[eng] += inc
        return self.cnt[eng]

    def dma(self, ch, fn, waits=(), eng="S"):
        assert ch in DMA_CH, ch
        self.items.append((eng, tuple(waits), fn, ch, 16))
        self.cnt[ch] += 16
        return self.cnt[ch]

    def now(self, key):
        return self.cnt[key]


def build_program(L=8192, layers=4, Ts=512, Lp=1024, debug_internals=False):
    nsp = L // Ts
    nlp = L // Lp
    HB = Lp // 512
    NT = Ts // 128
    nc = bass.Bass("TRN2", target_bir_lowering=False, debug=False)

    EI, EO, IN = "ExternalInput", "ExternalOutput", "Internal"
    x_in = nc.dram_tensor("x_in", [DM, L], F32, kind=EI).ap()
    winT = nc.dram_tensor("winT", [layers, NEC, KC, 128, 128], F32, kind=EI).ap()
    woutT = nc.dram_tensor("woutT", [layers, KC, NDC, 128, 128], F32, kind=EI).ap()
    wxT = nc.dram_tensor("wxT", [layers, 128, NDC * 96], F32, kind=EI).ap()
    wdtT = nc.dram_tensor("wdtT", [layers, 64, NDC * 128], F32, kind=EI).ap()
    alog = nc.dram_tensor("alog", [layers, 128, NDC * NS], F32, kind=EI).ap()
    convw = nc.dram_tensor("convw", [layers, 128, NDC * DCV], F32, kind=EI).ap()
    convb = nc.dram_tensor("convb", [layers, 128, NDC], F32, kind=EI).ap()
    bdt = nc.dram_tensor("bdt", [layers, 128, NDC], F32, kind=EI).ap()
    dvec = nc.dram_tensor("dvec", [layers, 128, NDC], F32, kind=EI).ap()
    normw = nc.dram_tensor("normw", [layers, 128, KC], F32, kind=EI).ap()
    fw = nc.dram_tensor("fw", [KC, 128, DM], F32, kind=EI).ap()
    fb = nc.dram_tensor("fb", [128, KC], F32, kind=EI).ap()
    ident_in = nc.dram_tensor("ident_in", [128, 128], BF16, kind=EI).ap()
    identf_in = nc.dram_tensor("identf_in", [128, 128], F32, kind=EI).ap()

    y_out = nc.dram_tensor("y_out", [DM, L], F32, kind=EO).ap()

    ikind = EO if debug_internals else IN
    xA = nc.dram_tensor("xA", [KC, 128, L], F32, kind=ikind).ap()
    XC = nc.dram_tensor("XCb", [NDC, 128, L], F32, kind=ikind).ap()
    SZ = nc.dram_tensor("SZb", [NDC, 128, L], BF16, kind=ikind).ap()
    YG = nc.dram_tensor("YGb", [NDC, 128, L], BF16, kind=ikind).ap()
    BCB = nc.dram_tensor("BCB", [2, NS, L], BF16, kind=ikind).ap()
    DR = nc.dram_tensor("DRb", [64, L], BF16, kind=IN).ap()
    WINB = nc.dram_tensor("WINB", [NEC, 128, KC * 128], BF16, kind=IN).ap()
    WOUTB = nc.dram_tensor("WOUTB", [KC, 128, NDC * 128], BF16, kind=IN).ap()

    ctx = ExitStack()
    sb = lambda name, shape, dt: ctx.enter_context(nc.sbuf_tensor(name, shape, dt))
    psb = lambda name, shape, dt: ctx.enter_context(nc.psum_tensor(name, shape, dt))

    dr_sb = sb("dr_sb", [64, 2 * Lp], BF16)
    stage96 = sb("stage96", [128, 512], BF16)
    browA = sb("browA", [1, 2 * Lp], BF16)
    browC = sb("browC", [1, 2 * Lp], BF16)
    states = sb("states", [128, NDC * NS], F32)
    halo = sb("halo", [128, NDC * (DCV - 1)], F32)
    wx_sb = sb("wx_sb", [128, NDC * 96], F32)
    wdt_sb = sb("wdt_sb", [64, NDC * 128], BF16)
    A_sb = sb("A_sb", [128, NDC * NS], F32)
    cw_sb = sb("cw_sb", [128, NDC * DCV], F32)
    cb_sb = sb("cb_sb", [128, NDC], F32)
    bdt_sb = sb("bdt_sb", [128, NDC], F32)
    dv_sb = sb("dv_sb", [128, NDC], F32)
    nw_sb = sb("nw_sb", [128, KC], F32)
    wsc_sb = sb("wsc_sb", [128, KC], F32)
    fb_sb = sb("fb_sb", [128, KC], F32)
    am_sb = sb("am_sb", [128, 2 * NT], F32)
    ident = sb("ident", [128, 128], BF16)
    identf = sb("identf", [128, 128], F32)
    ones_f = sb("ones_f", [1, 128], F32)
    ones_bf = sb("ones_bf", [1, 128], BF16)
    ones_bc = sb("ones_bc", [128, 1], BF16)
    wwin = sb("wwin", [128, 2 * KC * 128], BF16)
    wowin = sb("wowin", [128, 2 * NDC * 128], BF16)
    stg_f = sb("stg_f", [128, 2048], F32)
    stg_b = sb("stg_b", [128, 2048], BF16)
    rowbuf = sb("rowbuf", [1, 3 * Ts], F32)

    AFC = 13318
    ABC = 50176
    arena_f = sb("arena_f", [128, AFC], F32)
    arena_b = sb("arena_b", [128, ABC], BF16)

    def fsl(off, cols):
        assert off + cols <= AFC, (off, cols)
        return arena_f[:, off:off + cols]

    def bsl(off, cols):
        assert off + cols <= ABC, (off, cols)
        return arena_b[:, off:off + cols]

    delta_sl = [fsl(i * Lp, Lp) for i in range(2)]
    da_sl = [fsl(2 * Lp + i * Lp, Lp) for i in range(2)]
    xc2_sl = [fsl(4 * Lp + i * Lp, Lp) for i in range(2)]
    t2_sl = fsl(6 * Lp, Lp)
    _p1 = 7 * Lp
    xsp = fsl(_p1, KC * Ts)
    xi_buf = [fsl(_p1 + KC * Ts + i * (Ts + 3), Ts + 3) for i in range(2)]
    xc_sl = [fsl(_p1 + KC * Ts + 2 * (Ts + 3) + i * Ts, Ts) for i in range(2)]
    brep = bsl(0, NS * Lp)
    crep = bsl(NS * Lp, NS * Lp)
    _o = 2 * NS * Lp
    dx_sl = [bsl(_o + i * Lp, Lp) for i in range(2)]
    bx_sl = [bsl(_o + 2 * Lp + i * Lp, Lp) for i in range(2)]
    h_sl = [bsl(_o + 4 * Lp + i * Lp, Lp) for i in range(2)]
    t_sl = [bsl(_o + 6 * Lp + i * Lp, Lp) for i in range(2)]
    sz2_sl = [bsl(_o + 8 * Lp + i * Lp, Lp) for i in range(2)]
    yg2_sl = [bsl(_o + 10 * Lp + i * Lp, Lp) for i in range(2)]
    _p1b = _o + 12 * Lp
    hrms = bsl(_p1b, KC * Ts)
    sz_sl = [bsl(_p1b + KC * Ts + i * Ts, Ts) for i in range(2)]

    xsp3 = [fsl(i * KC * Ts, KC * Ts) for i in range(2)]
    ygsp = bsl(0, NDC * Ts)

    u_sp = fsl(0, KC * Ts)
    o1_sp = fsl(KC * Ts, KC * Ts)
    srep_sp = fsl(2 * KC * Ts, Ts)
    xq_sp = bsl(0, KC * Ts)
    ub_sp = bsl(KC * Ts, KC * Ts)
    wqt_sb = bsl(2 * KC * Ts, KC * KC * 128)

    ps_mm = psb("ps_mm", [128, 512], F32)
    ps_mm2 = psb("ps_mm2", [128, 512], F32)
    ps_y = psb("ps_y", [128, Lp], F32)
    ps_bc = psb("ps_bc", [128, 1024], F32)
    ps_st = psb("ps_st", [128, 512], F32)
    ps_aux = psb("ps_aux", [128, 512], F32)

    SEMS = {}
    for k in ENG + DMA_CH:
        SEMS[k] = ctx.enter_context(nc.semaphore("sem_" + k))

    sch = Sched()

    sch.dma("idl", lambda e: e.dma_start(out=ident[:], in_=ident_in[:, :]))
    sch.dma("idl", lambda e: e.dma_start(out=identf[:], in_=identf_in[:, :]))
    pre_S = sch.now("idl")
    sch.emit("V", lambda e: e.memset(ones_f[:], 1.0))
    sch.emit("V", lambda e: e.memset(ones_bf[:], 1.0))
    sch.emit("V", lambda e: e.memset(ones_bc[:], 1.0), inc=1)
    pre_V = sch.now("V")

    def x_src(li, kc, l0, l1):
        if li == 0:
            return x_in[kc * 128:(kc + 1) * 128, l0:l1]
        return xA[kc, :, l0:l1]

    xas_tot = [0, 0]
    psbc_prev = [0]

    for li in range(layers):
        sch.emit("V", lambda e: e.memset(states[:], 0.0))
        sch.emit("V", lambda e: e.memset(halo[:], 0.0), inc=1)
        zero_V = sch.now("V")

        for (dst, src) in ((cw_sb, convw[li]), (cb_sb, convb[li]), (bdt_sb, bdt[li]),
                           (dv_sb, dvec[li]), (nw_sb, normw[li]), (A_sb, alog[li])):
            sch.dma("misc", lambda e, d=dst, s=src: e.dma_start(out=d[:], in_=s))
        cA_S = sch.now("misc")
        sch.emit("A", lambda e: e.activation(out=A_sb[:], in_=A_sb[:], func=AF.Exp),
                 waits=[("misc", cA_S)], inc=1)
        expA_A = sch.now("A")
        sch.emit("V", lambda e: e.tensor_scalar_mul(A_sb[:], A_sb[:], -1.0),
                 waits=[("A", expA_A)], inc=1)
        negA_V = sch.now("V")
        wxw = [("V", negA_V), ("A", sch.now("A"))]
        if li > 0:
            wxw.append(("T", sch.now("T")))
        sch.dma("misc", lambda e, s=wxT[li]: e.dma_start(out=wx_sb[:], in_=s), waits=wxw)
        sch.dma("misc", lambda e, s=wdtT[li]: e.dma_start(out=stg_f[0:64, 0:NDC * 128], in_=s))
        cWd_S = sch.now("misc")
        sch.emit("A", lambda e: e.activation(out=wdt_sb[:], in_=stg_f[0:64, 0:NDC * 128], func=AF.Copy),
                 waits=[("misc", cWd_S)], inc=1)
        const_A = sch.now("A")

        stage_cast = []
        stage_store = {0: [], 1: []}
        units = []
        for ec in range(NEC):
            units.append((winT[li, ec], WINB[ec], 0))
        for kc in range(KC):
            units.append((woutT[li, kc, 0:8], WOUTB[kc], 0))
            units.append((woutT[li, kc, 8:16], WOUTB[kc], 1024))
        for u, (src, dstrow, dcol) in enumerate(units):
            par = u % 2
            chL, chS = f"stgl{par}", f"stgs{par}"
            f_s = stg_f[:, par * 1024:(par + 1) * 1024]
            b_s = stg_b[:, par * 1024:(par + 1) * 1024]
            w = [("A", const_A)] if u < 2 else [("A", stage_cast[u - 2])]
            sch.dma(chL, lambda e, d=f_s, src=src:
                    e.dma_start(out=d.rearrange("p (k c) -> p k c", k=8),
                                in_=src.rearrange("k p c -> p k c")),
                    waits=w)
            sL = sch.now(chL)
            w2 = [(chL, sL)]
            if len(stage_store[par]) >= 1:
                w2.append((chS, stage_store[par][-1]))
            sch.emit("A", lambda e, d=b_s, sf=f_s: e.activation(out=d, in_=sf, func=AF.Copy),
                     waits=w2, inc=1)
            stage_cast.append(sch.now("A"))
            sch.dma(chS, lambda e, d=dstrow, dcol=dcol, sbf=b_s:
                    e.dma_start(out=d[:, dcol:dcol + 1024], in_=sbf),
                    waits=[("A", stage_cast[-1])], eng="A")
            stage_store[par].append(sch.now(chS))
        wstage = [("stgs0", sch.now("stgs0")), ("stgs1", sch.now("stgs1"))]

        NPIECE = 40
        xsp_free = []
        wwin_freeT = []
        psum_freeA = []
        xbA = {0: 0, 1: 0}
        xbV = {0: 0, 1: 0}
        xcs_tot = []
        szs_tot = []
        bcs_tot = []
        wxmm_T = []
        xl_tot = {}
        hrv_prev = [0]
        drbar = [0]
        rep_evacA = []

        def emit_xload(s):
            l0, l1 = s * Ts, (s + 1) * Ts
            if s == 0:
                w = [("V", zero_V), ("xas0", xas_tot[0]), ("xas1", xas_tot[1])]
            else:
                w = [("V", xsp_free[s - 1])]
            for kc in range(KC):
                sch.dma("xl0", lambda e, kc=kc, l0=l0, l1=l1, li=li:
                        e.dma_start(out=xsp[:, kc * Ts:(kc + 1) * Ts], in_=x_src(li, kc, l0, l1)),
                        waits=(w if kc == 0 else []))
            xl_tot[s] = sch.now("xl0")

        def p1_gen():
            for s in range(nsp):
                l0, l1 = s * Ts, (s + 1) * Ts
                gp = (s // 2 - 1) % 2
                if s == 0:
                    emit_xload(0)
                w_sq = [("xl0", xl_tot[s])]
                if wwin_freeT:
                    w_sq.append(("T", wwin_freeT[-1]))
                for kc in range(KC):
                    sch.emit("A", lambda e, kc=kc:
                             e.activation(out=hrms[:, kc * Ts:(kc + 1) * Ts],
                                          in_=xsp[:, kc * Ts:(kc + 1) * Ts], func=AF.Square),
                             waits=(w_sq if kc == 0 else []), inc=(1 if kc == KC - 1 else 0))
                sq_A = sch.now("A")
                yield
                w_st = [("A", sq_A), ("V", pre_V)]
                if hrv_prev[0]:
                    w_st.append(("V", hrv_prev[0]))
                for kc in range(KC):
                    sch.emit("T", lambda e, kc=kc:
                             e.matmul(ps_aux[0:1, 0:Ts], ones_bc[:], hrms[:, kc * Ts:(kc + 1) * Ts],
                                      start=(kc == 0), stop=(kc == KC - 1)),
                             waits=(w_st if kc == 0 else []), inc=(1 if kc == KC - 1 else 0))
                st_T = sch.now("T")
                yield
                sch.emit("V", lambda e: e.tensor_scalar(rowbuf[0:1, 0:Ts], ps_aux[0:1, 0:Ts],
                                                        1.0 / DM, RMS_EPS, ALU.mult, ALU.add),
                         waits=[("T", st_T)], inc=1)
                ms_V = sch.now("V")
                sch.emit("A", lambda e: e.activation(out=rowbuf[0:1, Ts:2 * Ts], in_=rowbuf[0:1, 0:Ts],
                                                     func=AF.Ln),
                         waits=[("V", ms_V)])
                sch.emit("A", lambda e: e.activation(out=rowbuf[0:1, Ts:2 * Ts], in_=rowbuf[0:1, Ts:2 * Ts],
                                                     func=AF.Exp, scale=-0.5), inc=1)
                rstd_A = sch.now("A")
                yield
                sch.emit("T", lambda e: e.matmul(ps_aux[:, 0:Ts], ones_f[:], rowbuf[0:1, Ts:2 * Ts],
                                                 start=True, stop=True),
                         waits=[("A", rstd_A), ("idl", pre_S)], inc=1)
                bc_T = sch.now("T")
                yield
                for kc in range(KC):
                    sch.emit("V", lambda e, kc=kc:
                             e.scalar_tensor_tensor(hrms[:, kc * Ts:(kc + 1) * Ts],
                                                    xsp[:, kc * Ts:(kc + 1) * Ts],
                                                    nw_sb[:, kc:kc + 1], ps_aux[:, 0:Ts],
                                                    ALU.mult, ALU.mult),
                             waits=([("T", bc_T)] if kc == 0 else []), inc=(1 if kc == KC - 1 else 0))
                hr_V = sch.now("V")
                hrv_prev[0] = hr_V
                xsp_free.append(hr_V)
                yield
                silu_Vs = {}
                conv_Vs = {}

                def emit_wx(dc):
                    wmx = [("V", silu_Vs[dc])]
                    if dc == 0:
                        wmx.append(("A", const_A))
                        if drbar[0]:
                            wmx.append(drbar[0])
                    sch.emit("T", lambda e, dc=dc:
                             e.matmul(ps_st[0:96, 0:Ts], wx_sb[:, dc * 96:(dc + 1) * 96],
                                      xc_sl[dc % 2][:, 0:Ts],
                                      start=(dc == 0), stop=(dc == NDC - 1)),
                             waits=wmx, inc=1)
                    wxmm_T.append(sch.now("T"))

                def emit_sig_store(dc):
                    xcv = xc_sl[dc % 2]
                    sg = xi_buf[dc % 2]
                    sch.emit("A", lambda e, xcv=xcv, sg=sg:
                             e.activation(out=sg[:, 0:Ts], in_=xcv, func=AF.Exp, scale=-1.0),
                             waits=[("V", conv_Vs[dc])])
                    sch.emit("A", lambda e, sg=sg:
                             e.activation(out=sg[:, 0:Ts], in_=sg[:, 0:Ts], func=AF.Ln, bias=1.0))
                    sch.emit("A", lambda e, sg=sg:
                             e.activation(out=sg[:, 0:Ts], in_=sg[:, 0:Ts], func=AF.Exp, scale=-1.0),
                             inc=1)
                    xbA[dc % 2] = sch.now("A")
                    sch.emit("V", lambda e, xcv=xcv, sg=sg: e.tensor_mul(xcv, xcv, sg[:, 0:Ts]),
                             waits=[("A", xbA[dc % 2])], inc=1)
                    silu_Vs[dc] = sch.now("V")
                    xbV[dc % 2] = sch.now("V")
                    chXS = f"xcs{gp}{dc % 2}"
                    sch.dma(chXS, lambda e, dc=dc, l0=l0, l1=l1:
                            e.dma_start(out=XC[dc, :, l0:l1], in_=xc_sl[dc % 2]),
                            waits=[("V", silu_Vs[dc])], eng="G")
                    xcs_tot.append((chXS, sch.now(chXS)))

                for ec in range(NEC):
                    gi = s * NEC + ec
                    par = ec % 2
                    chW = f"wwl{par}"
                    wv = wwin[:, par * KC * 128:(par + 1) * KC * 128]
                    w2 = list(wstage) if gi < 2 else [("T", wwin_freeT[gi - 2])]
                    sch.dma(chW, lambda e, d=wv, ec=ec: e.dma_start(out=d, in_=WINB[ec]), waits=w2)
                    wl_S = sch.now(chW)
                    psel = ps_bc[:, 0:512] if ec % 2 == 0 else ps_bc[:, 512:1024]
                    for kc in range(KC):
                        w3 = []
                        if kc == 0:
                            w3 = [(chW, wl_S), ("V", hr_V)]
                            if gi >= 2:
                                w3.append(psum_freeA[gi - 2])
                            if rep_evacA:
                                w3.append(("A", rep_evacA[-1]))
                            elif psbc_prev[0]:
                                w3.append(("A", psbc_prev[0]))
                        sch.emit("T", lambda e, kc=kc, wv=wv, psel=psel:
                                 e.matmul(psel[:, 0:Ts], wv[:, kc * 128:(kc + 1) * 128],
                                          hrms[:, kc * Ts:(kc + 1) * Ts],
                                          start=(kc == 0), stop=(kc == KC - 1)),
                                 waits=w3, inc=(1 if kc == KC - 1 else 0))
                    mm_T = sch.now("T")
                    wwin_freeT.append(mm_T)
                    if ec < NDC:
                        dc = ec
                        if dc >= 2:
                            emit_wx(dc - 2)
                        xb = xi_buf[dc % 2]
                        w4 = [("A", xbA[dc % 2])] if xbA[dc % 2] else []
                        sch.emit("V", lambda e, xb=xb, dc=dc:
                                 e.tensor_copy(xb[:, 0:DCV - 1],
                                               halo[:, dc * (DCV - 1):(dc + 1) * (DCV - 1)]),
                                 waits=w4, inc=1)
                        wev = [("T", mm_T)]
                        if xbV[dc % 2]:
                            wev.append(("V", xbV[dc % 2]))
                        sch.emit("A", lambda e, xb=xb, psel=psel:
                                 e.activation(out=xb[:, DCV - 1:DCV - 1 + Ts], in_=psel[:, 0:Ts],
                                              func=AF.Copy),
                                 waits=wev, inc=1)
                        psum_freeA.append(("A", sch.now("A")))
                        ev_A = sch.now("A")
                        xcv = xc_sl[dc % 2]
                        w5 = [("A", ev_A)]
                        if dc == 0:
                            if len(xcs_tot) >= 2:
                                w5.append(xcs_tot[-2])
                        elif xcs_tot:
                            w5.append(xcs_tot[-1])
                        if dc >= 2 or (dc == 1 and wxmm_T):
                            w5.append(("T", wxmm_T[-1]))
                        elif dc == 0 and len(wxmm_T) >= 2:
                            w5.append(("T", wxmm_T[-2]))
                        sch.emit("V", lambda e, xb=xb, xcv=xcv, dc=dc:
                                 e.tensor_scalar(xcv, xb[:, 0:Ts], cw_sb[:, dc * DCV:dc * DCV + 1],
                                                 cb_sb[:, dc:dc + 1], ALU.mult, ALU.add),
                                 waits=w5)
                        for k in range(1, DCV):
                            sch.emit("V", lambda e, xb=xb, xcv=xcv, dc=dc, k=k:
                                     e.scalar_tensor_tensor(xcv, xb[:, k:k + Ts],
                                                            cw_sb[:, dc * DCV + k:dc * DCV + k + 1],
                                                            xcv, ALU.mult, ALU.add))
                        sch.emit("V", lambda e, xb=xb, dc=dc:
                                 e.tensor_copy(halo[:, dc * (DCV - 1):(dc + 1) * (DCV - 1)],
                                               xb[:, Ts:Ts + DCV - 1]),
                                 inc=1)
                        conv_Vs[dc] = sch.now("V")
                        if dc >= 1:
                            emit_sig_store(dc - 1)
                        if dc == 2 and s + 1 < nsp:
                            emit_xload(s + 1)
                    else:
                        if ec == NDC:
                            emit_sig_store(NDC - 1)
                        dcz = ec - NDC
                        par2 = dcz % 2
                        szv = sz_sl[par2]
                        sg = xi_buf[par2]
                        chSS = f"szs{gp}{par2}"
                        wz = [("T", mm_T)]
                        if xbV[par2]:
                            wz.append(("V", xbV[par2]))
                        sch.emit("A", lambda e, sg=sg, psel=psel:
                                 e.activation(out=sg[:, 0:Ts], in_=psel[:, 0:Ts], func=AF.Exp,
                                              scale=-1.0),
                                 waits=wz)
                        sch.emit("A", lambda e, sg=sg:
                                 e.activation(out=sg[:, 0:Ts], in_=sg[:, 0:Ts], func=AF.Ln, bias=1.0))
                        sch.emit("A", lambda e, sg=sg:
                                 e.activation(out=sg[:, 0:Ts], in_=sg[:, 0:Ts], func=AF.Exp,
                                              scale=-1.0), inc=1)
                        xbA[par2] = sch.now("A")
                        wzm = [("A", xbA[par2])]
                        if len(szs_tot) >= 2:
                            wzm.append(szs_tot[-2])
                        sch.emit("V", lambda e, szv=szv, sg=sg, psel=psel:
                                 e.tensor_mul(szv, psel[:, 0:Ts], sg[:, 0:Ts]),
                                 waits=wzm, inc=1)
                        psum_freeA.append(("V", sch.now("V")))
                        xbV[par2] = sch.now("V")
                        sz_V = sch.now("V")
                        sch.dma(chSS, lambda e, dcz=dcz, l0=l0, l1=l1, szv=szv:
                                e.dma_start(out=SZ[dcz, :, l0:l1], in_=szv),
                                waits=[("V", sz_V)], eng="G")
                        szs_tot.append((chSS, sch.now(chSS)))
                    yield
                emit_wx(NDC - 2)
                emit_wx(NDC - 1)
                wx_T = sch.now("T")
                yield
                yield
                w7 = [("T", wx_T)]
                if bcs_tot:
                    w7.append(bcs_tot[-1])
                sch.emit("A", lambda e:
                         e.activation(out=stage96[0:96, 0:Ts], in_=ps_st[0:96, 0:Ts], func=AF.Copy),
                         waits=w7, inc=1)
                bc_A = sch.now("A")
                chBC = f"bcs{gp}"
                sch.dma(chBC, lambda e, l0=l0, l1=l1:
                        e.dma_start(out=DR[:, l0:l1], in_=stage96[0:64, 0:Ts]),
                        waits=[("A", bc_A)], eng="A")
                sch.dma(chBC, lambda e, l0=l0, l1=l1:
                        e.dma_start(out=BCB[0, :, l0:l1], in_=stage96[64:80, 0:Ts]),
                        eng="A")
                sch.dma(chBC, lambda e, l0=l0, l1=l1:
                        e.dma_start(out=BCB[1, :, l0:l1], in_=stage96[80:96, 0:Ts]),
                        eng="A")
                bcs_tot.append((chBC, sch.now(chBC)))
                drbar[0] = ("A", bc_A)
                yield

        gen_bar = {}

        def record_bar(lp, gp):
            gen_bar[lp] = {
                "xc": [(f"xcs{gp}0", sch.now(f"xcs{gp}0")), (f"xcs{gp}1", sch.now(f"xcs{gp}1")),
                       (f"szs{gp}0", sch.now(f"szs{gp}0")), (f"szs{gp}1", sch.now(f"szs{gp}1"))],
                "bc": [(f"bcs{gp}", sch.now(f"bcs{gp}"))],
            }

        p2_ld_freeV = []
        ygs_tot = []
        psmm_freeA = []
        da_freeV = []
        tmm_T = []
        psy_freeV = []
        brow_T = []

        bevac = {}
        cevac = {}
        vend_at_lp = [0]

        def p2_rep_build(lp, n):
            g0, g1 = lp * Lp, (lp + 1) * Lp
            par = n % 2
            chB = f"brl{par}"
            rsA = browA[0:1, par * Lp:(par + 1) * Lp]
            rsC = browC[0:1, par * Lp:(par + 1) * Lp]
            w = list(gen_bar[lp]["bc"]) if n == 0 else []
            if len(brow_T) >= 2:
                w.append(("T", brow_T[-2]))
            sch.dma(chB, lambda e, d=rsA, n=n, g0=g0, g1=g1:
                    e.dma_start(out=d, in_=BCB[0, n:n + 1, g0:g1]), waits=w)
            sch.dma(chB, lambda e, d=rsC, n=n, g0=g0, g1=g1:
                    e.dma_start(out=d, in_=BCB[1, n:n + 1, g0:g1]))
            rl_S = sch.now(chB)
            for bi, (row, rep, evl) in enumerate(((rsA, brep, bevac), (rsC, crep, cevac))):
                w2 = [(chB, rl_S)]
                if rep_evacA:
                    w2.append(("A", rep_evacA[-1]))
                if psum_freeA:
                    w2.append(psum_freeA[-1])
                for hb in range(HB):
                    sch.emit("T", lambda e, d=row, hb=hb:
                             e.matmul(ps_bc[:, hb * 512:(hb + 1) * 512], ones_bf[:],
                                      d[0:1, hb * 512:(hb + 1) * 512], start=True, stop=True),
                             waits=(w2 if hb == 0 else []), inc=(1 if hb == HB - 1 else 0))
                rep_T = sch.now("T")
                wE = [("T", rep_T)]
                if lp > 0:
                    wE.append(("V", vend_at_lp[0]))
                sch.emit("A", lambda e, rep=rep, n=n:
                         e.activation(out=rep[:, n * Lp:(n + 1) * Lp], in_=ps_bc[:, 0:Lp], func=AF.Copy),
                         waits=wE, inc=1)
                rep_evacA.append(sch.now("A"))
                evl[n] = sch.now("A")
            brow_T.append(sch.now("T"))

        sp_As = {}
        dx_Vs = {}
        ld_Ss = {}
        wdt_last = {}
        drl_tot = {}

        def p2_prolog(lp, dc):
            g0, g1 = lp * Lp, (lp + 1) * Lp
            dsl = dc % 2
            chL = f"xcl{dsl}"
            dls = delta_sl[dsl]
            dxs = dx_sl[dsl]
            xcv = xc2_sl[dsl]
            szv = sz2_sl[dsl]
            gidx = lp * NDC + dc
            w = list(gen_bar[lp]["xc"]) if dc == 0 else []
            if gidx >= 2:
                w.append(("V", p2_ld_freeV[gidx - 2]))
            sch.dma(chL, lambda e, dc=dc, g0=g0, g1=g1, d=xcv:
                    e.dma_start(out=d, in_=XC[dc, :, g0:g1]), waits=w)
            sch.dma(chL, lambda e, dc=dc, g0=g0, g1=g1, d=szv:
                    e.dma_start(out=d, in_=SZ[dc, :, g0:g1]))
            ld_Ss[dc] = sch.now(chL)
            for hb in range(HB):
                psel = ps_mm if hb % 2 == 0 else ps_mm2
                w2 = []
                if dc == 0 and hb == 0:
                    w2.append(("drl", drl_tot[lp]))
                if len(psmm_freeA) >= 2:
                    w2.append(("A", psmm_freeA[-2]))
                sch.emit("T", lambda e, dc=dc, hb=hb, psel=psel, lp=lp:
                         e.matmul(psel[:, 0:512], wdt_sb[:, dc * 128:(dc + 1) * 128],
                                  dr_sb[:, (lp % 2) * Lp + hb * 512:(lp % 2) * Lp + (hb + 1) * 512],
                                  start=True, stop=True),
                         waits=w2, inc=1)
                dt_T = sch.now("T")
                sch.emit("A", lambda e, d=dls, hb=hb, dc=dc, psel=psel:
                         e.activation(out=d[:, hb * 512:(hb + 1) * 512], in_=psel[:, 0:512],
                                      func=AF.Exp, bias=bdt_sb[:, dc:dc + 1]),
                         waits=[("T", dt_T)])
                sch.emit("A", lambda e, d=dls, hb=hb:
                         e.activation(out=d[:, hb * 512:(hb + 1) * 512], in_=d[:, hb * 512:(hb + 1) * 512],
                                      func=AF.Ln, bias=1.0), inc=1)
                psmm_freeA.append(sch.now("A"))
            wdt_last[lp] = sch.now("T")
            sp_As[dc] = sch.now("A")
            sch.emit("V", lambda e, d=dxs, a=dls, b=xcv: e.tensor_mul(d, a, b),
                     waits=[("A", sp_As[dc]), (chL, ld_Ss[dc])], inc=1)
            dx_Vs[dc] = sch.now("V")

        def p2_lp(lp, gen=None, bud=None):
            g0, g1 = lp * Lp, (lp + 1) * Lp
            vend_at_lp[0] = sch.now("V")
            drv = dr_sb[:, (lp % 2) * Lp:(lp % 2 + 1) * Lp]
            wdr = list(gen_bar[lp]["bc"])
            if lp >= 2:
                wdr.append(("T", wdt_last[lp - 2]))
            sch.dma("drl", lambda e, d=drv, g0=g0, g1=g1:
                    e.dma_start(out=d, in_=DR[:, g0:g1]), waits=wdr)
            drl_tot[lp] = sch.now("drl")
            p2_rep_build(lp, 0)
            p2_rep_build(lp, 1)
            p2_prolog(lp, 0)

            for dci in range(NDC):
                dc = dci
                dsl = dci % 2
                dls = delta_sl[dsl]
                dxs = dx_sl[dsl]
                xcv = xc2_sl[dsl]
                szv = sz2_sl[dsl]
                ygv = yg2_sl[dsl]
                sp_A = sp_As[dc]
                dx_V = dx_Vs[dc]
                for n in range(NS):
                    it = len(da_freeV)
                    asl = it % 2
                    dav = da_sl[asl]
                    bxv = bx_sl[asl]
                    hv = h_sl[asl]
                    tv = t_sl[asl]
                    w3 = [("A", sp_A)] if n == 0 else []
                    if it == 0 and lp == 0 and li == 0:
                        w3.append(("V", negA_V))
                    if it >= 2:
                        w3.append(("V", da_freeV[it - 2]))
                    sch.emit("A", lambda e, d=dav, s_=dls, dc=dc, n=n:
                             e.activation(out=d, in_=s_, func=AF.Exp,
                                          scale=A_sb[:, dc * NS + n:dc * NS + n + 1]),
                             waits=w3, inc=1)
                    exp_A = sch.now("A")
                    w4 = [("V", dx_V)] if n == 0 else []
                    if dci == 0:
                        w4.append(("A", bevac[n]))
                    sch.emit("V", lambda e, d=bxv, a=dxs, n=n:
                             e.tensor_mul(d, a, brep[:, n * Lp:(n + 1) * Lp]), waits=w4)
                    sidx = dc * NS + n
                    sch.emit("V", lambda e, d=hv, a=dav, b=bxv, sidx=sidx:
                             e.tensor_tensor_scan(d, a, b, states[:, sidx:sidx + 1], ALU.mult, ALU.add),
                             waits=[("A", exp_A)])
                    sch.emit("V", lambda e, d=hv, sidx=sidx:
                             e.tensor_copy(states[:, sidx:sidx + 1], d[:, Lp - 1:Lp]))
                    w4b = [("A", cevac[n])] if dci == 0 else []
                    if it >= 2:
                        w4b.append(("T", tmm_T[it - 2]))
                    sch.emit("V", lambda e, d=tv, a=hv, n=n:
                             e.tensor_mul(d, a, crep[:, n * Lp:(n + 1) * Lp]), waits=w4b, inc=1)
                    da_freeV.append(sch.now("V"))
                    t_V = sch.now("V")
                    if gen is not None and bud[0] > 0 and dci >= 1 and n % 2 == 1:
                        next(gen)
                        bud[0] -= 1
                    if dci == 0 and n < NS - 2:
                        p2_rep_build(lp, n + 2)
                    if n == 4 and dci + 1 < NDC:
                        p2_prolog(lp, dci + 1)
                    for hb in range(HB):
                        w5 = [("V", t_V)] if hb == 0 else []
                        if n == 0 and hb == 0 and psy_freeV:
                            w5.append(("V", psy_freeV[-1]))
                        sch.emit("T", lambda e, d=tv, hb=hb, n=n:
                                 e.matmul(ps_y[:, hb * 512:(hb + 1) * 512], ident[:],
                                          d[:, hb * 512:(hb + 1) * 512],
                                          start=(n == 0), stop=(n == NS - 1)),
                                 waits=w5, inc=(1 if hb == HB - 1 else 0))
                    tmm_T.append(sch.now("T"))
                y_T = sch.now("T")
                sch.emit("V", lambda e, a=xcv, dc=dc:
                         e.scalar_tensor_tensor(t2_sl, a, dv_sb[:, dc:dc + 1], ps_y[:, 0:Lp],
                                                ALU.mult, ALU.add),
                         waits=[("T", y_T)], inc=1)
                psy_freeV.append(sch.now("V"))
                chY = f"ygs{dsl}"
                wyg = []
                if len(ygs_tot) >= 2:
                    wyg.append((chY, ygs_tot[-2]))
                sch.emit("V", lambda e, d=ygv, b=szv: e.tensor_mul(d, t2_sl, b), waits=wyg, inc=1)
                yg_V = sch.now("V")
                p2_ld_freeV.append(yg_V)
                sch.dma(chY, lambda e, dc=dc, g0=g0, g1=g1, d=ygv:
                        e.dma_start(out=YG[dc, :, g0:g1], in_=d),
                        waits=[("V", yg_V)], eng="G")
                ygs_tot.append(sch.now(chY))

        gen = p1_gen()
        for _ in range(2 * NPIECE):
            next(gen)
        record_bar(0, 1)
        for lp in range(nlp):
            if lp < nlp - 1:
                bud = [2 * NPIECE]
                p2_lp(lp, gen, bud)
                assert bud[0] == 0, (li, lp, bud[0])
                record_bar(lp + 1, lp % 2)
            else:
                p2_lp(lp)
        exhausted = False
        try:
            next(gen)
        except StopIteration:
            exhausted = True
        assert exhausted, "p1 generator not exhausted"
        psbc_prev[0] = rep_evacA[-1]
        p1_A = sch.now("A")
        p2_bar = [("ygs0", sch.now("ygs0")), ("ygs1", sch.now("ygs1"))]

        p3_freeV = []
        wowin_freeT = []
        p3psum_freeV = []
        xas_tots = []

        def p3_span(s):
            l0, l1 = s * Ts, (s + 1) * Ts
            xv = xsp3[s % 2]
            chX = f"xl{s % 2}"
            chYL = f"xcl{s % 2}"
            w = list(p2_bar) if s == 0 else [("V", p3_freeV[-1])]
            for dc in range(NDC):
                sch.dma(chYL, lambda e, dc=dc, l0=l0, l1=l1:
                        e.dma_start(out=ygsp[:, dc * Ts:(dc + 1) * Ts], in_=YG[dc, :, l0:l1]),
                        waits=(w if dc == 0 else []))
            ygl_S = sch.now(chYL)
            for kc in range(KC):
                wxl = []
                if s >= 2 and kc == 0:
                    wxl = [("V", p3_freeV[-2]), xas_tots[s - 2]]
                sch.dma(chX, lambda e, kc=kc, l0=l0, l1=l1, xv=xv, li=li:
                        e.dma_start(out=xv[:, kc * Ts:(kc + 1) * Ts], in_=x_src(li, kc, l0, l1)),
                        waits=wxl)
            xl_S = sch.now(chX)
            for kc in range(KC):
                gi = s * KC + kc
                par = kc % 2
                chW = f"wol{par}"
                wv = wowin[:, par * NDC * 128:(par + 1) * NDC * 128]
                w2 = list(wstage) if gi < 2 else [("T", wowin_freeT[gi - 2])]
                sch.dma(chW, lambda e, d=wv, kc=kc: e.dma_start(out=d, in_=WOUTB[kc]), waits=w2)
                wl_S = sch.now(chW)
                psel = ps_mm if par == 0 else ps_mm2
                for dc in range(NDC):
                    w3 = []
                    if dc == 0:
                        w3 = [(chYL, ygl_S), (chW, wl_S)]
                        if gi >= 2:
                            w3.append(("V", p3psum_freeV[gi - 2]))
                        else:
                            w3.append(("A", p1_A))
                    sch.emit("T", lambda e, dc=dc, wv=wv, psel=psel:
                             e.matmul(psel[:, 0:Ts], wv[:, dc * 128:(dc + 1) * 128],
                                      ygsp[:, dc * Ts:(dc + 1) * Ts],
                                      start=(dc == 0), stop=(dc == NDC - 1)),
                             waits=w3, inc=(1 if dc == NDC - 1 else 0))
                mm_T = sch.now("T")
                wowin_freeT.append(mm_T)
                sch.emit("V", lambda e, kc=kc, xv=xv, psel=psel:
                         e.tensor_add(xv[:, kc * Ts:(kc + 1) * Ts], xv[:, kc * Ts:(kc + 1) * Ts],
                                      psel[:, 0:Ts]),
                         waits=([("T", mm_T), (chX, xl_S)] if kc == 0 else [("T", mm_T)]), inc=1)
                p3psum_freeV.append(sch.now("V"))
                xa_V = sch.now("V")
                sch.dma(f"xas{s % 2}", lambda e, kc=kc, l0=l0, l1=l1, xv=xv:
                        e.dma_start(out=xA[kc, :, l0:l1], in_=xv[:, kc * Ts:(kc + 1) * Ts]),
                        waits=[("V", xa_V)], eng="G")
            p3_freeV.append(sch.now("V"))
            xas_tots.append((f"xas{s % 2}", sch.now(f"xas{s % 2}")))

        for s in range(nsp):
            p3_span(s)
        xas_tot = [sch.now("xas0"), sch.now("xas1")]

    p3end = [("xas0", xas_tot[0]), ("xas1", xas_tot[1])]
    wq_Alist = []
    for oc in range(KC):
        w = list(p3end) if oc == 0 else [("A", wq_Alist[-1])]
        sch.dma("misc", lambda e, oc=oc: e.dma_start(out=stg_f[:, 0:512], in_=fw[oc, :, 0:512]), waits=w)
        sch.dma("misc", lambda e, oc=oc: e.dma_start(out=stg_f[:, 512:1024], in_=fw[oc, :, 512:1024]))
        fw_S = sch.now("misc")
        sch.emit("V", lambda e: e.reduce_max(am_sb[:, 0:1], stg_f[:, 0:DM],
                                             axis=mybir.AxisListType.X, apply_absolute_value=True),
                 waits=[("misc", fw_S)])
        sch.emit("V", lambda e: e.tensor_scalar_max(am_sb[:, 0:1], am_sb[:, 0:1], FFN_EPS))
        sch.emit("V", lambda e: e.reciprocal(am_sb[:, 1:2], am_sb[:, 0:1]))
        sch.emit("V", lambda e, oc=oc: e.tensor_scalar_mul(wsc_sb[:, oc:oc + 1], am_sb[:, 0:1], 1.0 / QMAX))
        sch.emit("V", lambda e: e.tensor_scalar_mul(am_sb[:, 1:2], am_sb[:, 1:2], QMAX))
        sch.emit("V", lambda e: e.tensor_scalar(stg_f[:, 0:DM], stg_f[:, 0:DM],
                                                am_sb[:, 1:2], RND_M, ALU.mult, ALU.add))
        sch.emit("V", lambda e: e.tensor_scalar(stg_f[:, 0:DM], stg_f[:, 0:DM],
                                                RND_M, None, ALU.subtract), inc=1)
        wq_V = sch.now("V")
        for ic in range(KC):
            sch.emit("T", lambda e, ic=ic: e.transpose(ps_aux[:, 128:256], stg_f[:, ic * 128:(ic + 1) * 128],
                                                       identf[:]),
                     waits=([("V", wq_V)] if ic == 0 else [("A", wq_Alist[-1])]), inc=1)
            tr_T = sch.now("T")
            sch.emit("A", lambda e, oc=oc, ic=ic:
                     e.activation(out=wqt_sb[:, (ic * KC + oc) * 128:(ic * KC + oc + 1) * 128],
                                  in_=ps_aux[:, 128:256], func=AF.Copy),
                     waits=[("T", tr_T)] + ([("ygs0", sch.now("ygs0")), ("ygs1", sch.now("ygs1"))]
                                            if (oc == 0 and ic == 0) else []),
                     inc=1)
            wq_Alist.append(sch.now("A"))
    sch.dma("idl", lambda e: e.dma_start(out=fb_sb[:, 0:KC], in_=fb[:, :]))
    wq_A = sch.now("A")
    fb_S = sch.now("idl")

    ffn_freeV = []
    ffn_store = []

    def ln_span(src_sp, dst_sp, first_waits):
        for kc in range(KC):
            sch.emit("A", lambda e, kc=kc, s_=src_sp:
                     e.activation(out=ub_sp[:, kc * Ts:(kc + 1) * Ts], in_=s_[:, kc * Ts:(kc + 1) * Ts],
                                  func=AF.Copy),
                     waits=(list(first_waits) if kc == 0 else []), inc=(1 if kc == KC - 1 else 0))
        cast_A = sch.now("A")
        for kc in range(KC):
            sch.emit("T", lambda e, kc=kc:
                     e.matmul(ps_st[0:1, 0:Ts], ones_bc[:], ub_sp[:, kc * Ts:(kc + 1) * Ts],
                              start=(kc == 0), stop=(kc == KC - 1)),
                     waits=([("A", cast_A)] if kc == 0 else []), inc=(1 if kc == KC - 1 else 0))
        sum_T = sch.now("T")
        sch.emit("V", lambda e: e.tensor_scalar_mul(rowbuf[0:1, 0:Ts], ps_st[0:1, 0:Ts], 1.0 / DM),
                 waits=[("T", sum_T)], inc=1)
        m_V = sch.now("V")
        for kc in range(KC):
            sch.emit("A", lambda e, kc=kc, s_=src_sp:
                     e.activation(out=ub_sp[:, kc * Ts:(kc + 1) * Ts], in_=s_[:, kc * Ts:(kc + 1) * Ts],
                                  func=AF.Square),
                     waits=([("T", sum_T)] if kc == 0 else []), inc=(1 if kc == KC - 1 else 0))
        sq_A = sch.now("A")
        for kc in range(KC):
            sch.emit("T", lambda e, kc=kc:
                     e.matmul(ps_st[0:1, 0:Ts], ones_bc[:], ub_sp[:, kc * Ts:(kc + 1) * Ts],
                              start=(kc == 0), stop=(kc == KC - 1)),
                     waits=([("A", sq_A), ("V", m_V)] if kc == 0 else []), inc=(1 if kc == KC - 1 else 0))
        sq_T = sch.now("T")
        sch.emit("V", lambda e: e.tensor_scalar_mul(rowbuf[0:1, Ts:2 * Ts], ps_st[0:1, 0:Ts], 1.0 / DM),
                 waits=[("T", sq_T)])
        sch.emit("V", lambda e: e.tensor_mul(rowbuf[0:1, 2 * Ts:3 * Ts], rowbuf[0:1, 0:Ts], rowbuf[0:1, 0:Ts]))
        sch.emit("V", lambda e: e.tensor_sub(rowbuf[0:1, Ts:2 * Ts], rowbuf[0:1, Ts:2 * Ts],
                                             rowbuf[0:1, 2 * Ts:3 * Ts]))
        sch.emit("V", lambda e: e.tensor_scalar_add(rowbuf[0:1, Ts:2 * Ts], rowbuf[0:1, Ts:2 * Ts], LN_EPS),
                 inc=1)
        var_V = sch.now("V")
        sch.emit("A", lambda e: e.activation(out=rowbuf[0:1, Ts:2 * Ts], in_=rowbuf[0:1, Ts:2 * Ts], func=AF.Ln),
                 waits=[("V", var_V)])
        sch.emit("A", lambda e: e.activation(out=rowbuf[0:1, Ts:2 * Ts], in_=rowbuf[0:1, Ts:2 * Ts],
                                             func=AF.Exp, scale=-0.5), inc=1)
        rstd_A = sch.now("A")
        sch.emit("V", lambda e: e.tensor_mul(rowbuf[0:1, 2 * Ts:3 * Ts], rowbuf[0:1, 0:Ts],
                                             rowbuf[0:1, Ts:2 * Ts]),
                 waits=[("A", rstd_A)], inc=1)
        mr_V = sch.now("V")
        sch.emit("T", lambda e: e.matmul(ps_aux[:, 0:Ts], ones_f[:], rowbuf[0:1, Ts:2 * Ts],
                                         start=True, stop=True),
                 waits=[("V", mr_V)])
        sch.emit("T", lambda e: e.matmul(ps_bc[:, 0:Ts], ones_f[:], rowbuf[0:1, 2 * Ts:3 * Ts],
                                         start=True, stop=True), inc=1)
        bc_T = sch.now("T")
        for kc in range(KC):
            sch.emit("V", lambda e, kc=kc, s_=src_sp, d_=dst_sp:
                     e.tensor_mul(d_[:, kc * Ts:(kc + 1) * Ts], s_[:, kc * Ts:(kc + 1) * Ts], ps_aux[:, 0:Ts]),
                     waits=([("T", bc_T)] if kc == 0 else []))
            sch.emit("V", lambda e, kc=kc, d_=dst_sp:
                     e.tensor_sub(d_[:, kc * Ts:(kc + 1) * Ts], d_[:, kc * Ts:(kc + 1) * Ts], ps_bc[:, 0:Ts]),
                     inc=(1 if kc == KC - 1 else 0))
        return sch.now("V")

    def ffn_span(s):
        l0, l1 = s * Ts, (s + 1) * Ts
        if s == 0:
            w = [("A", wq_A), ("idl", fb_S)]
        else:
            w = [("V", ffn_freeV[-1]), ("fos", ffn_store[-1])]
        for kc in range(KC):
            sch.dma("ffl", lambda e, kc=kc, l0=l0, l1=l1:
                    e.dma_start(out=u_sp[:, kc * Ts:(kc + 1) * Ts], in_=x_in[kc * 128:(kc + 1) * 128, l0:l1]),
                    waits=(w if kc == 0 else []))
            sch.dma("ffl", lambda e, kc=kc, l0=l0, l1=l1:
                    e.dma_start(out=o1_sp[:, kc * Ts:(kc + 1) * Ts], in_=xA[kc, :, l0:l1]))
        ld_S = sch.now("ffl")
        for kc in range(KC):
            sch.emit("V", lambda e, kc=kc:
                     e.tensor_add(u_sp[:, kc * Ts:(kc + 1) * Ts], u_sp[:, kc * Ts:(kc + 1) * Ts],
                                  o1_sp[:, kc * Ts:(kc + 1) * Ts]),
                     waits=([("ffl", ld_S)] if kc == 0 else []), inc=(1 if kc == KC - 1 else 0))
        u_V = sch.now("V")
        ln1_V = ln_span(u_sp, o1_sp, [("V", u_V)])
        amred_V = None
        for tb in range(NT):
            for kc in range(KC):
                w_tr = []
                if kc == 0:
                    w_tr = [("V", ln1_V)] if tb == 0 else [("V", amred_V)]
                sch.emit("T", lambda e, kc=kc, tb=tb:
                         e.transpose(ps_bc[:, kc * 128:(kc + 1) * 128],
                                     o1_sp[:, kc * Ts + tb * 128:kc * Ts + (tb + 1) * 128], identf[:]),
                         waits=w_tr, inc=(1 if kc == KC - 1 else 0))
            tr_T = sch.now("T")
            sch.emit("V", lambda e, tb=tb:
                     e.reduce_max(am_sb[:, tb:tb + 1], ps_bc[:, 0:DM],
                                  axis=mybir.AxisListType.X, apply_absolute_value=True),
                     waits=[("T", tr_T)], inc=1)
            amred_V = sch.now("V")
        sch.emit("V", lambda e: e.tensor_scalar_max(am_sb[:, 0:NT], am_sb[:, 0:NT], FFN_EPS), inc=1)
        amc_V = sch.now("V")
        for c in range(NT):
            sch.emit("T", lambda e, c=c: e.transpose(ps_aux[0:1, c * 128:(c + 1) * 128],
                                                     am_sb[:, c:c + 1], identf[:]),
                     waits=([("V", amc_V)] if c == 0 else []), inc=(1 if c == NT - 1 else 0))
        amt_T = sch.now("T")
        sch.emit("A", lambda e: e.activation(out=rowbuf[0:1, 0:Ts], in_=ps_aux[0:1, 0:Ts],
                                             func=AF.Copy, scale=1.0 / QMAX),
                 waits=[("T", amt_T)], inc=1)
        srow_A = sch.now("A")
        sch.emit("V", lambda e: e.reciprocal(rowbuf[0:1, Ts:2 * Ts], rowbuf[0:1, 0:Ts]),
                 waits=[("A", srow_A)], inc=1)
        rrow_V = sch.now("V")
        sch.emit("T", lambda e: e.matmul(ps_aux[:, 0:Ts], ones_f[:], rowbuf[0:1, Ts:2 * Ts],
                                         start=True, stop=True),
                 waits=[("V", rrow_V)])
        sch.emit("T", lambda e: e.matmul(ps_bc[:, 0:Ts], ones_f[:], rowbuf[0:1, 0:Ts],
                                         start=True, stop=True), inc=1)
        scl_T = sch.now("T")
        sch.emit("A", lambda e: e.activation(out=srep_sp, in_=ps_bc[:, 0:Ts], func=AF.Copy),
                 waits=[("T", scl_T)], inc=1)
        srep_A = sch.now("A")
        for kc in range(KC):
            sch.emit("V", lambda e, kc=kc:
                     e.tensor_mul(u_sp[:, kc * Ts:(kc + 1) * Ts], o1_sp[:, kc * Ts:(kc + 1) * Ts],
                                  ps_aux[:, 0:Ts]),
                     waits=([("T", scl_T)] if kc == 0 else []))
            sch.emit("V", lambda e, kc=kc:
                     e.tensor_scalar(u_sp[:, kc * Ts:(kc + 1) * Ts], u_sp[:, kc * Ts:(kc + 1) * Ts],
                                     RND_M, RND_M, ALU.add, ALU.subtract))
            sch.emit("V", lambda e, kc=kc:
                     e.tensor_copy(xq_sp[:, kc * Ts:(kc + 1) * Ts], u_sp[:, kc * Ts:(kc + 1) * Ts]),
                     inc=(1 if kc == KC - 1 else 0))
        xq_V = sch.now("V")
        ffnq_freeV = []
        for oc in range(KC):
            psel = ps_mm if oc % 2 == 0 else ps_mm2
            for ic in range(KC):
                w_q = []
                if ic == 0:
                    w_q = [("V", xq_V)]
                    if oc >= 2:
                        w_q.append(("V", ffnq_freeV[oc - 2]))
                sch.emit("T", lambda e, oc=oc, ic=ic, psel=psel:
                         e.matmul(psel[:, 0:Ts], wqt_sb[:, (ic * KC + oc) * 128:(ic * KC + oc + 1) * 128],
                                  xq_sp[:, ic * Ts:(ic + 1) * Ts],
                                  start=(ic == 0), stop=(ic == KC - 1)),
                         waits=w_q, inc=(1 if ic == KC - 1 else 0))
            q_T = sch.now("T")
            sch.emit("V", lambda e, oc=oc, psel=psel:
                     e.tensor_mul(u_sp[:, oc * Ts:(oc + 1) * Ts], psel[:, 0:Ts], srep_sp),
                     waits=[("T", q_T)] + ([("A", srep_A)] if oc == 0 else []))
            sch.emit("V", lambda e, oc=oc:
                     e.tensor_scalar(u_sp[:, oc * Ts:(oc + 1) * Ts], u_sp[:, oc * Ts:(oc + 1) * Ts],
                                     wsc_sb[:, oc:oc + 1], fb_sb[:, oc:oc + 1], ALU.mult, ALU.add))
            sch.emit("V", lambda e, oc=oc:
                     e.tensor_add(u_sp[:, oc * Ts:(oc + 1) * Ts], u_sp[:, oc * Ts:(oc + 1) * Ts],
                                  o1_sp[:, oc * Ts:(oc + 1) * Ts]), inc=1)
            ffnq_freeV.append(sch.now("V"))
        res_V = sch.now("V")
        ln2_V = ln_span(u_sp, o1_sp, [("V", res_V)])
        for kc in range(KC):
            sch.dma("fos", lambda e, kc=kc, l0=l0, l1=l1:
                    e.dma_start(out=y_out[kc * 128:(kc + 1) * 128, l0:l1],
                                in_=o1_sp[:, kc * Ts:(kc + 1) * Ts]),
                    waits=([("V", ln2_V)] if kc == 0 else []), eng="G")
        ffn_freeV.append(ln2_V)
        ffn_store.append(sch.now("fos"))

    for s in range(nsp):
        ffn_span(s)
    out_S = sch.now("fos")
    sch.emit("V", lambda e: e.nop(), waits=[("fos", out_S)])

    block = ctx.enter_context(nc.Block())
    by_eng = {"T": [], "A": [], "V": [], "S": [], "G": []}
    for eng, waits, fn, inc_key, inc_amt in sch.items:
        by_eng[eng].append((waits, fn, inc_key, inc_amt))

    def runner(key):
        def _f(eng):
            for waits, fn, inc_key, inc_amt in by_eng[key]:
                for (wk, wv) in waits:
                    if wv > 0:
                        eng.wait_ge(SEMS[wk], wv)
                ins = fn(eng)
                if inc_key is not None:
                    assert ins is not None
                    ins.then_inc(SEMS[inc_key], inc_amt)
        return _f

    block.tensor(runner("T"))
    block.scalar(runner("A"))
    block.vector(runner("V"))
    block.sync(runner("S"))
    block.gpsimd(runner("G"))
    ctx.close()
    return nc


def pack_inputs(inp, b, L=8192, layers=4):
    import ml_dtypes
    x = np.ascontiguousarray(np.asarray(inp["L0_attn_x"])[b].T)
    W_in = np.asarray(inp["L0_attn_W_in"])[:layers]
    winT = np.ascontiguousarray(
        W_in.transpose(0, 2, 1).reshape(layers, KC, 128, NEC, 128).transpose(0, 3, 1, 2, 4))
    W_out = np.asarray(inp["L0_attn_W_out"])[:layers]
    woutT = np.ascontiguousarray(
        W_out.transpose(0, 2, 1).reshape(layers, NDC, 128, KC, 128).transpose(0, 3, 1, 2, 4))
    W_x = np.asarray(inp["L0_attn_W_x"])[:layers]
    wxT = np.ascontiguousarray(
        W_x.transpose(0, 2, 1).reshape(layers, NDC, 128, 96)
        .transpose(0, 2, 1, 3).reshape(layers, 128, NDC * 96))
    W_dt = np.asarray(inp["L0_attn_W_dt"])[:layers]
    wdtT = np.ascontiguousarray(W_dt.transpose(0, 2, 1))
    alog = np.ascontiguousarray(np.asarray(inp["L0_attn_A_log"])[:layers]
                                .reshape(layers, NDC, 128, NS).transpose(0, 2, 1, 3)
                                .reshape(layers, 128, NDC * NS))
    convw = np.ascontiguousarray(np.asarray(inp["L0_attn_conv_w"])[:layers]
                                 .reshape(layers, NDC, 128, DCV).transpose(0, 2, 1, 3)
                                 .reshape(layers, 128, NDC * DCV))
    convb = np.ascontiguousarray(np.asarray(inp["L0_attn_conv_b"])[:layers]
                                 .reshape(layers, NDC, 128).transpose(0, 2, 1))
    bdt = np.ascontiguousarray(np.asarray(inp["L0_attn_b_dt"])[:layers]
                               .reshape(layers, NDC, 128).transpose(0, 2, 1))
    dvec = np.ascontiguousarray(np.asarray(inp["L0_attn_D"])[:layers]
                                .reshape(layers, NDC, 128).transpose(0, 2, 1))
    normw = np.ascontiguousarray(np.asarray(inp["L0_attn_norm_w"])[:layers]
                                 .reshape(layers, KC, 128).transpose(0, 2, 1))
    fwt = np.ascontiguousarray(np.asarray(inp["L0_ffn_weight"]).reshape(KC, 128, DM))
    fbt = np.ascontiguousarray(np.asarray(inp["L0_ffn_bias"]).reshape(KC, 128).T)
    return {
        "x_in": x.astype(np.float32),
        "winT": winT.astype(np.float32), "woutT": woutT.astype(np.float32),
        "wxT": wxT.astype(np.float32), "wdtT": wdtT.astype(np.float32),
        "alog": alog.astype(np.float32), "convw": convw.astype(np.float32),
        "convb": convb.astype(np.float32), "bdt": bdt.astype(np.float32),
        "dvec": dvec.astype(np.float32), "normw": normw.astype(np.float32),
        "fw": fwt.astype(np.float32), "fb": fbt.astype(np.float32),
        "ident_in": np.eye(128, dtype=ml_dtypes.bfloat16),
        "identf_in": np.eye(128, dtype=np.float32),
    }


def kernel(**inputs):
    from concourse.bass_utils import run_bass_kernel_spmd
    L = 8192
    layers = 4
    nc = build_program(L=L, layers=layers, debug_internals=False)
    im = [pack_inputs(inputs, 0, L=L, layers=layers),
          pack_inputs(inputs, 1, L=L, layers=layers)]
    in_maps = [im[c] if c < 2 else im[0] for c in range(8)]
    res = run_bass_kernel_spmd(nc, in_maps, core_ids=list(range(8)))
    out0 = np.ascontiguousarray(res.results[0]["y_out"].T)
    out1 = np.ascontiguousarray(res.results[1]["y_out"].T)
    return np.stack([out0, out1]).astype(np.float32)
